# version 01
# baseline (speedup 1.0000x reference)
import numpy as np
import concourse.bass as bass
import concourse.mybir as mybir
from concourse.bass_utils import run_bass_kernel_spmd
from contextlib import ExitStack

F32 = mybir.dt.float32
BF16 = mybir.dt.bfloat16
AF = mybir.ActivationFunctionType
ALU = mybir.AluOpType
AX = mybir.AxisListType

NCORES = 8
T = 16384
TC = T // NCORES
D = 1024
P = 128
DC = D // P
LN_EPS = 1e-5


class Ctx:

    def __init__(self, nc, es):
        self.nc = nc
        self.es = es
        self._sems = {}
        self._cnt = {}

    def sem(self, name):
        if name not in self._sems:
            self._sems[name] = self.es.enter_context(self.nc.semaphore(name))
            self._cnt[name] = 0
        return self._sems[name]

    def inc(self, instr, name, n=1):
        h = self.sem(name)
        instr.then_inc(h, n)
        self._cnt[name] += n
        return self._cnt[name]

    def val(self, name):
        self.sem(name)
        return self._cnt[name]

    def wait(self, eng, name, v=None):
        h = self.sem(name)
        eng.wait_ge(h, self._cnt[name] if v is None else v)


def sb(ctx, name, shape, dtype):
    return ctx.es.enter_context(ctx.nc.sbuf_tensor(name, shape, dtype))


def ps(ctx, name, shape, dtype=F32):
    return ctx.es.enter_context(ctx.nc.psum_tensor(name, shape, dtype))



def dma(ctx, eng, out, in_, name):
    return ctx.inc(eng.dma_start(out=out, in_=in_), name, 16)

def make_consts(ctx):
    ones_colf = sb(ctx, "ones_colf", [P, 1], F32)
    ones_colb = sb(ctx, "ones_colb", [P, 1], BF16)
    ones_rowf = sb(ctx, "ones_rowf", [1, P], F32)
    ones_rowb = sb(ctx, "ones_rowb", [1, P], BF16)
    ones_tokb = sb(ctx, "ones_tokb", [P, 1], BF16)
    return dict(ones_colf=ones_colf, ones_colb=ones_colb, ones_rowf=ones_rowf,
                ones_rowb=ones_rowb, ones_tokb=ones_tokb)


def emit_consts_init(ctx, block, C):
    @block.vector
    def _(v):
        v.memset(C["ones_colf"][:, :], 1.0)
        v.memset(C["ones_colb"][:, :], 1.0)
        v.memset(C["ones_rowf"][:, :], 1.0)
        v.memset(C["ones_rowb"][:, :], 1.0)
        i = v.memset(C["ones_tokb"][:, :], 1.0)
        ctx.inc(i, "consts")


def ensure_ln_shared(ctx, l2=False):
    if not hasattr(ctx, "ln_shared"):
        ctx.ln_shared = dict(
            sq=sb(ctx, "ln_sq", [P, DC * 512], BF16),
            rows=sb(ctx, "ln_rows", [1, 4 * 512], F32),
            app=sb(ctx, "ln_app", [P, 512], F32),
            stats=ps(ctx, "ln_st", [1, 512], F32),
            stats2=ps(ctx, "ln_st2", [1, 512], F32),
            bc=ps(ctx, "ln_bc", [P, 1024], F32),
        )
    if l2 and "rows2" not in ctx.ln_shared:
        ctx.ln_shared["rows2"] = sb(ctx, "ln_rows2", [1, 2 * 512], F32)


def emit_ln_fm(ctx, block, C, x_fm, stride, tag, *, t0=0, t1=None, out_bf=None,
               out_f32=None, gamma=None, beta=None, l2inv=None, l2off=0,
               out_stride=None, out_t0=None, wait_names=()):
    nc = ctx.nc
    if t1 is None:
        t1 = stride
    if out_stride is None:
        out_stride = stride
    if out_t0 is None:
        out_t0 = t0
    ensure_ln_shared(ctx, l2=(l2inv is not None))
    sh = ctx.ln_shared
    sq, rows, app = sh["sq"], sh["rows"], sh["app"]
    rows2 = sh.get("rows2")
    stats, stats2, bc = sh["stats"], sh["stats2"], sh["bc"]

    ntiles = (t1 - t0 + 511) // 512
    for it in range(ntiles):
        tt = t0 + it * 512
        tw = min(512, t1 - tt)

        @block.scalar
        def _(s):
            if it == 0:
                for w in wait_names:
                    ctx.wait(s, w)
                ctx.wait(s, "consts")
            ctx.wait(s, f"ln_apply")
            for c in range(DC):
                i = s.activation(sq[:, c * 512:c * 512 + tw],
                                 x_fm[:, c * stride + tt:c * stride + tt + tw], AF.Square)
            ctx.inc(i, f"ln_sq")

        @block.tensor
        def _(t):
            ctx.wait(t, f"ln_sq")
            for c in range(DC):
                t.matmul(stats[0:1, 0:tw], C["ones_colf"][:, 0:1],
                         x_fm[:, c * stride + tt:c * stride + tt + tw],
                         start=(c == 0), stop=(c == DC - 1))
            for c in range(DC):
                i = t.matmul(stats2[0:1, 0:tw], C["ones_colb"][:, 0:1],
                             sq[:, c * 512:c * 512 + tw],
                             start=(c == 0), stop=(c == DC - 1))
            ctx.inc(i, f"ln_st")

        @block.vector
        def _(v):
            ctx.wait(v, f"ln_st")
            mu = rows[0:1, 0:tw]
            ex2 = rows[0:1, 512:512 + tw]
            a = rows[0:1, 1024:1024 + tw]
            b = rows[0:1, 1536:1536 + tw]
            vr = rows2[0:1, 0:tw] if rows2 is not None else a
            v.tensor_scalar(mu, stats[0:1, 0:tw], 1.0 / D, None, ALU.mult)
            v.tensor_scalar(ex2, stats2[0:1, 0:tw], 1.0 / D, None, ALU.mult)
            v.tensor_tensor(a, mu, mu, ALU.mult)
            if rows2 is not None:
                v.tensor_tensor(vr, ex2, a, ALU.subtract)
                i = v.tensor_scalar(ex2, vr, LN_EPS, None, ALU.add)
            else:
                i = v.tensor_tensor(ex2, ex2, a, ALU.subtract)
                ctx.inc(i, "ln_v0")
                ctx.wait(v, "ln_v0")
                i = v.tensor_scalar(ex2, ex2, LN_EPS, None, ALU.add)
            ctx.inc(i, f"ln_var")

        @block.scalar
        def _(s):
            ctx.wait(s, f"ln_var")
            ex2 = rows[0:1, 512:512 + tw]
            i = s.activation(ex2, ex2, AF.Sqrt)
            if l2inv is not None:
                vr = rows2[0:1, 0:tw]
                sdr = rows2[0:1, 512:512 + tw]
                i = s.activation(sdr, vr, AF.Sqrt)
            ctx.inc(i, f"ln_sd")

        @block.vector
        def _(v):
            ctx.wait(v, f"ln_sd")
            mu = rows[0:1, 0:tw]
            sd = rows[0:1, 512:512 + tw]
            a = rows[0:1, 1024:1024 + tw]
            b = rows[0:1, 1536:1536 + tw]
            i = v.reciprocal(a, sd)
            ctx.inc(i, f"ln_rcp")
            ctx.wait(v, f"ln_rcp")
            i = v.tensor_tensor(b, mu, a, ALU.mult)
            if l2inv is not None:
                sdr = rows2[0:1, 512:512 + tw]
                nr = rows2[0:1, 0:tw]
                v.tensor_tensor(nr, sdr, a, ALU.mult)
                v.tensor_scalar(nr, nr, float(np.sqrt(D)), None, ALU.mult)
                i = v.tensor_scalar(nr, nr, 1e-12, None, ALU.max)
                ctx.inc(i, f"ln_nrm")
                ctx.wait(v, f"ln_nrm")
                i = v.reciprocal(l2inv[0:1, tt - l2off:tt - l2off + tw], nr)
            ctx.inc(i, f"ln_ab")

        @block.tensor
        def _(t):
            ctx.wait(t, f"ln_ab")
            a = rows[0:1, 1024:1024 + tw]
            b = rows[0:1, 1536:1536 + tw]
            t.matmul(bc[:, 0:tw], C["ones_rowf"][0:1, :], a, start=True, stop=True)
            i = t.matmul(bc[:, 512:512 + tw], C["ones_rowf"][0:1, :], b,
                         start=True, stop=True)
            ctx.inc(i, f"ln_bc")

        @block.vector
        def _(v):
            ctx.wait(v, f"ln_bc")
            for c in range(DC):
                col = c * stride + tt
                ocol = c * out_stride + out_t0 + (tt - t0)
                v.tensor_tensor(app[:, 0:tw], x_fm[:, col:col + tw],
                                bc[:, 0:tw], ALU.mult)
                i = v.tensor_tensor(app[:, 0:tw], app[:, 0:tw],
                                    bc[:, 512:512 + tw], ALU.subtract)
                if gamma is not None:
                    i = v.tensor_scalar(app[:, 0:tw], app[:, 0:tw],
                                        gamma[:, c:c + 1], beta[:, c:c + 1],
                                        ALU.mult, ALU.add)
                if out_f32 is not None:
                    i = v.tensor_copy(out_f32[:, ocol:ocol + tw], app[:, 0:tw])
                if out_bf is not None:
                    i = v.tensor_copy(out_bf[:, ocol:ocol + tw], app[:, 0:tw])
            ctx.inc(i, f"ln_apply")
    return "ln_apply"



def build_A():
    nc = bass.Bass("TRN2", target_bir_lowering=False)
    x0 = nc.dram_tensor("x0", [D, TC], F32, kind="ExternalInput").ap()
    wkT = nc.dram_tensor("wkT", [D, D], F32, kind="ExternalInput").ap()
    bk = nc.dram_tensor("bk", [1, D], F32, kind="ExternalInput").ap()
    wvT = nc.dram_tensor("wvT", [D, D], F32, kind="ExternalInput").ap()
    bv = nc.dram_tensor("bv", [1, D], F32, kind="ExternalInput").ap()
    pkv = nc.dram_tensor("pkv", [64, 16 * 65], F32, kind="ExternalOutput").ap()
    ln0_out = nc.dram_tensor("ln0", [D, TC], BF16, kind="ExternalOutput").ap()

    with ExitStack() as es:
        ctx = Ctx(nc, es)
        C = make_consts(ctx)
        x0f = sb(ctx, "x0f", [P, DC * TC], F32)
        lnbf = sb(ctx, "lnbf", [P, DC * TC], BF16)
        wk_sb = sb(ctx, "wk_sb", [P, DC * D], BF16)
        wv_sb = sb(ctx, "wv_sb", [P, DC * D], BF16)
        bk_sb = sb(ctx, "bk_sb", [1, D], BF16)
        bv_sb = sb(ctx, "bv_sb", [1, D], BF16)
        kk_tm = sb(ctx, "kk_tm", [P, 16 * 512], BF16)
        v_tm = sb(ctx, "v_tm", [P, 16 * 512], BF16)
        negmax = sb(ctx, "negmax", [P, 8], F32)
        kv_sb = sb(ctx, "kv_sb", [64, 16 * 65], F32)

        with nc.Block() as block:
            @block.sync
            def _(s):
                for c in range(DC):
                    dma(ctx, s, x0f[:, c * TC:(c + 1) * TC],
                        x0[c * P:(c + 1) * P, :], "ld_x")

            @block.gpsimd
            def _(g):
                for c in range(DC):
                    dma(ctx, g, wk_sb[:, c * D:(c + 1) * D],
                        wkT[c * P:(c + 1) * P, :], "ld_w")
                    dma(ctx, g, wv_sb[:, c * D:(c + 1) * D],
                        wvT[c * P:(c + 1) * P, :], "ld_w")
                dma(ctx, g, bk_sb[:, :], bk, "ld_b")
                dma(ctx, g, bv_sb[:, :], bv, "ld_b")

            emit_consts_init(ctx, block, C)
            lnfin = emit_ln_fm(ctx, block, C, x0f, TC, "A",
                               out_bf=lnbf, wait_names=("ld_x",))

            kps = ps(ctx, "kps", [P, 512], F32)
            vps = ps(ctx, "vps", [P, 512], F32)
            kvp = [ps(ctx, f"kvp{j}", [64, 512], F32) for j in range(2)]

            for hg in range(2):
                for tb in range(16):
                    @block.tensor
                    def _(t):
                        if hg == 0 and tb == 0:
                            ctx.wait(t, "ld_w")
                            ctx.wait(t, "ld_b")
                            ctx.wait(t, lnfin)
                        else:
                            ctx.wait(t, "kexp")
                            ctx.wait(t, "vcp")
                        for c in range(DC):
                            lhs = lnbf[:, c * TC + tb * P: c * TC + tb * P + P]
                            t.matmul(kps[:, 0:512], lhs,
                                     wk_sb[:, c * D + hg * 512: c * D + hg * 512 + 512],
                                     start=(c == 0), stop=False)
                        t.matmul(kps[:, 0:512], C["ones_rowb"][0:1, 0:P],
                                 bk_sb[0:1, hg * 512:hg * 512 + 512],
                                 start=False, stop=True)
                        for c in range(DC):
                            lhs = lnbf[:, c * TC + tb * P: c * TC + tb * P + P]
                            t.matmul(vps[:, 0:512], lhs,
                                     wv_sb[:, c * D + hg * 512: c * D + hg * 512 + 512],
                                     start=(c == 0), stop=False)
                        i = t.matmul(vps[:, 0:512], C["ones_rowb"][0:1, 0:P],
                                     bv_sb[0:1, hg * 512:hg * 512 + 512],
                                     start=False, stop=True)
                        ctx.inc(i, "kv_mm")

                    @block.vector
                    def _(v):
                        ctx.wait(v, "kv_mm")
                        if not (hg == 0 and tb == 0):
                            ctx.wait(v, "kexp")
                        i = v.reduce_max(negmax[:, 0:8],
                                     kps.ap().rearrange("p (h d) -> p h d", d=64),
                                     axis=AX.X)
                        ctx.inc(i, "red")
                        ctx.wait(v, "red")
                        v.tensor_scalar(negmax[:, 0:8], negmax[:, 0:8], -1.0, None,
                                        ALU.mult)
                        i = v.tensor_copy(v_tm[:, tb * 512:(tb + 1) * 512], vps[:, 0:512])
                        ctx.inc(i, "vcp")

                    @block.scalar
                    def _(s):
                        ctx.wait(s, "vcp")
                        for h in range(8):
                            i = s.activation(kk_tm[:, tb * 512 + h * 64: tb * 512 + (h + 1) * 64],
                                             kps[:, h * 64:(h + 1) * 64], AF.Exp,
                                             bias=negmax[:, h:h + 1])
                        ctx.inc(i, "kexp")

                    @block.tensor
                    def _(t):
                        ctx.wait(t, "kexp")
                        for h in range(8):
                            lhs = kk_tm[:, tb * 512 + h * 64: tb * 512 + (h + 1) * 64]
                            t.matmul(kvp[h // 4][:, (h % 4) * 65:(h % 4) * 65 + 64], lhs,
                                     v_tm[:, tb * 512 + h * 64: tb * 512 + (h + 1) * 64],
                                     start=(tb == 0 and h % 4 == 0), stop=(tb == 15))
                            i = t.matmul(kvp[h // 4][:, (h % 4) * 65 + 64:(h % 4) * 65 + 65],
                                         lhs, C["ones_tokb"][:, 0:1],
                                         start=False, stop=(tb == 15))
                        ctx.inc(i, "kvacc")

                @block.vector
                def _(v):
                    ctx.wait(v, "kvacc")
                    i = v.tensor_copy(kv_sb[:, hg * 8 * 65:(hg * 8 + 4) * 65], kvp[0][:, 0:260])
                    i = v.tensor_copy(kv_sb[:, (hg * 8 + 4) * 65:(hg * 8 + 8) * 65],
                                      kvp[1][:, 0:260])
                    ctx.inc(i, "kvev")

                if hg == 0:
                    @block.tensor
                    def _(t):
                        ctx.wait(t, "kvev")

            @block.sync
            def _(s):
                ctx.wait(s, "kvev")
                ctx.wait(s, lnfin)
                dma(ctx, s, pkv, kv_sb[:, :], "st_A")
                for c in range(DC):
                    dma(ctx, s, ln0_out[c * P:(c + 1) * P, :],
                        lnbf[:, c * TC:(c + 1) * TC], "st_A")
                ctx.wait(s, "st_A")
    return nc


def run_A(inputs, shards):
    nc = build_A()
    wkT = np.ascontiguousarray(inputs["L0_attn_Wk"].T)
    wvT = np.ascontiguousarray(inputs["L0_attn_Wv"].T)
    bk = np.asarray(inputs["L0_attn_bk"], np.float32).reshape(1, D)
    bv = np.asarray(inputs["L0_attn_bv"], np.float32).reshape(1, D)
    in_maps = [{"x0": shards[c], "wkT": wkT, "bk": bk, "wvT": wvT, "bv": bv}
               for c in range(NCORES)]
    res = run_bass_kernel_spmd(nc, in_maps, core_ids=list(range(NCORES)))
    pkv = np.stack([res.results[c]["pkv"] for c in range(NCORES)], axis=0)
    ln0 = [res.results[c]["ln0"] for c in range(NCORES)]
    return pkv, ln0



def build_B():
    nc = bass.Bass("TRN2", target_bir_lowering=False)
    x0 = nc.dram_tensor("x0", [D, TC], F32, kind="ExternalInput").ap()
    ln0 = nc.dram_tensor("ln0", [D, TC], BF16, kind="ExternalInput").ap()
    pkvs = nc.dram_tensor("pkvs", [64, 8 * 1040], F32, kind="ExternalInput").ap()
    wqT = nc.dram_tensor("wqT", [D, D], F32, kind="ExternalInput").ap()
    bq = nc.dram_tensor("bq", [1, D], F32, kind="ExternalInput").ap()
    woT = nc.dram_tensor("woT", [D, D], F32, kind="ExternalInput").ap()
    bo = nc.dram_tensor("bo", [1, D], F32, kind="ExternalInput").ap()
    gwT = nc.dram_tensor("gwT", [D, 16], F32, kind="ExternalInput").ap()
    wg = nc.dram_tensor("wg", [16 * D, 512], F32, kind="ExternalInput").ap()
    wu = nc.dram_tensor("wu", [16 * D, 512], F32, kind="ExternalInput").ap()
    wd = nc.dram_tensor("wd", [16 * 512, D], F32, kind="ExternalInput").ap()
    swg = nc.dram_tensor("swg", [D, 512], F32, kind="ExternalInput").ap()
    swu = nc.dram_tensor("swu", [D, 512], F32, kind="ExternalInput").ap()
    swd = nc.dram_tensor("swd", [512, D], F32, kind="ExternalInput").ap()
    gam = nc.dram_tensor("gam", [P, DC], F32, kind="ExternalInput").ap()
    bet = nc.dram_tensor("bet", [P, DC], F32, kind="ExternalInput").ap()
    ident = nc.dram_tensor("ident", [P, P], F32, kind="ExternalInput").ap()
    x2o = nc.dram_tensor("x2o", [D, TC], F32, kind="ExternalOutput").ap()
    uo = nc.dram_tensor("uo", [D, TC], F32, kind="ExternalOutput").ap()
    x1d = nc.dram_tensor("x1o", [D, TC], F32, kind="ExternalOutput").ap()
    dbg_cw = nc.dram_tensor("dbg_cw", [16, TC], F32, kind="ExternalOutput").ap()
    dbg_x0t = nc.dram_tensor("dbg_x0t", [D, 512], F32, kind="ExternalOutput").ap()
    dbg_l2 = nc.dram_tensor("dbg_l2", [1, TC], F32, kind="ExternalOutput").ap()
    dbg_ln1 = nc.dram_tensor("dbg_ln1", [D, TC], BF16, kind="ExternalOutput").ap()

    NT4 = TC // 512

    with ExitStack() as es:
        ctx = Ctx(nc, es)
        C = make_consts(ctx)
        ensure_ln_shared(ctx, l2=True)
        tokrow_b = sb(ctx, "tokrow_b", [1, 512], BF16)
        onesq = sb(ctx, "onesq", [1, 1], F32)

        with nc.Block() as block:
            @block.vector
            def _(v):
                v.memset(C["ones_colf"][:, :], 1.0)
                v.memset(C["ones_colb"][:, :], 1.0)
                v.memset(C["ones_rowf"][:, :], 1.0)
                v.memset(C["ones_rowb"][:, :], 1.0)
                v.memset(C["ones_tokb"][:, :], 1.0)
                v.memset(tokrow_b[:, :], 1.0)
                i = v.memset(onesq[:, :], 1.0)
                ctx.inc(i, "consts")

            with ExitStack() as es1:
                ctx.es, es_outer = es1, ctx.es
                ln0b = sb(ctx, "ln0b", [P, DC * TC], BF16)
                wq_sb = sb(ctx, "wq_sb", [P, DC * D], BF16)
                wo_sb = sb(ctx, "wo_sb", [P, DC * D], BF16)
                bq_sb = sb(ctx, "bq_sb", [1, D], BF16)
                bo_sb = sb(ctx, "bo_sb", [1, D], BF16)
                pkv_sb = sb(ctx, "pkv_sb", [64, 8 * 1040], F32)
                kv_b = sb(ctx, "kv_b", [64, 1040], BF16)
                qk_t = sb(ctx, "qk_t", [64, 16 * 512], BF16)
                yfm_t = sb(ctx, "yfm_t", [P, DC * 512], BF16)
                ysc = sb(ctx, "ysc", [64, 512], F32)
                drow = sb(ctx, "drow", [1, 512], F32)
                x0t = sb(ctx, "x0t", [P, DC * 512], F32)
                x1t = sb(ctx, "x1t", [P, DC * 512], F32)
                qps = ps(ctx, "qps", [P, 512], F32)
                yps = ps(ctx, "yps", [65, 512], F32)
                ybc = ps(ctx, "ybc", [64, 512], F32)
                ops = ps(ctx, "ops", [P, 512], F32)

                @block.sync
                def _(s):
                    for c in range(DC):
                        dma(ctx, s, ln0b[:, c * TC:(c + 1) * TC],
                            ln0[c * P:(c + 1) * P, :], "ld_ln0")
                    dma(ctx, s, pkv_sb[:, :], pkvs, "ld_pkv")

                @block.gpsimd
                def _(g):
                    for c in range(DC):
                        dma(ctx, g, wq_sb[:, c * D:(c + 1) * D],
                            wqT[c * P:(c + 1) * P, :], "ld_wq")
                    dma(ctx, g, bq_sb[:, :], bq, "ld_wq")
                    for c in range(DC):
                        dma(ctx, g, wo_sb[:, c * D:(c + 1) * D],
                            woT[c * P:(c + 1) * P, :], "ld_wo")
                    dma(ctx, g, bo_sb[:, :], bo, "ld_wo")

                @block.vector
                def _(v):
                    ctx.wait(v, "ld_pkv")
                    for a in range(1, 8):
                        i = v.tensor_tensor(pkv_sb[:, 0:1040], pkv_sb[:, 0:1040],
                                            pkv_sb[:, a * 1040:(a + 1) * 1040], ALU.add)
                        ctx.inc(i, "kvadd")
                        ctx.wait(v, "kvadd")
                    i = v.tensor_copy(kv_b[:, :], pkv_sb[:, 0:1040])
                    ctx.inc(i, "kvb")

                for it in range(NT4):
                    tt = it * 512

                    @block.scalar
                    def _(s):
                        if it > 0:
                            ctx.wait(s, "x1w", 8 * it)
                        for c in range(DC):
                            dma(ctx, s, x0t[:, c * 512:(c + 1) * 512],
                                x0[c * P:(c + 1) * P, tt:tt + 512], f"ld_x0_{it}")

                    for oc in range(DC):
                        @block.tensor
                        def _(t):
                            if it == 0 and oc == 0:
                                ctx.wait(t, "ld_ln0")
                                ctx.wait(t, "ld_wq")
                                ctx.wait(t, "consts")
                            else:
                                ctx.wait(t, "qexp")
                            for c in range(DC):
                                t.matmul(qps[:, :], wq_sb[:, c * D + oc * P: c * D + (oc + 1) * P],
                                         ln0b[:, c * TC + tt: c * TC + tt + 512],
                                         start=(c == 0), stop=False)
                            i = t.matmul(qps[:, :], bq_sb[0:1, oc * P:(oc + 1) * P],
                                         tokrow_b[0:1, :], start=False, stop=True)
                            ctx.inc(i, "qmm")

                        @block.scalar
                        def _(s):
                            ctx.wait(s, "qmm")
                            s.activation(qk_t[:, (2 * oc) * 512:(2 * oc + 1) * 512],
                                         qps[0:64, :], AF.Exp)
                            i = s.activation(qk_t[:, (2 * oc + 1) * 512:(2 * oc + 2) * 512],
                                             qps[64:128, :], AF.Exp)
                            ctx.inc(i, "qexp")

                    for h in range(16):
                        @block.tensor
                        def _(t):
                            if it == 0 and h == 0:
                                ctx.wait(t, "kvb")
                            ctx.wait(t, "qexp")
                            if not (it == 0 and h == 0):
                                ctx.wait(t, "ynorm")
                            i = t.matmul(yps[:, :], kv_b[:, h * 65:(h + 1) * 65],
                                         qk_t[0:64, h * 512:(h + 1) * 512],
                                         start=True, stop=True)
                            ctx.inc(i, "ymm")

                        @block.vector
                        def _(v):
                            ctx.wait(v, "ymm")
                            i = v.tensor_scalar(drow[0:1, :], yps[64:65, :], 1e-6, None,
                                                ALU.add)
                            ctx.inc(i, "dadd")
                            ctx.wait(v, "dadd")
                            i = v.reciprocal(drow[0:1, :], drow[0:1, :])
                            ctx.inc(i, "drcp")
                            ctx.wait(v, "drcp")
                            i = v.tensor_copy(ysc[:, :], yps[0:64, :])
                            ctx.inc(i, "yev")

                        @block.tensor
                        def _(t):
                            ctx.wait(t, "drcp")
                            i = t.matmul(ybc[:, :], C["ones_rowf"][0:1, 0:64],
                                         drow[0:1, :], start=True, stop=True)
                            ctx.inc(i, "ybcmm")

                        @block.vector
                        def _(v):
                            ctx.wait(v, "ybcmm")
                            ctx.wait(v, "yev")
                            i = v.tensor_tensor(
                                yfm_t[(h % 2) * 64:(h % 2) * 64 + 64,
                                      (h // 2) * 512:(h // 2 + 1) * 512],
                                ysc[:, :], ybc[:, :], ALU.mult)
                            ctx.inc(i, "ynorm")

                    for oc in range(DC):
                        @block.tensor
                        def _(t):
                            ctx.wait(t, "ynorm")
                            if it == 0 and oc == 0:
                                ctx.wait(t, "ld_wo")
                            if not (it == 0 and oc == 0):
                                ctx.wait(t, "x1w")
                            for c in range(DC):
                                t.matmul(ops[:, :], wo_sb[:, c * D + oc * P: c * D + (oc + 1) * P],
                                         yfm_t[:, c * 512:(c + 1) * 512],
                                         start=(c == 0), stop=False)
                            i = t.matmul(ops[:, :], bo_sb[0:1, oc * P:(oc + 1) * P],
                                         tokrow_b[0:1, :], start=False, stop=True)
                            ctx.inc(i, "omm")

                        @block.vector
                        def _(v):
                            ctx.wait(v, "omm")
                            ctx.wait(v, f"ld_x0_{it}")
                            if it > 0 and oc == 0:
                                ctx.wait(v, f"st_x1_{it - 1}")
                            i = v.tensor_tensor(x1t[:, oc * 512:(oc + 1) * 512],
                                                ops[:, :],
                                                x0t[:, oc * 512:(oc + 1) * 512],
                                                ALU.add)
                            ctx.inc(i, "x1w")

                    @block.sync
                    def _(s):
                        ctx.wait(s, "x1w")
                        if it == 0:
                            for c in range(DC):
                                dma(ctx, s, dbg_x0t[c * P:(c + 1) * P, :],
                                    x0t[:, c * 512:(c + 1) * 512], "dbg0")
                        for c in range(DC):
                            dma(ctx, s, x1d[c * P:(c + 1) * P, tt:tt + 512],
                                x1t[:, c * 512:(c + 1) * 512], f"st_x1_{it}")

                    @block.sync
                    def _(s):
                        ctx.wait(s, "x1w")

                ctx.es = es_outer

            ln1b = sb(ctx, "ln1b", [P, DC * TC], BF16)
            routed = sb(ctx, "routed", [P, DC * TC], F32)
            cwT = sb(ctx, "cwT", [16, TC], F32)
            l2row = sb(ctx, "l2row", [1, TC], F32)
            gwn_sb = sb(ctx, "gwn_sb", [P, DC * 16], F32)
            gam_sb = sb(ctx, "gam_sb", [P, DC], F32)
            bet_sb = sb(ctx, "bet_sb", [P, DC], F32)
            id_sb = sb(ctx, "id_sb", [P, P], F32)

            with ExitStack() as es2:
                ctx.es, es_outer = es2, ctx.es
                gwT_sb = sb(ctx, "gwT_sb", [P, DC * 16], F32)
                gsq = sb(ctx, "gsq", [P, DC * 16], F32)
                grow = sb(ctx, "grow", [1, 16], F32)
                x1t2 = sb(ctx, "x1t2", [P, DC * 512], F32)
                lnf_t = sb(ctx, "lnf_t", [P, DC * 512], F32)
                sc_sb = sb(ctx, "sc_sb", [P, 16], F32)
                msc = sb(ctx, "msc", [P, 16], F32)
                sel = sb(ctx, "sel", [P, 16], F32)
                selw = sb(ctx, "selw", [P, 16], F32)
                mrow = sb(ctx, "mrow", [P, 1], F32)
                wsum = sb(ctx, "wsum", [P, 1], F32)
                ic_sb = sb(ctx, "ic_sb", [P, 1], F32)
                zps = ps(ctx, "zps", [P, 512], F32)
                icps = ps(ctx, "icps", [P, 512], F32)
                trps = ps(ctx, "trps", [16, P], F32)

                @block.sync
                def _(s):
                    ctx.wait(s, f"st_x1_{NT4 - 1}")
                    for c in range(DC):
                        dma(ctx, s, gwT_sb[:, c * 16:(c + 1) * 16],
                            gwT[c * P:(c + 1) * P, :], "ld_gw")
                    dma(ctx, s, gam_sb[:, :], gam, "ld_gb")
                    dma(ctx, s, bet_sb[:, :], bet, "ld_gb")
                    dma(ctx, s, id_sb[:, :], ident, "ld_gb")

                @block.scalar
                def _(s):
                    ctx.wait(s, "ld_gw")
                    for c in range(DC):
                        i = s.activation(gsq[:, c * 16:(c + 1) * 16],
                                         gwT_sb[:, c * 16:(c + 1) * 16], AF.Square)
                    ctx.inc(i, "gsq")

                @block.tensor
                def _(t):
                    ctx.wait(t, "gsq")
                    ctx.wait(t, "consts")
                    for c in range(DC):
                        i = t.matmul(zps[0:1, 0:16], C["ones_colf"][:, 0:1],
                                     gsq[:, c * 16:(c + 1) * 16],
                                     start=(c == 0), stop=(c == DC - 1))
                    ctx.inc(i, "gnmm")

                @block.vector
                def _(v):
                    ctx.wait(v, "gnmm")
                    i = v.tensor_copy(grow[0:1, :], zps[0:1, 0:16])
                    ctx.inc(i, "gn1")

                @block.scalar
                def _(s):
                    ctx.wait(s, "gn1")
                    i = s.activation(grow[0:1, :], grow[0:1, :], AF.Sqrt)
                    ctx.inc(i, "gn2")

                @block.vector
                def _(v):
                    ctx.wait(v, "gn2")
                    i = v.tensor_scalar(grow[0:1, :], grow[0:1, :], 1e-12, None, ALU.max)
                    ctx.inc(i, "gn3")
                    ctx.wait(v, "gn3")
                    i = v.reciprocal(grow[0:1, :], grow[0:1, :])
                    ctx.inc(i, "gn4")

                @block.tensor
                def _(t):
                    ctx.wait(t, "gn4")
                    i = t.matmul(icps[:, 0:16], C["ones_rowf"][0:1, :], grow[0:1, :],
                                 start=True, stop=True)
                    ctx.inc(i, "gn5")

                @block.vector
                def _(v):
                    ctx.wait(v, "gn5")
                    for c in range(DC):
                        i = v.tensor_tensor(gwn_sb[:, c * 16:(c + 1) * 16],
                                            gwT_sb[:, c * 16:(c + 1) * 16],
                                            icps[:, 0:16], ALU.mult)
                    ctx.inc(i, "gwn")
                    for c in range(DC):
                        i = v.memset(routed[:, c * TC:(c + 1) * TC], 0.0)
                    ctx.inc(i, "rz")

                for it in range(NT4):
                    tt = it * 512

                    @block.sync
                    def _(s):
                        if it > 0:
                            ctx.wait(s, f"route_{it - 1}")
                            ctx.wait(s, "ln1bc")
                        for c in range(DC):
                            dma(ctx, s, x1t2[:, c * 512:(c + 1) * 512],
                                x1d[c * P:(c + 1) * P, tt:tt + 512], f"ld_x1r_{it}")

                    emit_ln_fm(ctx, block, C, x1t2, 512, f"B2ln_{it}",
                               out_f32=lnf_t,
                               l2inv=l2row, l2off=-tt,
                               wait_names=(f"ld_x1r_{it}",))

                    @block.vector
                    def _(v):
                        ctx.wait(v, "ln_apply")
                        for c in range(DC):
                            i = v.tensor_copy(ln1b[:, c * TC + tt: c * TC + tt + 512],
                                              lnf_t[:, c * 512:(c + 1) * 512])
                        ctx.inc(i, "ln1bc")

                    for tb in range(4):
                        tg = tt + tb * P

                        @block.tensor
                        def _(t):
                            ctx.wait(t, "ln_apply")
                            ctx.wait(t, "gwn")
                            if not (it == 0 and tb == 0):
                                ctx.wait(t, "lgts")
                            for c in range(DC):
                                t.matmul(zps[:, 0:16],
                                         lnf_t[:, c * 512 + tb * P: c * 512 + (tb + 1) * P],
                                         gwn_sb[:, c * 16:(c + 1) * 16],
                                         start=(c == 0), stop=(c == DC - 1))
                            i = t.matmul(icps[:, 0:1], l2row[0:1, tg:tg + P],
                                         onesq[0:1, 0:1], start=True, stop=True)
                            ctx.inc(i, "zmm")

                        @block.vector
                        def _(v):
                            ctx.wait(v, "zmm")
                            i = v.tensor_copy(ic_sb[:, 0:1], icps[:, 0:1])
                            ctx.inc(i, "icc")
                            ctx.wait(v, "icc")
                            i = v.tensor_scalar(sc_sb[:, :], zps[:, 0:16], ic_sb[:, 0:1],
                                                None, ALU.mult)
                            ctx.inc(i, "lgts")

                        @block.scalar
                        def _(s):
                            ctx.wait(s, "lgts")
                            i = s.activation(sc_sb[:, :], sc_sb[:, :], AF.Sigmoid)
                            ctx.inc(i, "sig")

                        @block.vector
                        def _(v):
                            ctx.wait(v, "sig")
                            v.tensor_copy(msc[:, :], sc_sb[:, :])
                            v.memset(sel[:, :], 0.0)
                            i = v.memset(wsum[:, :], 0.0)
                            ctx.inc(i, "tk0")
                            ctx.wait(v, "tk0")
                            for kk in range(8):
                                i = v.reduce_max(mrow[:, 0:1], msc[:, :], axis=AX.X)
                                ctx.inc(i, "tkr")
                                ctx.wait(v, "tkr")
                                v.tensor_tensor(wsum[:, :], wsum[:, :], mrow[:, :], ALU.add)
                                i = v.tensor_scalar(selw[:, :], msc[:, :], mrow[:, 0:1], None,
                                                    ALU.is_ge)
                                ctx.inc(i, "tkg")
                                ctx.wait(v, "tkg")
                                v.tensor_tensor(sel[:, :], sel[:, :], selw[:, :], ALU.add)
                                i = v.tensor_scalar(selw[:, :], selw[:, :], 1e30, None,
                                                    ALU.mult)
                                ctx.inc(i, "tks")
                                ctx.wait(v, "tks")
                                i = v.tensor_tensor(msc[:, :], msc[:, :], selw[:, :],
                                                    ALU.subtract)
                                ctx.inc(i, "tki")
                                ctx.wait(v, "tki")
                            i = v.reciprocal(wsum[:, :], wsum[:, :])
                            ctx.inc(i, "tkw")
                            ctx.wait(v, "tkw")
                            i = v.tensor_scalar(sel[:, :], sel[:, :], 1.0, None, ALU.min)
                            ctx.inc(i, "tkm")
                            ctx.wait(v, "tkm")
                            i = v.tensor_tensor(selw[:, :], sel[:, :], sc_sb[:, :], ALU.mult)
                            ctx.inc(i, "tkm")
                            ctx.wait(v, "tkm")
                            i = v.tensor_scalar(selw[:, :], selw[:, :], wsum[:, 0:1], None,
                                                ALU.mult)
                            ctx.inc(i, "comb")

                        @block.tensor
                        def _(t):
                            ctx.wait(t, "comb")
                            if it == 0 and tb == 0:
                                ctx.wait(t, "ld_gb")
                            i = t.transpose(trps[:, :], selw[:, :], id_sb[:, :])
                            ctx.inc(i, "trmm")

                        @block.vector
                        def _(v):
                            ctx.wait(v, "trmm")
                            i = v.tensor_copy(cwT[:, tg:tg + P], trps[:, :])
                            ctx.inc(i, f"route_{it}" if tb == 3 else "cwc")
                            if tb == 3:
                                ctx.inc(v.engine_nop(), "cwc")

                ctx.es = es_outer

            with ExitStack() as es3:
                ctx.es, es_outer = es3, ctx.es
                wg_b = [sb(ctx, f"wg_b{j}", [P, DC * 512], BF16) for j in range(2)]
                wu_b = [sb(ctx, f"wu_b{j}", [P, DC * 512], BF16) for j in range(2)]
                wd_b = [sb(ctx, f"wd_b{j}", [P, 4 * D], BF16) for j in range(2)]
                sg_sb = sb(ctx, "sg_sb", [P, 512], F32)
                h_sb = sb(ctx, "h_sb", [P, 512], F32)
                hs_sb = sb(ctx, "hs_sb", [P, 4 * 512], BF16)
                cwb_sb = sb(ctx, "cwb_sb", [P, 512], F32)
                cwrow = sb(ctx, "cwrow", [1, 512], F32)
                gps_ = ps(ctx, "gps", [P, 512], F32)
                ups_ = ps(ctx, "ups", [P, 512], F32)
                dps_ = ps(ctx, "dps", [P, 512], F32)
                cps_ = ps(ctx, "cps", [P, 512], F32)
                ldtags = []

                for e in range(17):
                    par = e % 2

                    @block.gpsimd
                    def _(g):
                        if e == 0:
                            ctx.wait(g, f"route_{NT4 - 1}")
                        if e >= 2:
                            ctx.wait(g, "edone", e - 1)
                        for c in range(DC):
                            if e < 16:
                                dma(ctx, g, wg_b[par][:, c * 512:(c + 1) * 512],
                                    wg[e * D + c * P: e * D + (c + 1) * P, :], f"wld{par}")
                                dma(ctx, g, wu_b[par][:, c * 512:(c + 1) * 512],
                                    wu[e * D + c * P: e * D + (c + 1) * P, :], f"wld{par}")
                            else:
                                dma(ctx, g, wg_b[par][:, c * 512:(c + 1) * 512],
                                    swg[c * P:(c + 1) * P, :], f"wld{par}")
                                dma(ctx, g, wu_b[par][:, c * 512:(c + 1) * 512],
                                    swu[c * P:(c + 1) * P, :], f"wld{par}")
                        for hc in range(4):
                            if e < 16:
                                dma(ctx, g, wd_b[par][:, hc * D:(hc + 1) * D],
                                    wd[e * 512 + hc * P: e * 512 + (hc + 1) * P, :], f"wld{par}")
                            else:
                                dma(ctx, g, wd_b[par][:, hc * D:(hc + 1) * D],
                                    swd[hc * P:(hc + 1) * P, :], f"wld{par}")
                    ldtags.append(ctx.val(f"wld{par}"))

                    for it in range(NT4):
                        tt = it * 512

                        if e < 16:
                            @block.sync
                            def _(sy):
                                if e == 0 and it == 0:
                                    ctx.wait(sy, f"route_{NT4 - 1}")
                                if not (e == 0 and it == 0):
                                    ctx.wait(sy, "cwmm")
                                dma(ctx, sy, cwrow[0:1, :], cwT[e:e + 1, tt:tt + 512],
                                    "cwdma")

                            @block.tensor
                            def _(t):
                                ctx.wait(t, "cwdma")
                                if e == 0 and it == 0:
                                    ctx.wait(t, "rz")
                                if not (e == 0 and it == 0):
                                    ctx.wait(t, "cwbc")
                                i = t.matmul(cps_[:, :], C["ones_rowf"][0:1, :],
                                             cwrow[0:1, :], start=True, stop=True)
                                ctx.inc(i, "cwmm")

                            @block.vector
                            def _(v):
                                ctx.wait(v, "cwmm")
                                i = v.tensor_copy(cwb_sb[:, :], cps_[:, :])
                                ctx.inc(i, "cwbc")

                        for hc in range(4):
                            @block.tensor
                            def _(t):
                                ctx.wait(t, f"wld{par}", ldtags[-1])
                                if not (e == 0 and it == 0 and hc == 0):
                                    ctx.wait(t, "hsil")
                                for c in range(DC):
                                    i = t.matmul(gps_[:, :],
                                                 wg_b[par][:, c * 512 + hc * P: c * 512 + (hc + 1) * P],
                                                 ln1b[:, c * TC + tt: c * TC + tt + 512],
                                                 start=(c == 0), stop=(c == DC - 1))
                                ctx.inc(i, "gmm")
                                if not (e == 0 and it == 0 and hc == 0):
                                    ctx.wait(t, "hmul")
                                for c in range(DC):
                                    i = t.matmul(ups_[:, :],
                                                 wu_b[par][:, c * 512 + hc * P: c * 512 + (hc + 1) * P],
                                                 ln1b[:, c * TC + tt: c * TC + tt + 512],
                                                 start=(c == 0), stop=(c == DC - 1))
                                ctx.inc(i, "umm")

                            @block.scalar
                            def _(s):
                                ctx.wait(s, "gmm")
                                if not (e == 0 and it == 0 and hc == 0):
                                    ctx.wait(s, "hmul")
                                i = s.activation(sg_sb[:, :], gps_[:, :], AF.Silu)
                                ctx.inc(i, "hsil")

                            @block.vector
                            def _(v):
                                ctx.wait(v, "umm")
                                ctx.wait(v, "hsil")
                                if e < 16:
                                    if not (e == 0 and it == 0 and hc == 0):
                                        ctx.wait(v, "hcw")
                                    i = v.tensor_tensor(h_sb[:, :], sg_sb[:, :], ups_[:, :],
                                                        ALU.mult)
                                    ctx.inc(i, "hmul")
                                else:
                                    i = v.tensor_tensor(hs_sb[:, hc * 512:(hc + 1) * 512],
                                                        sg_sb[:, :], ups_[:, :], ALU.mult)
                                    ctx.inc(i, "hmul")
                                    ctx.inc(v.engine_nop(), "hcw")

                            if e < 16:
                                @block.gpsimd
                                def _(g):
                                    ctx.wait(g, "hmul")
                                    ctx.wait(g, "cwbc")
                                    i = g.tensor_tensor(hs_sb[:, hc * 512:(hc + 1) * 512],
                                                        h_sb[:, :], cwb_sb[:, :], ALU.mult)
                                    ctx.inc(i, "hcw")

                        for oc in range(DC):
                            @block.tensor
                            def _(t):
                                ctx.wait(t, "hcw")
                                if not (e == 0 and it == 0 and oc == 0):
                                    ctx.wait(t, "racc")
                                for hc in range(4):
                                    i = t.matmul(dps_[:, :],
                                                 wd_b[par][:, hc * D + oc * P: hc * D + (oc + 1) * P],
                                                 hs_sb[:, hc * 512:(hc + 1) * 512],
                                                 start=(hc == 0), stop=(hc == 3))
                                ctx.inc(i, "dmm")

                            @block.vector
                            def _(v):
                                ctx.wait(v, "dmm")
                                i = v.tensor_tensor(routed[:, oc * TC + tt: oc * TC + tt + 512],
                                                    routed[:, oc * TC + tt: oc * TC + tt + 512],
                                                    dps_[:, :], ALU.add)
                                ctx.inc(i, "racc")

                    @block.gpsimd
                    def _(g):
                        ctx.wait(g, "racc")
                        ctx.inc(g.engine_nop(), "edone")

                ctx.es = es_outer

            with ExitStack() as es4:
                ctx.es, es_outer = es4, ctx.es
                x1t3 = sb(ctx, "x1t3", [P, DC * 512], F32)
                x2t = sb(ctx, "x2t", [P, DC * 512], F32)
                lnA = sb(ctx, "lnA", [P, DC * 512], F32)
                ut = sb(ctx, "ut", [P, DC * 512], F32)

                for it in range(NT4):
                    tt = it * 512

                    @block.sync
                    def _(s):
                        ctx.wait(s, "edone", 17)
                        if it > 0:
                            ctx.wait(s, f"x2st_{it - 1}")
                        for c in range(DC):
                            dma(ctx, s, x1t3[:, c * 512:(c + 1) * 512],
                                x1d[c * P:(c + 1) * P, tt:tt + 512], f"ld_x1a_{it}")

                    @block.vector
                    def _(v):
                        ctx.wait(v, f"ld_x1a_{it}")
                        for c in range(DC):
                            i = v.tensor_tensor(x2t[:, c * 512:(c + 1) * 512],
                                                x1t3[:, c * 512:(c + 1) * 512],
                                                routed[:, c * TC + tt: c * TC + tt + 512],
                                                ALU.add)
                        ctx.inc(i, f"x2_{it}")

                    @block.sync
                    def _(s):
                        ctx.wait(s, f"x2_{it}")
                        for c in range(DC):
                            dma(ctx, s, x2o[c * P:(c + 1) * P, tt:tt + 512],
                                x2t[:, c * 512:(c + 1) * 512], f"x2st_{it}")

                    emit_ln_fm(ctx, block, C, x2t, 512, f"lnA_{it}",
                               out_f32=lnA, wait_names=(f"x2_{it}",))
                    emit_ln_fm(ctx, block, C, lnA, 512, f"lnB_{it}",
                               out_f32=ut, gamma=gam_sb, beta=bet_sb,
                               wait_names=("ld_gb",))

                    @block.sync
                    def _(s):
                        ctx.wait(s, "ln_apply")
                        for c in range(DC):
                            dma(ctx, s, uo[c * P:(c + 1) * P, tt:tt + 512],
                                ut[:, c * 512:(c + 1) * 512], f"ust_{it}")

                    @block.vector
                    def _(v):
                        ctx.wait(v, f"ust_{it}")
                        ctx.inc(v.engine_nop(), f"asm_{it}")

                @block.sync
                def _(s):
                    dma(ctx, s, dbg_cw, cwT[:, :], "dbgst")
                    dma(ctx, s, dbg_l2, l2row[0:1, :], "dbgst")
                    for c in range(DC):
                        dma(ctx, s, dbg_ln1[c * P:(c + 1) * P, :],
                            ln1b[:, c * TC:(c + 1) * TC], "dbgst")
                    ctx.wait(s, f"x2st_{NT4 - 1}")
                    ctx.wait(s, f"ust_{NT4 - 1}")
                    ctx.wait(s, "dbgst")
                    ctx.wait(s, "dbg0")
                ctx.es = es_outer
    return nc


def run_B(inputs, shards, pkv, ln0):
    nc = build_B()
    wqT = np.ascontiguousarray(np.asarray(inputs["L0_attn_Wq"], np.float32).T)
    woT = np.ascontiguousarray(np.asarray(inputs["L0_attn_Wo"], np.float32).T)
    bq_ = np.asarray(inputs["L0_attn_bq"], np.float32).reshape(1, D)
    bo_ = np.asarray(inputs["L0_attn_bo"], np.float32).reshape(1, D)
    gwT = np.ascontiguousarray(np.asarray(inputs["L0_ffn_gate_w"], np.float32).T)
    wg_ = np.ascontiguousarray(np.asarray(inputs["L0_ffn_w_gate"], np.float32).reshape(16 * D, 512))
    wu_ = np.ascontiguousarray(np.asarray(inputs["L0_ffn_w_up"], np.float32).reshape(16 * D, 512))
    wd_ = np.ascontiguousarray(np.asarray(inputs["L0_ffn_w_down"], np.float32).reshape(16 * 512, D))
    swg_ = np.ascontiguousarray(np.asarray(inputs["L0_ffn_sw_gate"], np.float32))
    swu_ = np.ascontiguousarray(np.asarray(inputs["L0_ffn_sw_up"], np.float32))
    swd_ = np.ascontiguousarray(np.asarray(inputs["L0_ffn_sw_down"], np.float32))
    gam = np.ascontiguousarray(np.asarray(inputs["L1_attn_gamma"], np.float32).reshape(DC, P).T)
    bet = np.ascontiguousarray(np.asarray(inputs["L1_attn_beta"], np.float32).reshape(DC, P).T)
    ident = np.eye(P, dtype=np.float32)
    pkvs = np.ascontiguousarray(pkv.transpose(1, 0, 2).reshape(64, 8 * 1040))
    in_maps = [{"x0": shards[c], "ln0": ln0[c], "pkvs": pkvs, "wqT": wqT, "bq": bq_,
                "woT": woT, "bo": bo_, "gwT": gwT, "wg": wg_, "wu": wu_, "wd": wd_,
                "swg": swg_, "swu": swu_, "swd": swd_, "gam": gam, "bet": bet,
                "ident": ident}
               for c in range(NCORES)]
    res = run_bass_kernel_spmd(nc, in_maps, core_ids=list(range(NCORES)))
    x2 = [res.results[c]["x2o"] for c in range(NCORES)]
    u = [res.results[c]["uo"] for c in range(NCORES)]
    x1 = [res.results[c]["x1o"] for c in range(NCORES)]
    return x2, u, x1


NFFT = 2 * T
KSH = NFFT // NCORES


def vop(ctx, v, i, name="vtick"):
    ctx.inc(i, name)
    ctx.wait(v, name)
    return i


def build_C0():
    nc = bass.Bass("TRN2", target_bir_lowering=False)
    Lr4 = nc.dram_tensor("Lr4", [P, 4], F32, kind="ExternalInput").ap()
    Li4 = nc.dram_tensor("Li4", [P, 4], F32, kind="ExternalInput").ap()
    CrT = nc.dram_tensor("CrT", [512, D], F32, kind="ExternalInput").ap()
    CiT = nc.dram_tensor("CiT", [512, D], F32, kind="ExternalInput").ap()
    wr = nc.dram_tensor("wr", [1, KSH], F32, kind="ExternalInput").ap()
    wi = nc.dram_tensor("wi", [1, KSH], F32, kind="ExternalInput").ap()
    khr = nc.dram_tensor("khr", [D, KSH], F32, kind="ExternalOutput").ap()
    khi = nc.dram_tensor("khi", [D, KSH], F32, kind="ExternalOutput").ap()
    TWO_PI = float(2 * np.pi)
    INV2PI = float(1.0 / (2 * np.pi))

    with ExitStack() as es:
        ctx = Ctx(nc, es)
        C = make_consts(ctx)
        sc = sb(ctx, "c0_sc", [P, 80], F32)
        iq = sb(ctx, "c0_iq", [P, 4], mybir.dt.int32)
        cprT = sb(ctx, "cprT", [P, 4 * D], F32)
        cpiT = sb(ctx, "cpiT", [P, 4 * D], F32)
        cprb = sb(ctx, "cprb", [P, 4 * D], BF16)
        cpib = sb(ctx, "cpib", [P, 4 * D], BF16)
        crT_sb = sb(ctx, "crT_sb", [P, 4 * D], F32)
        ciT_sb = sb(ctx, "ciT_sb", [P, 4 * D], F32)
        wrow = sb(ctx, "wrow", [1, 2 * KSH], F32)
        wrb = sb(ctx, "wrb", [P, KSH], F32)
        wib = sb(ctx, "wib", [P, KSH], F32)
        g_r = sb(ctx, "g_r", [P, 4 * 512], BF16)
        g_i = sb(ctx, "g_i", [P, 4 * 512], BF16)
        g_in = sb(ctx, "g_in", [P, 4 * 512], BF16)
        t1 = sb(ctx, "c0t1", [P, D], F32)
        t2 = sb(ctx, "c0t2", [P, D], F32)
        t3 = sb(ctx, "c0t3", [P, D], F32)
        t4 = sb(ctx, "c0t4", [P, D], F32)
        khev = sb(ctx, "khev", [P, 1024], F32)

        def S(i0):
            return sc[:, i0:i0 + 4]

        with nc.Block() as block:
            @block.vector
            def _(v):
                v.memset(C["ones_colf"][:, :], 1.0)
                i = v.memset(C["ones_rowf"][:, :], 1.0)
                ctx.inc(i, "consts")

            @block.sync
            def _(s):
                dma(ctx, s, sc[:, 0:4], Lr4, "ld0")
                dma(ctx, s, sc[:, 4:8], Li4, "ld0")
                for c in range(4):
                    dma(ctx, s, crT_sb[:, c * D:(c + 1) * D],
                        CrT[c * P:(c + 1) * P, :], "ld0")
                    dma(ctx, s, ciT_sb[:, c * D:(c + 1) * D],
                        CiT[c * P:(c + 1) * P, :], "ld0")
                dma(ctx, s, wrow[0:1, 0:KSH], wr, "ld0")
                dma(ctx, s, wrow[0:1, KSH:2 * KSH], wi, "ld0")

            @block.scalar
            def _(s):
                ctx.wait(s, "ld0")
                s.activation(S(8), S(0), AF.Exp)
                i = s.activation(S(12), S(4), AF.Exp)
                ctx.inc(i, "c0a")

            def reduce_angle(v, src_col, dst_col, shift):
                i = v.tensor_scalar(S(dst_col), S(src_col), shift, None, ALU.add)
                vop(ctx, v, i)
                i = v.tensor_scalar(S(dst_col), S(dst_col), INV2PI, None, ALU.mult)
                vop(ctx, v, i)
                i = v.tensor_copy(iq[:, 0:4], S(dst_col))
                vop(ctx, v, i)
                i = v.tensor_copy(S(60), iq[:, 0:4])
                vop(ctx, v, i)
                i = v.tensor_tensor(S(dst_col), S(dst_col), S(60), ALU.subtract)
                vop(ctx, v, i)
                i = v.tensor_scalar(S(dst_col), S(dst_col), TWO_PI, None, ALU.mult)
                vop(ctx, v, i)
                i = v.tensor_scalar(S(56), S(dst_col), float(np.pi), None, ALU.is_gt)
                vop(ctx, v, i)
                i = v.tensor_scalar(S(56), S(56), TWO_PI, None, ALU.mult)
                vop(ctx, v, i)
                i = v.tensor_tensor(S(dst_col), S(dst_col), S(56), ALU.subtract)
                vop(ctx, v, i)
                i = v.tensor_scalar(S(56), S(dst_col), float(-np.pi), None, ALU.is_lt)
                vop(ctx, v, i)
                i = v.tensor_scalar(S(56), S(56), TWO_PI, None, ALU.mult)
                vop(ctx, v, i)
                i = v.tensor_tensor(S(dst_col), S(dst_col), S(56), ALU.add)
                vop(ctx, v, i)

            @block.vector
            def _(v):
                ctx.wait(v, "c0a")
                reduce_angle(v, 12, 48, 0.0)
                reduce_angle(v, 12, 52, float(np.pi / 2))
                ctx.inc(v.engine_nop(), "c0b")

            @block.scalar
            def _(s):
                ctx.wait(s, "c0b")
                s.activation(S(16), S(8), AF.Exp, scale=-1.0)
                s.activation(S(24), S(48), AF.Sin)
                i = s.activation(S(20), S(52), AF.Sin)
                ctx.inc(i, "c0c")

            @block.vector
            def _(v):
                ctx.wait(v, "c0c")
                i = v.tensor_tensor(S(28), S(16), S(20), ALU.mult)
                vop(ctx, v, i)
                i = v.tensor_tensor(S(32), S(16), S(24), ALU.mult)
                vop(ctx, v, i)
                i = v.tensor_tensor(S(48), S(8), S(8), ALU.mult)
                vop(ctx, v, i)
                i = v.tensor_tensor(S(36), S(12), S(12), ALU.mult)
                vop(ctx, v, i)
                i = v.tensor_tensor(S(36), S(36), S(48), ALU.add)
                vop(ctx, v, i)
                i = v.reciprocal(S(36), S(36))
                vop(ctx, v, i)
                i = v.tensor_scalar(S(48), S(28), -1.0, None, ALU.add)
                vop(ctx, v, i)
                i = v.tensor_tensor(S(40), S(48), S(8), ALU.mult)
                vop(ctx, v, i)
                i = v.tensor_scalar(S(40), S(40), -1.0, None, ALU.mult)
                vop(ctx, v, i)
                i = v.tensor_tensor(S(52), S(32), S(12), ALU.mult)
                vop(ctx, v, i)
                i = v.tensor_tensor(S(40), S(40), S(52), ALU.add)
                vop(ctx, v, i)
                i = v.tensor_tensor(S(40), S(40), S(36), ALU.mult)
                vop(ctx, v, i)
                i = v.tensor_tensor(S(44), S(32), S(8), ALU.mult)
                vop(ctx, v, i)
                i = v.tensor_scalar(S(44), S(44), -1.0, None, ALU.mult)
                vop(ctx, v, i)
                i = v.tensor_tensor(S(52), S(48), S(12), ALU.mult)
                vop(ctx, v, i)
                i = v.tensor_tensor(S(44), S(44), S(52), ALU.subtract)
                vop(ctx, v, i)
                i = v.tensor_tensor(S(44), S(44), S(36), ALU.mult)
                ctx.inc(i, "c0d")

            @block.vector
            def _(v):
                ctx.wait(v, "c0d")
                for c in range(4):
                    cs = slice(c * D, (c + 1) * D)
                    i = v.tensor_scalar(t1[:, :], crT_sb[:, cs], sc[:, 40 + c:41 + c],
                                        None, ALU.mult)
                    vop(ctx, v, i)
                    i = v.tensor_scalar(t2[:, :], ciT_sb[:, cs], sc[:, 44 + c:45 + c],
                                        None, ALU.mult)
                    vop(ctx, v, i)
                    i = v.tensor_tensor(cprT[:, cs], t1[:, :], t2[:, :], ALU.subtract)
                    vop(ctx, v, i)
                    i = v.tensor_scalar(t1[:, :], crT_sb[:, cs], sc[:, 44 + c:45 + c],
                                        None, ALU.mult)
                    vop(ctx, v, i)
                    i = v.tensor_scalar(t2[:, :], ciT_sb[:, cs], sc[:, 40 + c:41 + c],
                                        None, ALU.mult)
                    vop(ctx, v, i)
                    i = v.tensor_tensor(cpiT[:, cs], t1[:, :], t2[:, :], ALU.add)
                    vop(ctx, v, i)
                v.tensor_copy(cprb[:, :], cprT[:, :])
                i = v.tensor_copy(cpib[:, :], cpiT[:, :])
                ctx.inc(i, "cp_done")

            wps = ps(ctx, "wps", [P, 512], F32)
            NKT = KSH // 512

            for half in range(2):
                for kt in range(NKT):
                    @block.tensor
                    def _(t):
                        if half == 0 and kt == 0:
                            ctx.wait(t, "ld0")
                            ctx.wait(t, "consts")
                        else:
                            ctx.wait(t, "wbc_cp")
                        i = t.matmul(wps[:, :], C["ones_rowf"][0:1, :],
                                     wrow[0:1, half * KSH + kt * 512:
                                          half * KSH + (kt + 1) * 512],
                                     start=True, stop=True)
                        ctx.inc(i, "wbc_mm")

                    @block.vector
                    def _(v):
                        ctx.wait(v, "wbc_mm")
                        dst = wrb if half == 0 else wib
                        i = v.tensor_copy(dst[:, kt * 512:(kt + 1) * 512], wps[:, :])
                        ctx.inc(i, "wbc_cp")

            pr = ps(ctx, "pr", [P, 512], F32)
            pi_ = ps(ctx, "pi", [P, 512], F32)
            for kt in range(NKT):
                k0 = kt * 512

                @block.vector
                def _(v):
                    if kt == 0:
                        ctx.wait(v, "cp_done")
                        ctx.wait(v, "wbc_cp")
                    else:
                        ctx.wait(v, "khmm")
                    for c in range(4):
                        gsl = slice(c * 512, (c + 1) * 512)
                        i = v.tensor_scalar(t1[:, 0:512], wrb[:, k0:k0 + 512],
                                            sc[:, 28 + c:29 + c], None, ALU.mult)
                        vop(ctx, v, i)
                        i = v.tensor_scalar(t2[:, 0:512], wib[:, k0:k0 + 512],
                                            sc[:, 32 + c:33 + c], None, ALU.mult)
                        vop(ctx, v, i)
                        i = v.tensor_tensor(t1[:, 0:512], t1[:, 0:512], t2[:, 0:512],
                                            ALU.subtract)
                        vop(ctx, v, i)
                        i = v.tensor_scalar(t1[:, 0:512], t1[:, 0:512], -1.0, 1.0,
                                            ALU.mult, ALU.add)
                        vop(ctx, v, i)
                        i = v.tensor_scalar(t2[:, 0:512], wib[:, k0:k0 + 512],
                                            sc[:, 28 + c:29 + c], None, ALU.mult)
                        vop(ctx, v, i)
                        i = v.tensor_scalar(t3[:, 0:512], wrb[:, k0:k0 + 512],
                                            sc[:, 32 + c:33 + c], None, ALU.mult)
                        vop(ctx, v, i)
                        i = v.tensor_tensor(t2[:, 0:512], t2[:, 0:512], t3[:, 0:512],
                                            ALU.add)
                        vop(ctx, v, i)
                        i = v.tensor_tensor(t3[:, 0:512], t1[:, 0:512], t1[:, 0:512],
                                            ALU.mult)
                        vop(ctx, v, i)
                        i = v.tensor_tensor(t4[:, 0:512], t2[:, 0:512], t2[:, 0:512],
                                            ALU.mult)
                        vop(ctx, v, i)
                        i = v.tensor_tensor(t3[:, 0:512], t3[:, 0:512], t4[:, 0:512],
                                            ALU.add)
                        vop(ctx, v, i)
                        i = v.reciprocal(t3[:, 0:512], t3[:, 0:512])
                        vop(ctx, v, i)
                        i = v.tensor_tensor(g_r[:, gsl], t1[:, 0:512], t3[:, 0:512],
                                            ALU.mult)
                        vop(ctx, v, i)
                        i = v.tensor_tensor(t4[:, 0:512], t2[:, 0:512], t3[:, 0:512],
                                            ALU.mult)
                        vop(ctx, v, i)
                        i = v.tensor_copy(g_i[:, gsl], t4[:, 0:512])
                        vop(ctx, v, i)
                        i = v.tensor_scalar(g_in[:, gsl], t4[:, 0:512], -1.0, None,
                                            ALU.mult)
                        vop(ctx, v, i)
                    ctx.inc(v.engine_nop(), "gdone")

                for hc in range(DC):
                    @block.tensor
                    def _(t):
                        ctx.wait(t, "gdone")
                        if not (kt == 0 and hc == 0):
                            ctx.wait(t, "khcp")
                        for c in range(4):
                            lr = cprb[:, c * D + hc * P: c * D + (hc + 1) * P]
                            li = cpib[:, c * D + hc * P: c * D + (hc + 1) * P]
                            gsl = slice(c * 512, (c + 1) * 512)
                            t.matmul(pr[:, :], lr, g_r[:, gsl],
                                     start=(c == 0), stop=False)
                            t.matmul(pr[:, :], li, g_in[:, gsl],
                                     start=False, stop=(c == 3))
                            t.matmul(pi_[:, :], lr, g_i[:, gsl],
                                     start=(c == 0), stop=False)
                            i = t.matmul(pi_[:, :], li, g_r[:, gsl],
                                         start=False, stop=(c == 3))
                        ctx.inc(i, "khmm")

                    @block.vector
                    def _(v):
                        ctx.wait(v, "khmm")
                        v.tensor_copy(khev[:, 0:512], pr[:, :])
                        i = v.tensor_copy(khev[:, 512:1024], pi_[:, :])
                        ctx.inc(i, "khcp")

                    @block.sync
                    def _(s):
                        ctx.wait(s, "khcp")
                        dma(ctx, s, khr[hc * P:(hc + 1) * P, k0:k0 + 512],
                            khev[:, 0:512], "khst")
                        dma(ctx, s, khi[hc * P:(hc + 1) * P, k0:k0 + 512],
                            khev[:, 512:1024], "khst")
                    @block.vector
                    def _(v):
                        ctx.wait(v, "khst")

            @block.sync
            def _(s):
                ctx.wait(s, "khst")
    return nc


def run_C0(inputs):
    nc = build_C0()
    Lr4 = np.ascontiguousarray(np.asarray(inputs["L1_attn_Lambda_real"], np.float32).reshape(4, P).T)
    Li4 = np.ascontiguousarray(np.asarray(inputs["L1_attn_Lambda_imag"], np.float32).reshape(4, P).T)
    CrT = np.ascontiguousarray(np.asarray(inputs["L1_attn_C_real"], np.float32).T)
    CiT = np.ascontiguousarray(np.asarray(inputs["L1_attn_C_imag"], np.float32).T)
    kk = np.arange(NFFT, dtype=np.float64)
    om = np.exp(-2j * np.pi * kk / NFFT)
    in_maps = []
    for c in range(NCORES):
        sl = slice(c * KSH, (c + 1) * KSH)
        in_maps.append({"Lr4": Lr4, "Li4": Li4, "CrT": CrT, "CiT": CiT,
                        "wr": np.ascontiguousarray(om.real[None, sl].astype(np.float32)),
                        "wi": np.ascontiguousarray(om.imag[None, sl].astype(np.float32))})
    res = run_bass_kernel_spmd(nc, in_maps, core_ids=list(range(NCORES)))
    khr = np.concatenate([res.results[c]["khr"] for c in range(NCORES)], axis=1)
    khi = np.concatenate([res.results[c]["khi"] for c in range(NCORES)], axis=1)
    return khr, khi



def build_C2():
    nc = bass.Bass("TRN2", target_bir_lowering=False)
    NCH = P
    NB = 64
    u4 = nc.dram_tensor("u4", [NB, NCH * 256], F32, kind="ExternalInput").ap()
    W1 = nc.dram_tensor("W1", [NB, 2 * P], F32, kind="ExternalInput").ap()
    TWF = nc.dram_tensor("TWF", [256, 2 * P], F32, kind="ExternalInput").ap()
    W2 = nc.dram_tensor("W2", [256, 3 * 256], F32, kind="ExternalInput").ap()
    KHC = nc.dram_tensor("KHC", [P, 2 * 2 * P * P], F32, kind="ExternalInput").ap()
    WC = nc.dram_tensor("WC", [256, 3 * 256], F32, kind="ExternalInput").ap()
    TWC = nc.dram_tensor("TWC", [P, 2 * 256], F32, kind="ExternalInput").ap()
    WB = nc.dram_tensor("WB", [P, 2 * 64], F32, kind="ExternalInput").ap()
    yconv = nc.dram_tensor("yconv", [NCH, T], F32, kind="ExternalOutput").ap()
    NG = 4
    GC = 32

    with ExitStack() as es:
        ctx = Ctx(nc, es)
        C = make_consts(ctx)
        u_sb = sb(ctx, "u_sb", [NB, GC * 256], BF16)
        w1_sb = sb(ctx, "w1_sb", [NB, 2 * P], BF16)
        twf_sb = sb(ctx, "twf_sb", [P, 2 * 2 * P], F32)
        w2_sb = sb(ctx, "w2_sb", [P, 2 * 3 * 256], BF16)
        khc_g = sb(ctx, "khc_g", [P, 2 * 2 * GC * P], BF16)
        wc_sb = sb(ctx, "wc_sb", [P, 2 * 3 * 256], BF16)
        twc_sb = sb(ctx, "twc_sb", [P, 2 * 256], F32)
        wb_sb = sb(ctx, "wb_sb", [P, 2 * 64], BF16)
        s1r = sb(ctx, "s1r", [P, 2 * GC * P], BF16)
        s1i = sb(ctx, "s1i", [P, 2 * GC * P], BF16)
        s1pr = sb(ctx, "s1pr", [P, 2 * GC * P], BF16)
        s1pi = sb(ctx, "s1pi", [P, 2 * GC * P], BF16)
        vpr = sb(ctx, "vpr", [P, 2 * GC * P], BF16)
        vpi = sb(ctx, "vpi", [P, 2 * GC * P], BF16)
        pt1 = sb(ctx, "pt1", [P, 512], F32)
        pt2 = sb(ctx, "pt2", [P, 512], F32)
        rpr = sb(ctx, "rpr", [P, GC * 256], BF16)
        rpi = sb(ctx, "rpi", [P, GC * 256], BF16)
        ysb = sb(ctx, "ysb", [64, 512], F32)
        ps1r = ps(ctx, "ps1r", [P, 512], F32)
        ps1i = ps(ctx, "ps1i", [P, 512], F32)
        pvr = ps(ctx, "pvr", [P, 512], F32)
        pvi = ps(ctx, "pvi", [P, 512], F32)
        prr = ps(ctx, "prr", [P, 512], F32)
        pri = ps(ctx, "pri", [P, 512], F32)
        pyb = ps(ctx, "pyb", [64, 512], F32)

        with nc.Block() as block:
            @block.vector
            def _(v):
                i = v.memset(C["ones_rowf"][:, :], 1.0)
                ctx.inc(i, "consts")

            @block.gpsimd
            def _(g):
                dma(ctx, g, w1_sb[:, :], W1, "ldc")
                for ac in range(2):
                    dma(ctx, g, w2_sb[:, ac * 768:(ac + 1) * 768],
                        W2[ac * P:(ac + 1) * P, :], "ldc")
                    dma(ctx, g, wc_sb[:, ac * 768:(ac + 1) * 768],
                        WC[ac * P:(ac + 1) * P, :], "ldc")
                dma(ctx, g, wb_sb[:, :], WB, "ldc")

            @block.sync
            def _(s):
                for ac in range(2):
                    dma(ctx, s, twf_sb[:, ac * 256:(ac + 1) * 256],
                        TWF[ac * P:(ac + 1) * P, :], "ldc2")
                dma(ctx, s, twc_sb[:, :], TWC, "ldc2")

            for g in range(NG):
                @block.gpsimd
                def _(gg):
                    if g > 0:
                        ctx.wait(gg, "s1mm")
                        ctx.wait(gg, "vev")
                    dma(ctx, gg, u_sb[:, :],
                        u4[:, g * GC * 256:(g + 1) * GC * 256], f"ldu_{g}")
                    for k1c in range(2):
                        for ri in range(2):
                            dma(ctx, gg, khc_g[:, (ri * 2 + k1c) * GC * P:
                                                (ri * 2 + k1c + 1) * GC * P],
                                KHC[:, ri * 2 * P * P + k1c * P * P + g * GC * P:
                                    ri * 2 * P * P + k1c * P * P + (g + 1) * GC * P],
                                f"ldkh_{g}")

                for ac in range(2):
                    for ch in range(GC):
                        @block.tensor
                        def _(t):
                            if g == 0 and ac == 0 and ch == 0:
                                ctx.wait(t, "ldc")
                                ctx.wait(t, "consts")
                            ctx.wait(t, f"ldu_{g}")
                            if not (g == 0 and ac == 0 and ch == 0):
                                ctx.wait(t, "s1ev")
                            lhs = u_sb[0:NB, ch * 256 + ac * P: ch * 256 + (ac + 1) * P]
                            t.matmul(ps1r[:, 0:P], lhs, w1_sb[0:NB, 0:P],
                                     start=True, stop=True)
                            i = t.matmul(ps1i[:, 0:P], lhs, w1_sb[0:NB, P:2 * P],
                                         start=True, stop=True)
                            ctx.inc(i, "s1mm")

                        @block.vector
                        def _(v):
                            ctx.wait(v, "s1mm")
                            col = ac * GC * P + ch * P
                            v.tensor_copy(s1r[:, col:col + P], ps1r[:, 0:P])
                            i = v.tensor_copy(s1i[:, col:col + P], ps1i[:, 0:P])
                            ctx.inc(i, "s1ev")

                @block.vector
                def _(v):
                    ctx.wait(v, "s1ev")
                    for ac in range(2):
                        cs = slice(ac * GC * P, (ac + 1) * GC * P)
                        twr = twf_sb[:, ac * 256:ac * 256 + P] \
                            .rearrange("p (o k) -> p o k", o=1).broadcast_to([P, GC, P])
                        twi = twf_sb[:, ac * 256 + P:(ac + 1) * 256] \
                            .rearrange("p (o k) -> p o k", o=1).broadcast_to([P, GC, P])
                        sr = s1r[:, cs].rearrange("p (c k) -> p c k", c=GC)
                        si = s1i[:, cs].rearrange("p (c k) -> p c k", c=GC)
                        so_r = s1pr[:, cs].rearrange("p (c k) -> p c k", c=GC)
                        so_i = s1pi[:, cs].rearrange("p (c k) -> p c k", c=GC)
                        i = v.tensor_tensor(so_r, sr, twr, ALU.mult)
                        vop(ctx, v, i)
                        i = v.tensor_tensor(so_i, si, twr, ALU.mult)
                        vop(ctx, v, i)
                        i = v.tensor_tensor(si, si, twi, ALU.mult)
                        vop(ctx, v, i)
                        i = v.tensor_tensor(sr, sr, twi, ALU.mult)
                        vop(ctx, v, i)
                        i = v.tensor_tensor(so_r, so_r, si, ALU.subtract)
                        vop(ctx, v, i)
                        i = v.tensor_tensor(so_i, so_i, sr, ALU.add)
                        vop(ctx, v, i)
                    ctx.inc(v.engine_nop(), "s1tw")

                for k1c in range(2):
                    for nt in range(GC * P // 512):
                        @block.tensor
                        def _(t):
                            ctx.wait(t, "s1tw")
                            if not (g == 0 and k1c == 0 and nt == 0):
                                ctx.wait(t, "vev")
                            nsl = slice(nt * 512, (nt + 1) * 512)
                            for ac in range(2):
                                w2r = w2_sb[:, ac * 768 + k1c * P: ac * 768 + (k1c + 1) * P]
                                w2i = w2_sb[:, ac * 768 + 256 + k1c * P:
                                            ac * 768 + 256 + (k1c + 1) * P]
                                w2in = w2_sb[:, ac * 768 + 512 + k1c * P:
                                             ac * 768 + 512 + (k1c + 1) * P]
                                spr = s1pr[:, ac * GC * P:][:, nsl]
                                spi = s1pi[:, ac * GC * P:][:, nsl]
                                t.matmul(pvr[:, :], w2r, spr, start=(ac == 0), stop=False)
                                t.matmul(pvr[:, :], w2in, spi, start=False, stop=(ac == 1))
                                t.matmul(pvi[:, :], w2i, spr, start=(ac == 0), stop=False)
                                i = t.matmul(pvi[:, :], w2r, spi, start=False,
                                             stop=(ac == 1))
                            ctx.inc(i, "vmm")

                        @block.vector
                        def _(v):
                            ctx.wait(v, "vmm")
                            ctx.wait(v, f"ldkh_{g}")
                            ksl = slice(k1c * GC * P + nt * 512,
                                        k1c * GC * P + (nt + 1) * 512)
                            khr_s = khc_g[:, 0:2 * GC * P][:, ksl]
                            khi_s = khc_g[:, 2 * GC * P:][:, ksl]
                            i = v.tensor_tensor(pt1[:, :], khr_s, pvr[:, :], ALU.mult)
                            vop(ctx, v, i)
                            i = v.tensor_tensor(pt2[:, :], khi_s, pvi[:, :], ALU.mult)
                            vop(ctx, v, i)
                            vsl = slice(k1c * GC * P + nt * 512, k1c * GC * P + (nt + 1) * 512)
                            i = v.tensor_tensor(vpr[:, vsl], pt1[:, :], pt2[:, :],
                                                ALU.subtract)
                            vop(ctx, v, i)
                            i = v.tensor_tensor(pt1[:, :], khi_s, pvr[:, :], ALU.mult)
                            vop(ctx, v, i)
                            i = v.tensor_tensor(pt2[:, :], khr_s, pvi[:, :], ALU.mult)
                            vop(ctx, v, i)
                            i = v.tensor_tensor(vpi[:, vsl], pt1[:, :], pt2[:, :], ALU.add)
                            ctx.inc(i, "vev")

                for ch in range(GC):
                    @block.tensor
                    def _(t):
                        ctx.wait(t, "vev")
                        if not (g == 0 and ch == 0):
                            ctx.wait(t, "rev")
                        for k1c in range(2):
                            vr = vpr[:, k1c * GC * P + ch * P: k1c * GC * P + (ch + 1) * P]
                            vi = vpi[:, k1c * GC * P + ch * P: k1c * GC * P + (ch + 1) * P]
                            wcr = wc_sb[:, k1c * 768 + 0:k1c * 768 + 256]
                            wci = wc_sb[:, k1c * 768 + 256:k1c * 768 + 512]
                            wcin = wc_sb[:, k1c * 768 + 512:k1c * 768 + 768]
                            t.matmul(prr[:, 0:256], vr, wcr, start=(k1c == 0), stop=False)
                            t.matmul(prr[:, 0:256], vi, wcin, start=False, stop=(k1c == 1))
                            t.matmul(pri[:, 0:256], vr, wci, start=(k1c == 0), stop=False)
                            i = t.matmul(pri[:, 0:256], vi, wcr, start=False,
                                         stop=(k1c == 1))
                        ctx.inc(i, "rmm")

                    @block.vector
                    def _(v):
                        ctx.wait(v, "rmm")
                        twr = twc_sb[:, 0:256]
                        twi = twc_sb[:, 256:512]
                        i = v.tensor_tensor(pt1[:, 0:256], twr, prr[:, 0:256], ALU.mult)
                        vop(ctx, v, i)
                        i = v.tensor_tensor(pt2[:, 0:256], twi, pri[:, 0:256], ALU.mult)
                        vop(ctx, v, i)
                        rsl = slice(ch * 256, (ch + 1) * 256)
                        i = v.tensor_tensor(rpr[:, rsl], pt1[:, 0:256], pt2[:, 0:256],
                                            ALU.subtract)
                        vop(ctx, v, i)
                        i = v.tensor_tensor(pt1[:, 0:256], twi, prr[:, 0:256], ALU.mult)
                        vop(ctx, v, i)
                        i = v.tensor_tensor(pt2[:, 0:256], twr, pri[:, 0:256], ALU.mult)
                        vop(ctx, v, i)
                        i = v.tensor_tensor(rpi[:, rsl], pt1[:, 0:256], pt2[:, 0:256],
                                            ALU.add)
                        ctx.inc(i, "rev")

                for nt in range(GC * 256 // 512):
                    @block.tensor
                    def _(t):
                        ctx.wait(t, "rev")
                        if not (g == 0 and nt == 0):
                            ctx.wait(t, "yev")
                        nsl = slice(nt * 512, (nt + 1) * 512)
                        t.matmul(pyb[:, :], wb_sb[:, 0:64], rpr[:, nsl],
                                 start=True, stop=False)
                        i = t.matmul(pyb[:, :], wb_sb[:, 64:128], rpi[:, nsl],
                                     start=False, stop=True)
                        ctx.inc(i, "ymm")

                    @block.vector
                    def _(v):
                        ctx.wait(v, "ymm")
                        i = v.tensor_copy(ysb[:, :], pyb[:, :])
                        ctx.inc(i, "yev")

                    @block.sync
                    def _(s):
                        ctx.wait(s, "yev")
                        for chl in range(2):
                            ch = g * GC + (nt * 512 + chl * 256) // 256
                            i = s.dma_start(
                                out=yconv[ch:ch + 1, :]
                                .rearrange("o (m r) -> (o m) r", r=256),
                                in_=ysb[:, chl * 256:(chl + 1) * 256])
                            ctx.inc(i, "yst", 16)
                    @block.vector
                    def _(v):
                        ctx.wait(v, "yst")

            @block.sync
            def _(s):
                ctx.wait(s, "yst")
    return nc


def _c2_consts():
    b = np.arange(64)[:, None]
    k2 = np.arange(P)[None, :]
    W1c = np.exp(-2j * np.pi * b * k2 / P)
    W1 = np.concatenate([W1c.real, W1c.imag], axis=1).astype(np.float32)
    a = np.arange(256)[:, None]
    TWFc = np.exp(-2j * np.pi * a * k2 / NFFT)
    TWF = np.concatenate([TWFc.real, TWFc.imag], axis=1).astype(np.float32)
    k1 = np.arange(256)[None, :]
    W2c = np.exp(-2j * np.pi * a * k1 / 256)
    W2 = np.concatenate([W2c.real, W2c.imag, -W2c.imag], axis=1).astype(np.float32)
    r = np.arange(256)[None, :]
    WCc = np.exp(2j * np.pi * a * r / 256)
    WC = np.concatenate([WCc.real, WCc.imag, -WCc.imag], axis=1).astype(np.float32)
    k2c = np.arange(P)[:, None]
    TWCc = np.exp(2j * np.pi * k2c * r / NFFT)
    TWC = np.concatenate([TWCc.real, TWCc.imag], axis=1).astype(np.float32)
    m = np.arange(64)[None, :]
    WBc = np.exp(2j * np.pi * k2c * m / P) / NFFT
    WB = np.concatenate([WBc.real, -WBc.imag], axis=1).astype(np.float32)
    return W1, TWF, W2, WC, TWC, WB


def run_C2(u_full, khr, khi):
    nc = build_C2()
    W1, TWF, W2, WC, TWC, WB = _c2_consts()
    u3 = u_full.reshape(64, 256, D)
    in_maps = []
    for c in range(NCORES):
        sl = slice(c * P, (c + 1) * P)
        u4 = np.ascontiguousarray(u3[:, :, sl].transpose(0, 2, 1).reshape(64, P * 256))
        kh = []
        for arr in (khr, khi):
            A = arr[sl].reshape(P, 256, P)
            B = A.transpose(1, 0, 2).reshape(2, P, P, P)
            kh.append(B.transpose(1, 0, 2, 3).reshape(P, 2 * P * P))
        KHC = np.ascontiguousarray(np.concatenate(kh, axis=1))
        in_maps.append({"u4": u4, "W1": W1, "TWF": TWF, "W2": W2, "KHC": KHC,
                        "WC": WC, "TWC": TWC, "WB": WB})
    res = run_bass_kernel_spmd(nc, in_maps, core_ids=list(range(NCORES)))
    return np.concatenate([res.results[c]["yconv"] for c in range(NCORES)], axis=0)


HALO = 2048
ND = HALO + TC
HID2 = 2816
NHC = HID2 // P


def build_D():
    nc = bass.Bass("TRN2", target_bir_lowering=False)
    x2h = nc.dram_tensor("x2h", [D, ND], F32, kind="ExternalInput").ap()
    ych = nc.dram_tensor("ych", [D, ND], F32, kind="ExternalInput").ap()
    uh = nc.dram_tensor("uh", [D, ND], F32, kind="ExternalInput").ap()
    pD = nc.dram_tensor("pD", [P, DC], F32, kind="ExternalInput").ap()
    wqkvI = nc.dram_tensor("wqkvI", [2 * D, 1536], F32, kind="ExternalInput").ap()
    woI = nc.dram_tensor("woI", [2 * 512, D], F32, kind="ExternalInput").ap()
    qscI = nc.dram_tensor("qscI", [64, 2], F32, kind="ExternalInput").ap()
    kscI = nc.dram_tensor("kscI", [64, 2], F32, kind="ExternalInput").ap()
    lngI = nc.dram_tensor("lngI", [P, 2 * DC], F32, kind="ExternalInput").ap()
    lnbI = nc.dram_tensor("lnbI", [P, 2 * DC], F32, kind="ExternalInput").ap()
    w1I = nc.dram_tensor("w1I", [2 * NHC * D, 256], F32, kind="ExternalInput").ap()
    b1I = nc.dram_tensor("b1I", [2 * NHC, 256], F32, kind="ExternalInput").ap()
    w2I = nc.dram_tensor("w2I", [2 * HID2, D], F32, kind="ExternalInput").ap()
    b2I = nc.dram_tensor("b2I", [2, D], F32, kind="ExternalInput").ap()
    rtabI = nc.dram_tensor("rtabI", [64, 4 * ND], F32, kind="ExternalInput").ap()
    mask0I = nc.dram_tensor("mask0I", [7 * 1024, 512], F32, kind="ExternalInput").ap()
    mask1I = nc.dram_tensor("mask1I", [5 * 1024, 512], F32, kind="ExternalInput").ap()
    identI = nc.dram_tensor("identD", [P, P], F32, kind="ExternalInput").ap()
    x4o = nc.dram_tensor("x4o", [D, TC], F32, kind="ExternalOutput").ap()
    x3d = nc.dram_tensor("x3d", [D, ND], F32)
    xl0d = nc.dram_tensor("xl0o", [D, ND], F32, kind="ExternalOutput").ap()
    xl1d = nc.dram_tensor("xl1o", [D, 3584], F32, kind="ExternalOutput").ap()
    xl2d = nc.dram_tensor("xl2o", [D, 2560], F32, kind="ExternalOutput").ap()
    mrowd = nc.dram_tensor("mrowd", [1, 512], F32)

    with ExitStack() as es:
        ctx = Ctx(nc, es)
        C = make_consts(ctx)
        ensure_ln_shared(ctx)
        sh = ctx.ln_shared
        bcps, stps = sh["bc"], sh["stats"]
        tokrow_b = sb(ctx, "tokrow_b", [1, 512], BF16)
        yt = sb(ctx, "yt", [P, DC * 512], F32)
        wqkv_sb = sb(ctx, "wqkv_sb", [P, DC * 1536], BF16)
        wo_sb = sb(ctx, "wo_sb", [P, 4 * D], BF16)
        qs_sb = sb(ctx, "qs_sb", [64, 2], F32)
        ks_sb = sb(ctx, "ks_sb", [64, 2], F32)
        lng_sb = sb(ctx, "lng_sb", [P, 2 * DC], F32)
        lnb_sb = sb(ctx, "lnb_sb", [P, 2 * DC], F32)
        pD_sb = sb(ctx, "pD_sb", [P, DC], F32)
        b1w = [sb(ctx, f"b1w{j}", [1, 256], BF16) for j in range(2)]
        b2_sb = sb(ctx, "b2_sb", [1, D], BF16)
        rt_sb = sb(ctx, "rt_sb", [64, 4 * 512], F32)
        mk_sb = sb(ctx, "mk_sb", [P, 8 * 512], BF16)
        xw = sb(ctx, "xw", [P, DC * 513], BF16)
        resid = sb(ctx, "resid", [P, 512], F32)
        osc = resid
        qn = sb(ctx, "qn", [64, 512], F32)
        qt1 = sb(ctx, "qt1", [64, 512], F32)
        qt2 = sb(ctx, "qt2", [64, 512], F32)
        qr = sb(ctx, "qr", [64, 8 * 512], BF16)
        krp = [sb(ctx, f"kr{j}", [64, 8 * 512], BF16) for j in range(2)]
        vp = [sb(ctx, f"v{j}", [P, 4 * 512], BF16) for j in range(2)]
        sq_sc = sb(ctx, "sq_sc", [P, 512], BF16)
        rowbuf = sb(ctx, "rowbuf", [1, 512], F32)
        mtmp = sb(ctx, "mtmp", [4, P], F32)
        mcolb = sb(ctx, "mcolb", [P, 4], F32)
        mrow = sb(ctx, "mrow", [1, 512], F32)
        expT = sb(ctx, "expT", [P, 512], BF16)
        exs = sb(ctx, "exs", [P, 512], F32)
        o_fm = sb(ctx, "o_fm", [P, 4 * 512], BF16)
        xm = sb(ctx, "xm", [P, DC * 512], F32)
        xmt = sb(ctx, "xmt", [P, DC], F32)
        xs = sb(ctx, "xs", [P, DC * 512], F32)
        ln2b = sb(ctx, "ln2b", [P, DC * 512], BF16)
        w1w = [sb(ctx, f"w1w{j}", [P, DC * 256], BF16) for j in range(2)]
        w2w = [sb(ctx, f"w2w{j}", [P, D], BF16) for j in range(2)]
        hglu = sb(ctx, "hglu", [P, 256], BF16)
        gsb = sb(ctx, "gsb", [P, 256], F32)
        idm = sb(ctx, "idm", [P, P], F32)
        P1 = ps(ctx, "P1", [P, 512], F32)
        P2 = ps(ctx, "P2", [P, 512], F32)
        P3 = ps(ctx, "P3", [P, 512], F32)
        P4 = ps(ctx, "P4", [P, 512], F32)

        with nc.Block() as block:
            @block.vector
            def _(v):
                v.memset(C["ones_colf"][:, :], 1.0)
                v.memset(C["ones_colb"][:, :], 1.0)
                v.memset(C["ones_rowf"][:, :], 1.0)
                v.memset(C["ones_rowb"][:, :], 1.0)
                i = v.memset(tokrow_b[:, :], 1.0)
                ctx.inc(i, "consts")

            @block.gpsimd
            def _(g):
                for c in range(DC):
                    dma(ctx, g, wqkv_sb[:, c * 1536:(c + 1) * 1536],
                        wqkvI[c * P:(c + 1) * P, :], "ldw")
                dma(ctx, g, qs_sb[:, :], qscI, "ldw")
                dma(ctx, g, ks_sb[:, :], kscI, "ldw")
                dma(ctx, g, lng_sb[:, :], lngI, "ldw")
                dma(ctx, g, lnb_sb[:, :], lnbI, "ldw")
                dma(ctx, g, pD_sb[:, :], pD, "ldw")
                dma(ctx, g, idm[:, :], identI, "ldw")

            for it in range(ND // 512):
                tt = it * 512

                @block.sync
                def _(s):
                    if it > 0:
                        ctx.wait(s, "p0st", ctx.val("p0st"))
                    for c in range(DC):
                        dma(ctx, s, xs[:, c * 512:(c + 1) * 512],
                            ych[c * P:(c + 1) * P, tt:tt + 512], "p0a")

                @block.scalar
                def _(s2):
                    if it > 0:
                        ctx.wait(s2, "ln_apply")
                    for c in range(DC):
                        dma(ctx, s2, xm[:, c * 512:(c + 1) * 512],
                            uh[c * P:(c + 1) * P, tt:tt + 512], "p0u")

                @block.vector
                def _(v):
                    ctx.wait(v, "p0a")
                    ctx.wait(v, "p0u")
                    ctx.wait(v, "ldw")
                    for c in range(DC):
                        cs = slice(c * 512, (c + 1) * 512)
                        i = v.tensor_scalar(xm[:, cs], xm[:, cs], pD_sb[:, c:c + 1],
                                            None, ALU.mult)
                        ctx.inc(i, "p0m")
                        ctx.wait(v, "p0m")
                        i = v.tensor_tensor(xm[:, cs], xm[:, cs], xs[:, cs], ALU.add)
                    ctx.inc(i, "p0y")

                @block.sync
                def _(s):
                    ctx.wait(s, "p0y")
                    for c in range(DC):
                        dma(ctx, s, xs[:, c * 512:(c + 1) * 512],
                            x2h[c * P:(c + 1) * P, tt:tt + 512], "p0b")

                @block.vector
                def _(v):
                    ctx.wait(v, "p0b")
                    for c in range(DC):
                        cs = slice(c * 512, (c + 1) * 512)
                        i = v.tensor_tensor(xm[:, cs], xm[:, cs], xs[:, cs], ALU.add)
                    ctx.inc(i, "x3w")

                @block.sync
                def _(s):
                    ctx.wait(s, "x3w")
                    for c in range(DC):
                        dma(ctx, s, x3d[c * P:(c + 1) * P, tt:tt + 512],
                            xm[:, c * 512:(c + 1) * 512], "p0st")

                emit_ln_fm(ctx, block, C, xm, 512, f"d0_{it}", out_f32=xs,
                           wait_names=("x3w",))

                @block.sync
                def _(s):
                    ctx.wait(s, "ln_apply")
                    for c in range(DC):
                        dma(ctx, s, xl0d[c * P:(c + 1) * P, tt:tt + 512],
                            xs[:, c * 512:(c + 1) * 512], "p0st")

                @block.vector
                def _(v):
                    ctx.wait(v, "p0st")

            def emit_layer(li, xsrc, src_len, xdst, maskI, wstart=0):
                nwin = src_len // 512
                loff = li * DC

                @block.gpsimd
                def _(g):
                    dma(ctx, g, b2_sb[0:1, :], b2I[li:li + 1, :], "ldb12")
                    ctx.wait(g, "vmm", ctx.val("vmm"))
                    ctx.wait(g, "qkmm", ctx.val("qkmm"))
                    for c in range(DC):
                        dma(ctx, g, wqkv_sb[:, c * 1536:(c + 1) * 1536],
                            wqkvI[li * D + c * P: li * D + (c + 1) * P, :], "ldqkv")
                    ctx.wait(g, "xomm", ctx.val("xomm"))
                    for c in range(4):
                        dma(ctx, g, wo_sb[:, c * D:(c + 1) * D],
                            woI[li * 512 + c * P: li * 512 + (c + 1) * P, :], "ldwo")

                for w in range(wstart, nwin):
                    s0 = w * 512
                    par = w % 2
                    tag = f"l{li}w{w}"

                    @block.gpsimd
                    def _(g):
                        ctx.wait(g, "xwfree", ctx.val("xwfree"))
                        for c in range(DC):
                            if s0 == 0:
                                dma(ctx, g, xw[:, c * 513 + 1: c * 513 + 513],
                                    xsrc[c * P:(c + 1) * P, 0:512], "xwl")
                            else:
                                dma(ctx, g, xw[:, c * 513: c * 513 + 513],
                                    xsrc[c * P:(c + 1) * P, s0 - 1:s0 + 512],
                                    "xwl")

                    @block.sync
                    def _(s):
                        for j in range(4):
                            dma(ctx, s, rt_sb[:, j * 512:(j + 1) * 512],
                                rtabI[:, j * ND + (ND - src_len) + s0:
                                      j * ND + (ND - src_len) + s0 + 512], "xwl")

                    if w > wstart:
                        @block.gpsimd
                        def _(g):
                            ctx.wait(g, "exv", ctx.val("exv"))
                            src = maskI[(w - wstart - 1) * 1024:
                                        (w - wstart) * 1024, :] \
                                .rearrange("(kc p) q -> p kc q", p=P)
                            dst = mk_sb.ap().rearrange("p (kc q) -> p kc q", q=512)
                            dma(ctx, g, dst, src, "mkl")

                    for oc in range(8):
                        isq = oc < 4

                        @block.tensor
                        def _(t):
                            if li == 0 and w == wstart and oc == 0:
                                ctx.wait(t, "ldw")
                                ctx.wait(t, "consts")
                            ctx.wait(t, "ldqkv")
                            ctx.wait(t, "ldwo")
                            ctx.wait(t, "xwl")
                            ctx.wait(t, "pqsec", ctx.val("pqsec"))
                            for c in range(DC):
                                off = 0 if c < 4 else -1
                                rhs = xw[:, c * 513 + 1 + off: c * 513 + 513 + off]
                                i = t.matmul(P1[:, :],
                                             wqkv_sb[:, c * 1536 + oc * P:
                                                     c * 1536 + (oc + 1) * P],
                                             rhs, start=(c == 0), stop=(c == DC - 1))
                            ctx.inc(i, "qkmm")

                        @block.scalar
                        def _(sE):
                            ctx.wait(sE, "qkmm")
                            i = sE.activation(sq_sc[:, :], P1[:, :], AF.Square)
                            ctx.inc(i, "sqd")

                        for hh in range(2):
                            h = (oc % 4) * 2 + hh
                            base = hh * 64

                            @block.tensor
                            def _(t):
                                ctx.wait(t, "sqd")
                                ctx.wait(t, "pqsec", ctx.val("pqsec"))
                                i = t.matmul(stps[0:1, :],
                                             C["ones_colb"][base:base + 64, 0:1],
                                             sq_sc[base:base + 64, :],
                                             start=True, stop=True)
                                ctx.inc(i, "nrm")

                            @block.scalar
                            def _(sE):
                                ctx.wait(sE, "nrm")
                                i = sE.activation(rowbuf[0:1, :], stps[0:1, :], AF.Sqrt)
                                ctx.inc(i, "nrs")

                            @block.vector
                            def _(v):
                                ctx.wait(v, "nrs")
                                i = v.tensor_scalar(rowbuf[0:1, :], rowbuf[0:1, :],
                                                    1e-12, None, ALU.max)
                                vop(ctx, v, i)
                                i = v.reciprocal(rowbuf[0:1, :], rowbuf[0:1, :])
                                ctx.inc(i, "nrr")

                            @block.tensor
                            def _(t):
                                ctx.wait(t, "nrr")
                                i = t.matmul(P4[0:64, :], C["ones_rowf"][0:1, 0:64],
                                             rowbuf[0:1, :], start=True, stop=True)
                                ctx.inc(i, "nrb")

                            @block.vector
                            def _(v):
                                ctx.wait(v, "nrb")
                                i = v.tensor_copy(qn[:, :], P1[base:base + 64, :])
                                vop(ctx, v, i)
                                i = v.tensor_tensor(qn[:, :], qn[:, :], P4[0:64, :],
                                                    ALU.mult)
                                vop(ctx, v, i)
                                scv = qs_sb if isq else ks_sb
                                i = v.tensor_scalar(qn[:, :], qn[:, :],
                                                    scv[:, li:li + 1], None, ALU.mult)
                                vop(ctx, v, i)
                                csb = 0 if isq else 2 * 512
                                ssb = 512 if isq else 3 * 512
                                i = v.tensor_tensor(qt1[:, :], qn[:, :],
                                                    rt_sb[:, csb:csb + 512], ALU.mult)
                                vop(ctx, v, i)
                                i = v.tensor_copy(qt2[0:32, :], qn[32:64, :])
                                vop(ctx, v, i)
                                i = v.tensor_copy(qt2[32:64, :], qn[0:32, :])
                                vop(ctx, v, i)
                                i = v.tensor_tensor(qt2[:, :], qt2[:, :],
                                                    rt_sb[:, ssb:ssb + 512], ALU.mult)
                                vop(ctx, v, i)
                                i = v.tensor_tensor(qt1[:, :], qt1[:, :], qt2[:, :],
                                                    ALU.add)
                                vop(ctx, v, i)
                                dst = qr if isq else krp[par]
                                i = v.tensor_scalar(dst[:, h * 512:(h + 1) * 512],
                                                    qt1[:, :],
                                                    8.0 if isq else 1.0, None, ALU.mult)
                                ctx.inc(i, "pqsec")

                    for tb in range(4):
                        @block.tensor
                        def _(t):
                            ctx.wait(t, "xwl")
                            ctx.wait(t, "vev", ctx.val("vev"))
                            for c in range(DC):
                                off = 0 if c < 4 else -1
                                lhs = xw[:, c * 513 + 1 + off + tb * P:
                                         c * 513 + 1 + off + (tb + 1) * P]
                                i = t.matmul(P2[:, :], lhs,
                                             wqkv_sb[:, c * 1536 + 1024:
                                                     c * 1536 + 1536],
                                             start=(c == 0), stop=(c == DC - 1))
                            ctx.inc(i, "vmm")

                        @block.vector
                        def _(v):
                            ctx.wait(v, "vmm")
                            i = v.tensor_copy(vp[par][:, tb * 512:(tb + 1) * 512],
                                              P2[:, :])
                            ctx.inc(i, "vev")

                    if w == wstart:
                        @block.vector
                        def _(v):
                            ctx.wait(v, "pqsec", ctx.val("pqsec"))
                            ctx.wait(v, "vev", ctx.val("vev"))
                            ctx.inc(v.engine_nop(), "xwfree")
                        continue

                    for h in range(8):
                        for qs4 in range(4):
                            @block.tensor
                            def _(t):
                                ctx.wait(t, "pqsec", ctx.val("pqsec"))
                                ctx.wait(t, "mcr", ctx.val("mcr"))
                                lhs = qr[:, h * 512 + qs4 * P: h * 512 + (qs4 + 1) * P]
                                t.matmul(bcps[:, 0:512], lhs,
                                         krp[1 - par][:, h * 512:(h + 1) * 512],
                                         start=True, stop=True)
                                i = t.matmul(bcps[:, 512:1024], lhs,
                                             krp[par][:, h * 512:(h + 1) * 512],
                                             start=True, stop=True)
                                ctx.inc(i, "sqk")

                            @block.vector
                            def _(v):
                                ctx.wait(v, "sqk")
                                i = v.reduce_max(mcolb[:, qs4:qs4 + 1],
                                                 bcps[:, 0:1024], axis=AX.X)
                                ctx.inc(i, "mcr")

                        @block.tensor
                        def _(t):
                            ctx.wait(t, "mcr", ctx.val("mcr"))
                            i = t.transpose(P4[0:4, 0:P], mcolb[:, 0:4], idm[:, :])
                            ctx.inc(i, "mtr")

                        @block.vector
                        def _(v):
                            ctx.wait(v, "mtr")
                            i = v.tensor_copy(mtmp[:, :], P4[0:4, 0:P])
                            ctx.inc(i, "mtc")

                        @block.scalar
                        def _(sE):
                            ctx.wait(sE, "mtc")
                            dma(ctx, sE, mrowd.ap().rearrange("o (a b) -> (o a) b", a=4),
                                mtmp[:, :], "mrd1")
                            ctx.wait(sE, "mrd1")
                            dma(ctx, sE, mrow[0:1, :], mrowd.ap(), "mrd")

                        @block.tensor
                        def _(t):
                            ctx.wait(t, "mrd")
                            i = t.matmul(bcps[:, 0:512], C["ones_rowf"][0:1, :],
                                         mrow[0:1, :], start=True, stop=True)
                            ctx.inc(i, "mbc")

                        for kc in range(8):
                            kw = krp[1 - par] if kc < 4 else krp[par]
                            kb = (kc % 4) * P

                            @block.tensor
                            def _(t):
                                ctx.wait(t, "mbc")
                                ctx.wait(t, "exv", ctx.val("exv"))
                                i = t.matmul(P3[:, :],
                                             kw[:, h * 512 + kb: h * 512 + kb + P],
                                             qr[:, h * 512:(h + 1) * 512],
                                             start=True, stop=True)
                                ctx.inc(i, "stmm")

                            @block.vector
                            def _(v):
                                ctx.wait(v, "stmm")
                                ctx.wait(v, "mkl")
                                ctx.wait(v, "exw", ctx.val("exw"))
                                i = v.tensor_tensor(exs[:, :], P3[:, :],
                                                    mk_sb[:, kc * 512:(kc + 1) * 512],
                                                    ALU.add)
                                vop(ctx, v, i)
                                i = v.tensor_tensor(exs[:, :], exs[:, :],
                                                    bcps[:, 0:512], ALU.subtract)
                                ctx.inc(i, "exv")

                            @block.scalar
                            def _(sE):
                                ctx.wait(sE, "exv")
                                ctx.wait(sE, "oamm", ctx.val("oamm"))
                                i = sE.activation(expT[:, :], exs[:, :], AF.Exp)
                                ctx.inc(i, "exw")

                            @block.tensor
                            def _(t):
                                ctx.wait(t, "exw")
                                if kc == 0:
                                    ctx.wait(t, "oev", ctx.val("oev"))
                                vv = vp[1 - par] if kc < 4 else vp[par]
                                t.matmul(P2[0:64, :],
                                         vv[:, (kc % 4) * 512 + h * 64:
                                            (kc % 4) * 512 + h * 64 + 64],
                                         expT[:, :],
                                         start=(kc == 0), stop=(kc == 7))
                                i = t.matmul(stps[0:1, :], C["ones_colb"][:, 0:1],
                                             expT[:, :],
                                             start=(kc == 0), stop=(kc == 7))
                                ctx.inc(i, "oamm")

                        @block.vector
                        def _(v):
                            ctx.wait(v, "oamm")
                            i = v.tensor_copy(rowbuf[0:1, :], stps[0:1, :])
                            vop(ctx, v, i)
                            i = v.reciprocal(rowbuf[0:1, :], rowbuf[0:1, :])
                            ctx.inc(i, "odr")

                        @block.tensor
                        def _(t):
                            ctx.wait(t, "odr")
                            i = t.matmul(P4[0:64, :], C["ones_rowf"][0:1, 0:64],
                                         rowbuf[0:1, :], start=True, stop=True)
                            ctx.inc(i, "odb")

                        @block.vector
                        def _(v):
                            ctx.wait(v, "odb")
                            i = v.tensor_copy(osc[0:64, :], P2[0:64, :])
                            vop(ctx, v, i)
                            i = v.tensor_tensor(
                                o_fm[(h % 2) * 64:(h % 2) * 64 + 64,
                                     (h // 2) * 512:(h // 2 + 1) * 512],
                                osc[0:64, :], P4[0:64, :], ALU.mult)
                            ctx.inc(i, "oev")

                    @block.vector
                    def _(v):
                        ctx.wait(v, "oev", ctx.val("oev"))
                        if w > wstart + 1:
                            i = v.tensor_copy(
                                xmt[:, :],
                                xm.ap().rearrange("p (c t) -> p c t", c=DC)[:, :, 511])
                        else:
                            i = v.memset(xmt[:, :], 0.0)
                        ctx.inc(i, "xmtc")

                    for oc in range(DC):
                        @block.scalar
                        def _(sE):
                            ctx.wait(sE, "xmw", ctx.val("xmw"))
                            ctx.wait(sE, "oev", ctx.val("oev"))
                            dma(ctx, sE, resid[:, :],
                                xsrc[oc * P:(oc + 1) * P, s0:s0 + 512], "resl")

                        @block.tensor
                        def _(t):
                            ctx.wait(t, "oev", ctx.val("oev"))
                            ctx.wait(t, "xmtc")
                            ctx.wait(t, "xmw", ctx.val("xmw"))
                            for c in range(4):
                                i = t.matmul(P1[:, :],
                                             wo_sb[:, c * D + oc * P:
                                                   c * D + (oc + 1) * P],
                                             o_fm[:, c * 512:(c + 1) * 512],
                                             start=(c == 0), stop=(c == 3))
                            ctx.inc(i, "xomm")

                        @block.vector
                        def _(v):
                            ctx.wait(v, "xomm")
                            ctx.wait(v, "resl")
                            i = v.tensor_tensor(xm[:, oc * 512:(oc + 1) * 512],
                                                P1[:, :], resid[:, :], ALU.add)
                            ctx.inc(i, "xmw")

                    @block.vector
                    def _(v):
                        ctx.wait(v, "xmw", ctx.val("xmw"))
                        for c in range(DC):
                            if c < 4:
                                i = v.tensor_copy(xs[:, c * 512:(c + 1) * 512],
                                                  xm[:, c * 512:(c + 1) * 512])
                            else:
                                v.tensor_copy(xs[:, c * 512:c * 512 + 1],
                                              xmt[:, c:c + 1])
                                i = v.tensor_copy(xs[:, c * 512 + 1:(c + 1) * 512],
                                                  xm[:, c * 512:c * 512 + 511])
                        ctx.inc(i, "xsw")
                        ctx.inc(v.engine_nop(), "xwfree")

                    emit_ln_fm(ctx, block, C, xs, 512, f"dln_{tag}", out_bf=ln2b,
                               gamma=lng_sb[:, loff:loff + DC],
                               beta=lnb_sb[:, loff:loff + DC],
                               wait_names=("xsw",))

                    dps = [P1, P2, P3, P4]
                    for sub in range(2):
                        ts0 = sub * 256
                        for hc in range(NHC):
                            wpar = hc % 2

                            @block.gpsimd
                            def _(g):
                                ctx.wait(g, "dnmm", max(ctx.val("dnmm") - 1, 0))
                                src = w1I[(li * NHC + hc) * D:
                                          (li * NHC + hc + 1) * D, :] \
                                    .rearrange("(ic p) j -> p ic j", p=P)
                                dst = w1w[wpar].ap().rearrange(
                                    "p (ic j) -> p ic j", j=256)
                                dma(ctx, g, dst, src, f"w1l{wpar}")
                                dma(ctx, g, w2w[wpar][:, :],
                                    w2I[li * HID2 + hc * P:
                                        li * HID2 + (hc + 1) * P, :], f"w1l{wpar}")
                                dma(ctx, g, b1w[wpar][0:1, :],
                                    b1I[li * NHC + hc:li * NHC + hc + 1, :],
                                    f"w1l{wpar}")
                            ldm = ctx.val(f"w1l{wpar}")

                            @block.tensor
                            def _(t):
                                ctx.wait(t, f"w1l{wpar}", ldm)
                                ctx.wait(t, "ldb12")
                                ctx.wait(t, "ln_apply")
                                ctx.wait(t, "hgw", ctx.val("hgw"))
                                if hc == 0:
                                    ctx.wait(t, "xfw", ctx.val("xfw"))
                                for c in range(DC):
                                    t.matmul(bcps[:, 0:256],
                                             w1w[wpar][:, c * 256:c * 256 + P],
                                             ln2b[:, c * 512 + ts0:c * 512 + ts0 + 256],
                                             start=(c == 0), stop=False)
                                    i = t.matmul(bcps[:, 512:768],
                                                 w1w[wpar][:, c * 256 + P:(c + 1) * 256],
                                                 ln2b[:, c * 512 + ts0:c * 512 + ts0 + 256],
                                                 start=(c == 0), stop=False)
                                t.matmul(bcps[:, 0:256], b1w[wpar][0:1, 0:P],
                                         tokrow_b[0:1, 0:256], start=False, stop=True)
                                i = t.matmul(bcps[:, 512:768], b1w[wpar][0:1, P:256],
                                             tokrow_b[0:1, 0:256], start=False, stop=True)
                                ctx.inc(i, "upmm")

                            @block.scalar
                            def _(sE):
                                ctx.wait(sE, "upmm")
                                i = sE.activation(gsb[:, :], bcps[:, 512:768],
                                                  AF.Gelu)
                                ctx.inc(i, "gel")

                            @block.vector
                            def _(v):
                                ctx.wait(v, "gel")
                                i = v.tensor_tensor(hglu[:, 0:256], bcps[:, 0:256],
                                                    gsb[:, :], ALU.mult)
                                ctx.inc(i, "hgw")

                            @block.tensor
                            def _(t):
                                ctx.wait(t, "hgw")
                                for oc in range(DC):
                                    bank = dps[oc // 2]
                                    colr = slice((oc % 2) * 256, (oc % 2) * 256 + 256)
                                    t.matmul(bank[:, colr],
                                             w2w[wpar][:, oc * P:(oc + 1) * P],
                                             hglu[:, 0:256],
                                             start=(hc == 0), stop=False)
                                    if hc == NHC - 1:
                                        i = t.matmul(bank[:, colr],
                                                     b2_sb[0:1, oc * P:(oc + 1) * P],
                                                     tokrow_b[0:1, 0:256],
                                                     start=False, stop=True)
                                i = i if hc == NHC - 1 else t.nop()
                                ctx.inc(i, "dnmm")

                        for oc in range(DC):
                            bank = dps[oc // 2]
                            colr = slice((oc % 2) * 256, (oc % 2) * 256 + 256)

                            @block.vector
                            def _(v):
                                ctx.wait(v, "dnmm", ctx.val("dnmm"))
                                cs = slice(oc * 512 + ts0, oc * 512 + ts0 + 256)
                                i = v.tensor_tensor(xs[:, cs], bank[:, colr],
                                                    xm[:, cs], ALU.add)
                                ctx.inc(i, "xfw")

                    @block.sync
                    def _(s):
                        ctx.wait(s, "xfw", ctx.val("xfw"))
                        d0 = (w - wstart - 1) * 512
                        for c in range(DC):
                            dma(ctx, s, xdst[c * P:(c + 1) * P, d0:d0 + 512],
                                xs[:, c * 512:(c + 1) * 512], "xst")

                    @block.vector
                    def _(v):
                        ctx.wait(v, "xst")

            emit_layer(0, xl0d, ND, xl1d, mask0I)
            emit_layer(1, xl1d, 3584, xl2d, mask1I, wstart=1)

            for it in range(4):
                tt = it * 512

                @block.sync
                def _(s):
                    if it == 0:
                        ctx.wait(s, "xst", ctx.val("xst"))
                        ctx.wait(s, "xfw", ctx.val("xfw"))
                        ctx.wait(s, "xmw", ctx.val("xmw"))
                    else:
                        ctx.wait(s, "x4st", ctx.val("x4st"))
                    for c in range(DC):
                        dma(ctx, s, xm[:, c * 512:(c + 1) * 512],
                            x3d[c * P:(c + 1) * P, HALO + tt:HALO + tt + 512],
                            "f4a")
                    for c in range(DC):
                        dma(ctx, s, xs[:, c * 512:(c + 1) * 512],
                            xl2d[c * P:(c + 1) * P, 512 + tt:512 + tt + 512],
                            "f4a")

                @block.vector
                def _(v):
                    ctx.wait(v, "f4a")
                    for c in range(DC):
                        cs = slice(c * 512, (c + 1) * 512)
                        i = v.tensor_tensor(xm[:, cs], xm[:, cs], xs[:, cs], ALU.add)
                    ctx.inc(i, "x4w")

                @block.sync
                def _(s):
                    ctx.wait(s, "x4w")
                    for c in range(DC):
                        dma(ctx, s, x4o[c * P:(c + 1) * P, tt:tt + 512],
                            xm[:, c * 512:(c + 1) * 512], "x4st")

            @block.sync
            def _(s):
                ctx.wait(s, "x4st")
    return nc


def run_D(inputs, x2, u, yconv, core0_t0s=None):
    nc = build_D()
    x2f = np.concatenate([np.asarray(a, np.float32) for a in x2], axis=1)
    uf = np.concatenate([np.asarray(a, np.float32) for a in u], axis=1)
    ycf = np.asarray(yconv, np.float32)

    def halo_slice(arr, c):
        lo = c * TC - HALO
        out = np.zeros((D, ND), np.float32)
        s = max(lo, 0)
        out[:, s - lo:] = arr[:, s:c * TC + TC]
        return np.ascontiguousarray(out)

    pDh = np.ascontiguousarray(np.asarray(inputs["L1_attn_param_D"], np.float32)
                               .reshape(DC, P).T)
    wqkv = np.ascontiguousarray(np.asarray(inputs["L1_ffn_wqkv"], np.float32)
                                .reshape(2 * D, 1536))
    wo = np.ascontiguousarray(np.asarray(inputs["L1_ffn_wo"], np.float32)
                              .reshape(2 * 512, D))
    qs = np.ascontiguousarray(np.asarray(inputs["L1_ffn_q_scale"], np.float32).T)
    ks = np.ascontiguousarray(np.asarray(inputs["L1_ffn_k_scale"], np.float32).T)
    lng = np.ascontiguousarray(np.asarray(inputs["L1_ffn_ln_g"], np.float32)
                               .reshape(2, DC, P).transpose(2, 0, 1).reshape(P, 2 * DC))
    lnb = np.ascontiguousarray(np.asarray(inputs["L1_ffn_ln_b"], np.float32)
                               .reshape(2, DC, P).transpose(2, 0, 1).reshape(P, 2 * DC))
    HID = 2730
    w1 = np.asarray(inputs["L1_ffn_w1"], np.float32)
    b1 = np.asarray(inputs["L1_ffn_b1"], np.float32)
    w2 = np.asarray(inputs["L1_ffn_w2"], np.float32)
    b2 = np.asarray(inputs["L1_ffn_b2"], np.float32)
    w1a = np.zeros((2, D, HID2), np.float32)
    w1g = np.zeros((2, D, HID2), np.float32)
    w1a[:, :, :HID] = w1[:, :, :HID]
    w1g[:, :, :HID] = w1[:, :, HID:]
    b1p = np.zeros((2, NHC, 256), np.float32)
    b1f = np.zeros((2, HID2), np.float32)
    b1g = np.zeros((2, HID2), np.float32)
    b1f[:, :HID] = b1[:, :HID]
    b1g[:, :HID] = b1[:, HID:]
    for li in range(2):
        for hc in range(NHC):
            b1p[li, hc, :P] = b1f[li, hc * P:(hc + 1) * P]
            b1p[li, hc, P:] = b1g[li, hc * P:(hc + 1) * P]
    b1p = np.ascontiguousarray(b1p.reshape(2 * NHC, 256))
    w2p = np.zeros((2, HID2, D), np.float32)
    w2p[:, :HID] = w2
    w1I = np.zeros((2, NHC, DC, P, 256), np.float32)
    for li in range(2):
        for hc in range(NHC):
            w1I[li, hc, :, :, :P] = w1a[li, :, hc * P:(hc + 1) * P].reshape(DC, P, P)
            w1I[li, hc, :, :, P:] = w1g[li, :, hc * P:(hc + 1) * P].reshape(DC, P, P)
    w1I = np.ascontiguousarray(w1I.reshape(2 * NHC * D, 256))
    w2I = np.ascontiguousarray(w2p.reshape(2 * HID2, D))
    b2I = np.ascontiguousarray(b2)

    DH, WIN = 64, 512
    half_idx = np.arange(0, DH, 2, dtype=np.float64)
    inv_freq = 1.0 / 10000.0 ** (half_idx / DH)
    base = (half_idx + 0.4 * DH) / (1.4 * DH)
    identD = np.eye(P, dtype=np.float32)

    in_maps = []
    for c in range(NCORES):
        t = np.arange(c * TC - HALO, c * TC + TC, dtype=np.float64)
        fr = t[None, :] * inv_freq[:, None]
        fr2 = np.concatenate([fr, fr], axis=0)
        power = (t - T // 2) / float(WIN // 2)
        scale = base[:, None] ** power[None, :]
        scale2 = np.concatenate([scale, scale], axis=0)
        sgn = np.ones((64, 1))
        sgn[:32] = -1.0
        cs = np.cos(fr2) * scale2
        ss = np.sin(fr2) * scale2 * sgn
        ck = np.cos(fr2) / scale2
        sk = np.sin(fr2) / scale2 * sgn
        rtab = np.ascontiguousarray(
            np.concatenate([cs, ss, ck, sk], axis=1).astype(np.float32))

        def build_mask(nout, first_idx):
            il = np.arange(512)[None, :]
            jc = np.arange(1024)[:, None]
            inv = np.where(jc < 512, il > jc, il < jc - 512)
            m = np.where(inv, -1e30, 0.0).astype(np.float32)
            mf = m.copy()
            mf[:512, :] = -1e30
            out = np.stack([m] * nout, axis=0)
            if first_idx is not None:
                out[first_idx] = mf
            return np.ascontiguousarray(out.reshape(nout * 1024, 512))

        m0 = build_mask(7, 3 if c == 0 else None)
        m1 = build_mask(5, 1 if c == 0 else None)
        in_maps.append({
            "x2h": halo_slice(x2f, c), "ych": halo_slice(ycf, c),
            "uh": halo_slice(uf, c), "pD": pDh, "wqkvI": wqkv, "woI": wo,
            "qscI": qs, "kscI": ks, "lngI": lng, "lnbI": lnb, "w1I": w1I,
            "b1I": b1p, "w2I": w2I, "b2I": b2I, "rtabI": rtab,
            "mask0I": m0, "mask1I": m1, "identD": identD,
        })
    res = run_bass_kernel_spmd(nc, in_maps, core_ids=list(range(NCORES)))
    return ([res.results[c]["x4o"] for c in range(NCORES)],
            [res.results[c]["xl0o"] for c in range(NCORES)],
            [res.results[c]["xl1o"] for c in range(NCORES)],
            [res.results[c]["xl2o"] for c in range(NCORES)])


EH = 512
ENT = TC + 2 * EH
EIN = ENT + 4


def build_E():
    nc = bass.Bass("TRN2", target_bir_lowering=False)
    x4h = nc.dram_tensor("x4h", [D, EIN], F32, kind="ExternalInput").ap()
    cpe1w = nc.dram_tensor("cpe1w", [P, 3 * DC], F32, kind="ExternalInput").ap()
    cpe1b = nc.dram_tensor("cpe1b", [P, DC], F32, kind="ExternalInput").ap()
    n1g = nc.dram_tensor("n1g", [P, DC], F32, kind="ExternalInput").ap()
    n1b = nc.dram_tensor("n1b", [P, DC], F32, kind="ExternalInput").ap()
    actw = nc.dram_tensor("actw", [D, D], F32, kind="ExternalInput").ap()
    actb = nc.dram_tensor("actb", [1, D], F32, kind="ExternalInput").ap()
    inw = nc.dram_tensor("inw", [D, D], F32, kind="ExternalInput").ap()
    inb = nc.dram_tensor("inb", [1, D], F32, kind="ExternalInput").ap()
    dwcw = nc.dram_tensor("dwcw", [P, 3 * DC], F32, kind="ExternalInput").ap()
    dwcb = nc.dram_tensor("dwcb", [P, DC], F32, kind="ExternalInput").ap()
    qkwk = nc.dram_tensor("qkwk", [D, D], F32, kind="ExternalInput").ap()
    qkbk = nc.dram_tensor("qkbk", [1, D], F32, kind="ExternalInput").ap()
    ropeC = nc.dram_tensor("ropeC", [1, 2 * D], F32, kind="ExternalInput").ap()
    identb = nc.dram_tensor("identb", [P, P], BF16, kind="ExternalInput").ap()
    edge = nc.dram_tensor("edge", [P, 2], F32, kind="ExternalInput").ap()
    xc_o = nc.dram_tensor("xc_o", [D, ENT], F32, kind="ExternalOutput").ap()
    ar_o = nc.dram_tensor("ar_o", [D, ENT], BF16, kind="ExternalOutput").ap()
    xi_o = nc.dram_tensor("xi_o", [D, ENT], BF16, kind="ExternalOutput").ap()
    pkv_o = nc.dram_tensor("pkv_o", [64, 16 * 65], F32, kind="ExternalOutput").ap()
    xqd = nc.dram_tensor("xqd", [D, ENT], BF16).ap()

    NTE = ENT // 512

    with ExitStack() as es:
        ctx = Ctx(nc, es)
        C = make_consts(ctx)
        ensure_ln_shared(ctx)
        bcps = ctx.ln_shared["bc"]
        tokrow_b = sb(ctx, "tokrow_b", [1, 512], BF16)
        xibb = sb(ctx, "xibb", [P, DC * TC], BF16)
        ab_sb = sb(ctx, "ab_sb", [1, 3 * D], BF16)
        cw_sb = sb(ctx, "cw_sb", [P, 3 * DC], F32)
        cb_sb = sb(ctx, "cb_sb", [P, DC], F32)
        dw_sb = sb(ctx, "dw_sb", [P, 3 * DC], F32)
        db_sb = sb(ctx, "db_sb", [P, DC], F32)
        g_sb = sb(ctx, "g_sb", [P, DC], F32)
        b_sb = sb(ctx, "b_sb", [P, DC], F32)
        id_sb = sb(ctx, "id_sb", [P, P], BF16)
        edge_sb = sb(ctx, "edge_sb", [P, 2], F32)
        rope_r = sb(ctx, "rope_r", [1, 2 * D], F32)
        rcosb = sb(ctx, "rcosb", [P, D], F32)
        rsinb = sb(ctx, "rsinb", [P, D], F32)
        et1 = [sb(ctx, f"et1_{j}", [P, 514], F32) for j in range(2)]
        et2 = sb(ctx, "et2", [P, 1024], F32)
        et3 = sb(ctx, "et3", [P, 1024], F32)
        kvs = sb(ctx, "kvs", [64, 16 * 65], F32)
        PK = [ps(ctx, f"PKe{j}", [P, 512], F32) for j in range(2)]
        kvp = [ps(ctx, "kvpe0", [64, 512], F32)]
        tpb = ps(ctx, "tpbe", [P, P], BF16)
        es1 = es.enter_context(ExitStack())
        _e1 = Ctx.__new__(Ctx); _e1.__dict__ = dict(ctx.__dict__); _e1.es = es1
        xin = sb(_e1, "xin", [P, DC * 514], F32)
        xln = sb(_e1, "xln", [P, DC * 514], F32)
        xc_t = sb(_e1, "xc_t", [P, DC * 512], F32)
        xnb = sb(_e1, "xnb", [P, DC * 512], BF16)
        ar_t = sb(_e1, "ar_t", [P, DC * 512], BF16)
        xq_t = sb(_e1, "xq_t", [P, DC * 512], BF16)
        w1_sb = sb(_e1, "w1_sb", [P, DC * D], BF16)
        w2_sb = sb(_e1, "w2_sb", [P, DC * D], BF16)

        with nc.Block() as block:
            @block.vector
            def _(v):
                v.memset(C["ones_colf"][:, :], 1.0)
                v.memset(C["ones_colb"][:, :], 1.0)
                v.memset(C["ones_rowf"][:, :], 1.0)
                v.memset(C["ones_rowb"][:, :], 1.0)
                v.memset(C["ones_tokb"][:, :], 1.0)
                v.memset(tokrow_b[:, :], 1.0)
                i = v.memset(kvs[:, :], 0.0)
                ctx.inc(i, "consts")

            @block.gpsimd
            def _(g):
                for c in range(DC):
                    dma(ctx, g, w1_sb[:, c * D:(c + 1) * D],
                        actw[c * P:(c + 1) * P, :], "ldw")
                    dma(ctx, g, w2_sb[:, c * D:(c + 1) * D],
                        inw[c * P:(c + 1) * P, :], "ldw")
                dma(ctx, g, ab_sb[0:1, 0:D], actb, "ldw")
                dma(ctx, g, ab_sb[0:1, D:2 * D], inb, "ldw")
                dma(ctx, g, ab_sb[0:1, 2 * D:3 * D], qkbk, "ldw")

            @block.sync
            def _(s):
                dma(ctx, s, cw_sb[:, :], cpe1w, "ldc")
                dma(ctx, s, cb_sb[:, :], cpe1b, "ldc")
                dma(ctx, s, dw_sb[:, :], dwcw, "ldc")
                dma(ctx, s, db_sb[:, :], dwcb, "ldc")
                dma(ctx, s, g_sb[:, :], n1g, "ldc")
                dma(ctx, s, b_sb[:, :], n1b, "ldc")
                dma(ctx, s, rope_r[0:1, :], ropeC, "ldc")
                dma(ctx, s, id_sb[:, :], identb, "ldc")
                dma(ctx, s, edge_sb[:, :], edge, "ldc")

            @block.tensor
            def _(t):
                ctx.wait(t, "ldc")
                ctx.wait(t, "consts")
                for j in range(4):
                    ctx.wait(t, "rbc", j)
                    i = t.matmul(bcps[:, 0:512], C["ones_rowf"][0:1, :],
                                 rope_r[0:1, j * 512:(j + 1) * 512],
                                 start=True, stop=True)
                    ctx.inc(i, "rbm")

            @block.vector
            def _(v):
                for j in range(4):
                    ctx.wait(v, "rbm", j + 1)
                    dst = rcosb if j < 2 else rsinb
                    i = v.tensor_copy(dst[:, (j % 2) * 512:(j % 2) * 512 + 512],
                                      bcps[:, 0:512])
                    ctx.inc(i, "rbc")

            @block.tensor
            def _(t):
                ctx.wait(t, "rbc", 4)

            def dwconv(v, dst, src, w_sb_, b_sb_, c, width, in_off, tmp):
                v.tensor_scalar(dst[:, 0:width], src[:, in_off - 1:in_off - 1 + width],
                                w_sb_[:, c:c + 1], None, ALU.mult)
                v.tensor_scalar(tmp[:, 0:width], src[:, in_off:in_off + width],
                                w_sb_[:, DC + c:DC + c + 1], None, ALU.mult)
                v.tensor_tensor(dst[:, 0:width], dst[:, 0:width], tmp[:, 0:width],
                                ALU.add)
                v.tensor_scalar(tmp[:, 0:width], src[:, in_off + 1:in_off + 1 + width],
                                w_sb_[:, 2 * DC + c:2 * DC + c + 1], None, ALU.mult)
                v.tensor_tensor(dst[:, 0:width], dst[:, 0:width], tmp[:, 0:width],
                                ALU.add)
                return v.tensor_scalar(dst[:, 0:width], dst[:, 0:width],
                                       b_sb_[:, c:c + 1], None, ALU.add)

            for it in range(NTE):
                tt = it * 512

                @block.sync
                def _(s):
                    ctx.wait(s, "xcw", ctx.val("xcw"))
                    ctx.wait(s, "ln_apply", ctx.val("ln_apply"))
                    for c in range(DC):
                        dma(ctx, s, xin[:, c * 514:c * 514 + 514],
                            x4h[c * P:(c + 1) * P, 1 + tt:1 + tt + 514], "exl")

                emit_ln_fm(ctx, block, C, xin, 514, f"elnO_{it}", t1=512,
                           out_f32=xln, wait_names=("exl",))
                emit_ln_fm(ctx, block, C, xin, 514, f"elnP_{it}", t0=2, t1=514,
                           out_f32=xln)

                @block.vector
                def _(v):
                    ctx.wait(v, "ldc")
                    ctx.wait(v, "xcst", ctx.val("xcst"))
                    ctx.wait(v, "ln_apply", ctx.val("ln_apply"))
                    for c in range(DC):
                        dwconv(v, et2[:, 0:512], xln[:, c * 514:c * 514 + 514],
                               cw_sb, cb_sb, c, 512, 1, et3[:, 0:512])
                        i = v.tensor_tensor(xc_t[:, c * 512:(c + 1) * 512],
                                            xln[:, c * 514 + 1:c * 514 + 513],
                                            et2[:, 0:512], ALU.add)
                        ctx.inc(i, "xcw")

                @block.sync
                def _(s):
                    ctx.wait(s, "xcw", ctx.val("xcw"))
                    for c in range(DC):
                        dma(ctx, s, xc_o[c * P:(c + 1) * P, tt:tt + 512],
                            xc_t[:, c * 512:(c + 1) * 512], "xcst")

                @block.scalar
                def _(s):
                    ctx.wait(s, "armm", 8 * it)

                emit_ln_fm(ctx, block, C, xc_t, 512, f"eln_{it}", out_bf=xnb,
                           gamma=g_sb, beta=b_sb, wait_names=("xcw",))

                for oc in range(DC):
                    @block.tensor
                    def _(t):
                        ctx.wait(t, "ln_apply")
                        if it == 0 and oc == 0:
                            ctx.wait(t, "ldw")
                        else:
                            ctx.wait(t, "arsl")
                            ctx.wait(t, "xqc")
                        for c in range(DC):
                            t.matmul(PK[0][:, :],
                                     w1_sb[:, c * D + oc * P: c * D + (oc + 1) * P],
                                     xnb[:, c * 512:(c + 1) * 512],
                                     start=(c == 0), stop=False)
                        t.matmul(PK[0][:, :], ab_sb[0:1, oc * P:(oc + 1) * P],
                                 tokrow_b[0:1, :], start=False, stop=True)
                        for c in range(DC):
                            t.matmul(PK[1][:, :],
                                     w2_sb[:, c * D + oc * P: c * D + (oc + 1) * P],
                                     xnb[:, c * 512:(c + 1) * 512],
                                     start=(c == 0), stop=False)
                        i = t.matmul(PK[1][:, :], ab_sb[0:1, D + oc * P:D + (oc + 1) * P],
                                     tokrow_b[0:1, :], start=False, stop=True)
                        ctx.inc(i, "armm")

                    @block.scalar
                    def _(sE):
                        ctx.wait(sE, "armm")
                        if oc == 0:
                            ctx.wait(sE, "arst", ctx.val("arst"))
                        i = sE.activation(ar_t[:, oc * 512:(oc + 1) * 512], PK[0][:, :],
                                          AF.Silu)
                        ctx.inc(i, "arsl")

                    @block.vector
                    def _(v):
                        ctx.wait(v, "armm")
                        if oc == 0:
                            ctx.wait(v, "xqst", ctx.val("xqst"))
                        i = v.tensor_copy(xq_t[:, oc * 512:(oc + 1) * 512], PK[1][:, :])
                        ctx.inc(i, "xqc")

                @block.sync
                def _(s):
                    ctx.wait(s, "arsl", ctx.val("arsl"))
                    ctx.wait(s, "xqc", ctx.val("xqc"))
                    for c in range(DC):
                        dma(ctx, s, ar_o[c * P:(c + 1) * P, tt:tt + 512],
                            ar_t[:, c * 512:(c + 1) * 512], "arst")
                        dma(ctx, s, xqd[c * P:(c + 1) * P, tt:tt + 512],
                            xq_t[:, c * 512:(c + 1) * 512], "xqst")

            es1.close()
            xqh = sb(ctx, "xqh", [P, DC * 514], BF16)
            xib = sb(ctx, "xib", [P, DC * 512], BF16)
            w3_sb = sb(ctx, "w3_sb", [P, DC * D], BF16)

            @block.gpsimd
            def _(g):
                ctx.wait(g, "armm", ctx.val("armm"))
                ctx.wait(g, "arsl", ctx.val("arsl"))
                ctx.wait(g, "xqc", ctx.val("xqc"))
                ctx.wait(g, "ln_apply", ctx.val("ln_apply"))
                ctx.wait(g, "arst", ctx.val("arst"))
                ctx.wait(g, "xqst", ctx.val("xqst"))
                ctx.wait(g, "xcst", ctx.val("xcst"))
                for c in range(DC):
                    dma(ctx, g, w3_sb[:, c * D:(c + 1) * D],
                        qkwk[c * P:(c + 1) * P, :], "ldw3")
            ktm = sb(ctx, "ktm", [P, 1024], F32)
            ksw = sb(ctx, "ksw", [P, 1024], F32)
            khb = sb(ctx, "khb", [P, 1024], BF16)
            krm = sb(ctx, "krm", [P, 1024], BF16)
            vtm = sb(ctx, "vtm", [P, 1024], BF16)
            for it in range(NTE):
                tt = it * 512
                lo = max(0, tt - 1)
                hi = min(ENT, tt + 513)
                off = 1 - (tt - lo)

                @block.sync
                def _(s):
                    ctx.wait(s, "xqst", ctx.val("xqst"))
                    ctx.wait(s, "xiconv", ctx.val("xiconv"))
                    if it == 0:
                        ctx.wait(s, "armm", ctx.val("armm"))
                        ctx.wait(s, "arsl", ctx.val("arsl"))
                        ctx.wait(s, "xqc", ctx.val("xqc"))
                        ctx.wait(s, "ln_apply", ctx.val("ln_apply"))
                        ctx.wait(s, "arst", ctx.val("arst"))
                        ctx.wait(s, "xcst", ctx.val("xcst"))
                    for c in range(DC):
                        dma(ctx, s, xqh[:, c * 514 + off:c * 514 + off + (hi - lo)],
                            xqd[c * P:(c + 1) * P, lo:hi], "xql")

                @block.vector
                def _(v):
                    ctx.wait(v, "xql")
                    if it == 0:
                        for c in range(DC):
                            i = v.memset(xqh[:, c * 514:c * 514 + 1], 0.0)
                    if it == NTE - 1:
                        for c in range(DC):
                            i = v.memset(xqh[:, c * 514 + 513:c * 514 + 514], 0.0)
                    if it == 0 or it == NTE - 1:
                        ctx.inc(i, "xqz")
                    mcol = {0: 512, 1: 0, NTE - 2: 513, NTE - 1: 1}
                    if it in (0, 1, NTE - 2, NTE - 1):
                        ecol = 0 if it <= 1 else 1
                        for c in range(DC):
                            i = v.tensor_scalar(
                                xqh[:, c * 514 + mcol[it]:c * 514 + mcol[it] + 1],
                                xqh[:, c * 514 + mcol[it]:c * 514 + mcol[it] + 1],
                                edge_sb[:, ecol:ecol + 1], None, ALU.mult)
                        ctx.inc(i, "xqm")
                    for c in range(DC):
                        ctx.wait(v, "xisil", max(0, it * 8 + c - 1))
                        v.tensor_copy(et3[:, 0:514], xqh[:, c * 514:c * 514 + 514])
                        i = dwconv(v, et1[c % 2][:, 0:512], et3[:, 0:514],
                                   dw_sb, db_sb, c, 512, 1, et2[:, 0:512])
                        ctx.inc(i, "xiconv")

                for c in range(DC):
                    @block.scalar
                    def _(sE):
                        ctx.wait(sE, "xiconv", it * 8 + c + 1)
                        if c == 0:
                            ctx.wait(sE, "xist", ctx.val("xist"))
                            ctx.wait(sE, "xio", ctx.val("xio"))
                        i = sE.activation(xib[:, c * 512:(c + 1) * 512],
                                          et1[c % 2][:, 0:512], AF.Silu)
                        ctx.inc(i, "xisil")

                @block.sync
                def _(s):
                    ctx.wait(s, "xisil", ctx.val("xisil"))
                    for c in range(DC):
                        dma(ctx, s, xi_o[c * P:(c + 1) * P, tt:tt + 512],
                            xib[:, c * 512:(c + 1) * 512], "xist")

                if 1 <= it <= 4:
                    @block.vector
                    def _(v):
                        ctx.wait(v, "xisil", ctx.val("xisil"))
                        ot = tt - 512
                        for c in range(DC):
                            i = v.tensor_copy(xibb[:, c * TC + ot:c * TC + ot + 512],
                                              xib[:, c * 512:(c + 1) * 512])
                        ctx.inc(i, "xio")

            for tb in range(16):
                @block.tensor
                def _(t):
                    if tb == 0:
                        ctx.wait(t, "xio", ctx.val("xio"))
                        ctx.wait(t, "ldw3")
                        ctx.wait(t, "arsl", ctx.val("arsl"))
                        ctx.wait(t, "xqc", ctx.val("xqc"))
                    else:
                        ctx.wait(t, "kelu1")
                    for half in range(2):
                        for c in range(DC):
                            t.matmul(PK[half][:, :],
                                     xibb[:, c * TC + tb * P: c * TC + (tb + 1) * P],
                                     w3_sb[:, c * D + half * 512: c * D + half * 512 + 512],
                                     start=(c == 0), stop=False)
                        i = t.matmul(PK[half][:, :], C["ones_rowb"][0:1, 0:P],
                                     ab_sb[0:1, 2 * D + half * 512:2 * D + half * 512 + 512],
                                     start=False, stop=True)
                    ctx.inc(i, "kmm")

                @block.vector
                def _(v):
                    ctx.wait(v, "kmm")
                    if tb == 0:
                        ctx.wait(v, "xisil", ctx.val("xisil"))
                        ctx.wait(v, "xist", ctx.val("xist"))
                    for half in range(2):
                        i = v.tensor_scalar(et2[:, half * 512:half * 512 + 512],
                                            PK[half][:, :], 0.0, None, ALU.min)
                    ctx.inc(i, "kelu0")

                @block.scalar
                def _(sE):
                    ctx.wait(sE, "kelu0")
                    i = sE.activation(et3[:, 0:1024], et2[:, 0:1024], AF.Exp)
                    ctx.inc(i, "kexp")

                @block.vector
                def _(v):
                    ctx.wait(v, "kexp")
                    for half in range(2):
                        i = v.tensor_scalar(et2[:, half * 512:half * 512 + 512],
                                            PK[half][:, :], 0.0, None, ALU.max)
                    ctx.inc(i, "kelu1")
                    ctx.wait(v, "kelu1")
                    i = v.tensor_tensor(ktm[:, :], et2[:, 0:1024], et3[:, 0:1024],
                                        ALU.add)
                    ctx.inc(i, "ktmw")
                    ctx.wait(v, "ktmw")
                    i = v.tensor_copy(khb[:, :], ktm[:, :])
                    k3 = ktm.ap().rearrange("p (j two) -> p j two", two=2)
                    s3 = ksw.ap().rearrange("p (j two) -> p j two", two=2)
                    v.tensor_copy(s3[:, :, 0:1], k3[:, :, 1:2])
                    i = v.tensor_copy(s3[:, :, 1:2], k3[:, :, 0:1])
                    ctx.inc(i, "kswd")
                    ctx.wait(v, "kswd")
                    v.tensor_tensor(et2[:, 0:1024], ktm[:, :], rcosb[:, :], ALU.mult)
                    i = v.tensor_tensor(et3[:, 0:1024], ksw[:, :], rsinb[:, :], ALU.mult)
                    ctx.inc(i, "krot0")
                    ctx.wait(v, "krot0")
                    i = v.tensor_tensor(krm[:, :], et2[:, 0:1024], et3[:, 0:1024],
                                        ALU.add)
                    ctx.inc(i, "krw")

                for c in range(DC):
                    @block.tensor
                    def _(t):
                        if not (tb == 0 and c == 0):
                            ctx.wait(t, "vcp")
                        if tb == 0 and c == 0:
                            ctx.wait(t, "rbc", 4)
                        i = t.transpose(tpb[:, :],
                                        xibb[:, c * TC + tb * P: c * TC + (tb + 1) * P],
                                        id_sb[:, :])
                        ctx.inc(i, "vtr")

                    @block.vector
                    def _(v):
                        ctx.wait(v, "vtr")
                        i = v.tensor_copy(vtm[:, c * P:(c + 1) * P], tpb[:, :])
                        ctx.inc(i, "vcp")

                for j in range(4):
                    @block.tensor
                    def _(t):
                        ctx.wait(t, "krw")
                        ctx.wait(t, "vcp")
                        if tb > 0 or j > 0:
                            ctx.wait(t, "kvadd", ctx.val("kvadd"))
                        for hh in range(4):
                            h = j * 4 + hh
                            t.matmul(kvp[0][:, hh * 65:hh * 65 + 64],
                                     krm[:, h * 64:(h + 1) * 64],
                                     vtm[:, h * 64:(h + 1) * 64],
                                     start=(hh == 0), stop=False)
                            i = t.matmul(kvp[0][:, hh * 65 + 64:hh * 65 + 65],
                                         khb[:, h * 64:(h + 1) * 64],
                                         C["ones_tokb"][:, 0:1],
                                         start=False, stop=(hh == 3))
                        ctx.inc(i, "kvmm")

                    @block.vector
                    def _(v):
                        ctx.wait(v, "kvmm")
                        i = v.tensor_tensor(kvs[:, j * 260:(j + 1) * 260],
                                            kvs[:, j * 260:(j + 1) * 260],
                                            kvp[0][:, 0:260], ALU.add)
                        ctx.inc(i, "kvadd")

            @block.sync
            def _(s):
                ctx.wait(s, "kvadd", ctx.val("kvadd"))
                dma(ctx, s, pkv_o, kvs[:, :], "stp")
                ctx.wait(s, "stp")
                ctx.wait(s, "xcst", ctx.val("xcst"))
                ctx.wait(s, "arst", ctx.val("arst"))
                ctx.wait(s, "xist", ctx.val("xist"))
    return nc


def _dwc_host(w, b):
    w3 = np.asarray(w, np.float32).reshape(DC, P, 3)
    wt = np.empty((P, 3 * DC), np.float32)
    for tap in range(3):
        wt[:, tap * DC:(tap + 1) * DC] = w3[:, :, tap].T
    bt = np.asarray(b, np.float32).reshape(DC, P).T.copy()
    return wt, bt


def _fm_vec(v):
    return np.asarray(v, np.float32).reshape(DC, P).T.copy()


def run_E(inputs, x4_full):
    nc = build_E()
    cw, cb = _dwc_host(inputs["L2_attn_cpe1_w"], inputs["L2_attn_cpe1_b"])
    dw, db = _dwc_host(inputs["L2_attn_dwc_w"], inputs["L2_attn_dwc_b"])
    n1g = _fm_vec(inputs["L2_attn_norm1_g"])
    n1b = _fm_vec(inputs["L2_attn_norm1_b"])
    qk_w = np.asarray(inputs["L2_attn_qk_w"], np.float32)
    qk_b = np.asarray(inputs["L2_attn_qk_b"], np.float32)
    kmax = D // 2
    theta = 1.0 / 10000.0 ** (np.arange(kmax, dtype=np.float64) / kmax)
    rcos = np.repeat(np.cos(theta), 2)
    rsin = np.empty(D, np.float64)
    rsin[0::2] = -np.sin(theta)
    rsin[1::2] = np.sin(theta)
    ropeC = np.concatenate([rcos, rsin]).astype(np.float32).reshape(1, 2 * D)
    identb = np.eye(P, dtype=np.float32)
    import ml_dtypes
    identb = identb.astype(ml_dtypes.bfloat16)
    x4p = np.pad(x4_full, ((0, 0), (EH + 2, EH + 2)))
    base = dict(cpe1w=cw, cpe1b=cb, n1g=n1g, n1b=n1b,
                actw=np.asarray(inputs["L2_attn_actp_w"], np.float32),
                actb=np.asarray(inputs["L2_attn_actp_b"], np.float32).reshape(1, D),
                inw=np.asarray(inputs["L2_attn_in_w"], np.float32),
                inb=np.asarray(inputs["L2_attn_in_b"], np.float32).reshape(1, D),
                dwcw=dw, dwcb=db,
                qkwk=np.ascontiguousarray(qk_w[:, D:]),
                qkbk=np.ascontiguousarray(qk_b[D:]).reshape(1, D),
                ropeC=ropeC, identb=identb)
    in_maps = []
    for c in range(NCORES):
        t0 = c * TC
        m = dict(base)
        m["x4h"] = np.ascontiguousarray(x4p[:, t0:t0 + EIN])
        e = np.ones((P, 2), np.float32)
        if c == 0:
            e[:, 0] = 0.0
        if c == NCORES - 1:
            e[:, 1] = 0.0
        m["edge"] = e
        in_maps.append(m)
    res = run_bass_kernel_spmd(nc, in_maps, core_ids=list(range(NCORES)))
    pkv = np.stack([res.results[c]["pkv_o"] for c in range(NCORES)], axis=0)
    xc = [res.results[c]["xc_o"] for c in range(NCORES)]
    ar = [res.results[c]["ar_o"] for c in range(NCORES)]
    xi = [res.results[c]["xi_o"] for c in range(NCORES)]
    return pkv, xc, ar, xi



FR = TC + 512
FOF = EH - 256


def build_F1():
    nc = bass.Bass("TRN2", target_bir_lowering=False)
    xiI = nc.dram_tensor("xiI", [D, ENT], BF16, kind="ExternalInput").ap()
    arI = nc.dram_tensor("arI", [D, ENT], BF16, kind="ExternalInput").ap()
    xcI = nc.dram_tensor("xcI", [D, ENT], F32, kind="ExternalInput").ap()
    x4I = nc.dram_tensor("x4I", [D, TC], F32, kind="ExternalInput").ap()
    pkvs = nc.dram_tensor("pkvs", [64, 8 * 1040], F32, kind="ExternalInput").ap()
    qkwq = nc.dram_tensor("qkwq", [D, D], F32, kind="ExternalInput").ap()
    qkbq = nc.dram_tensor("qkbq", [1, D], F32, kind="ExternalInput").ap()
    ropeC = nc.dram_tensor("ropeC", [1, 2 * D], F32, kind="ExternalInput").ap()
    lepew = nc.dram_tensor("lepew", [P, 3 * DC], F32, kind="ExternalInput").ap()
    lepeb = nc.dram_tensor("lepeb", [P, DC], F32, kind="ExternalInput").ap()
    oww = nc.dram_tensor("oww", [D, D], F32, kind="ExternalInput").ap()
    owb = nc.dram_tensor("owb", [1, D], F32, kind="ExternalInput").ap()
    cpe2w = nc.dram_tensor("cpe2w", [P, 3 * DC], F32, kind="ExternalInput").ap()
    cpe2b = nc.dram_tensor("cpe2b", [P, DC], F32, kind="ExternalInput").ap()
    n2g = nc.dram_tensor("n2g", [P, DC], F32, kind="ExternalInput").ap()
    n2b = nc.dram_tensor("n2b", [P, DC], F32, kind="ExternalInput").ap()
    fc1 = nc.dram_tensor("fc1", [D, 4096], F32, kind="ExternalInput").ap()
    fc1b = nc.dram_tensor("fc1b", [1, 4096], F32, kind="ExternalInput").ap()
    fc2 = nc.dram_tensor("fc2", [4096, D], F32, kind="ExternalInput").ap()
    fc2b = nc.dram_tensor("fc2b", [1, D], F32, kind="ExternalInput").ap()
    identb = nc.dram_tensor("identb", [P, P], BF16, kind="ExternalInput").ap()
    identf = nc.dram_tensor("identf", [P, P], F32, kind="ExternalInput").ap()
    edge = nc.dram_tensor("edge", [P, 2], F32, kind="ExternalInput").ap()
    x5o = nc.dram_tensor("x5o", [D, TC], F32, kind="ExternalOutput").ap()
    x7d = nc.dram_tensor("x7d", [D, FR], F32, kind="ExternalOutput").ap()
    kmd = nc.dram_tensor("kmd", [1, D], F32, kind="ExternalOutput").ap()
    xc2d = nc.dram_tensor("xc2d", [D, TC], F32, kind="ExternalOutput").ap()
    mlpd = nc.dram_tensor("mlpd", [D, TC], F32, kind="ExternalOutput").ap()

    NT5 = FR // 512

    with ExitStack() as es:
        ctx = Ctx(nc, es)
        C = make_consts(ctx)
        tokrow_b = sb(ctx, "tokrow_b", [1, 512], BF16)
        id_sb = sb(ctx, "id_sb", [P, P], BF16)
        idf_sb = sb(ctx, "idf_sb", [P, P], F32)
        edge_sb = sb(ctx, "edge_sb", [P, 2], F32)
        rope_r = sb(ctx, "rope_r", [1, 2 * D], F32)
        rcosb = sb(ctx, "rcosb", [P, D], F32)
        rsinb = sb(ctx, "rsinb", [P, D], F32)
        kvbd = sb(ctx, "kvbd", [P, DC * P], BF16)
        kmrow = sb(ctx, "kmrow", [1, D], F32)
        kmb = sb(ctx, "kmb", [P, D], F32)
        et2 = sb(ctx, "et2", [P, 1024], F32)
        et3 = sb(ctx, "et3", [P, 1024], F32)
        et1x = sb(ctx, "et1x", [P, 514], F32)
        et3x = sb(ctx, "et3x", [P, 514], F32)
        lw_sb = sb(ctx, "lw_sb", [P, 3 * DC], F32)
        lb_sb = sb(ctx, "lb_sb", [P, DC], F32)
        cw_sb = sb(ctx, "cw_sb", [P, 3 * DC], F32)
        cb_sb = sb(ctx, "cb_sb", [P, DC], F32)
        g_sb = sb(ctx, "g_sb", [P, DC], F32)
        b_sb = sb(ctx, "b_sb", [P, DC], F32)
        PK = [ps(ctx, f"PKf{j}", [P, 512], F32) for j in range(2)]
        APS = ps(ctx, "APS", [P, 512], F32)

        with nc.Block() as block:
            @block.vector
            def _(v):
                v.memset(C["ones_colf"][:, :], 1.0)
                v.memset(C["ones_colb"][:, :], 1.0)
                v.memset(C["ones_rowf"][:, :], 1.0)
                v.memset(C["ones_rowb"][:, :], 1.0)
                v.memset(C["ones_tokb"][:, :], 1.0)
                v.memset(tokrow_b[:, :], 1.0)
                i = v.memset(kvbd[:, :], 0.0)
                ctx.inc(i, "consts")

            @block.sync
            def _(s):
                dma(ctx, s, id_sb[:, :], identb, "ldc")
                dma(ctx, s, idf_sb[:, :], identf, "ldc")
                dma(ctx, s, edge_sb[:, :], edge, "ldc")
                dma(ctx, s, rope_r[0:1, :], ropeC, "ldc")
                dma(ctx, s, lw_sb[:, :], lepew, "ldc")
                dma(ctx, s, lb_sb[:, :], lepeb, "ldc")
                dma(ctx, s, cw_sb[:, :], cpe2w, "ldc")
                dma(ctx, s, cb_sb[:, :], cpe2b, "ldc")
                dma(ctx, s, g_sb[:, :], n2g, "ldc")
                dma(ctx, s, b_sb[:, :], n2b, "ldc")

            @block.tensor
            def _(t):
                ctx.wait(t, "ldc")
                ctx.wait(t, "consts")
                for j in range(4):
                    ctx.wait(t, "rbc", j)
                    i = t.matmul(APS[:, :], C["ones_rowf"][0:1, :],
                                 rope_r[0:1, j * 512:(j + 1) * 512],
                                 start=True, stop=True)
                    ctx.inc(i, "rbm")

            @block.vector
            def _(v):
                for j in range(4):
                    ctx.wait(v, "rbm", j + 1)
                    dst = rcosb if j < 2 else rsinb
                    i = v.tensor_copy(dst[:, (j % 2) * 512:(j % 2) * 512 + 512],
                                      APS[:, :])
                    ctx.inc(i, "rbc")

            es0 = es.enter_context(ExitStack())
            _c0 = Ctx.__new__(Ctx); _c0.__dict__ = dict(ctx.__dict__); _c0.es = es0
            TPS = ps(_c0, "TPSf", [P, P], F32)
            pkv_sb = sb(_c0, "pkv_sb", [64, 8 * 1040], F32)
            kvf = sb(_c0, "kvf", [64, 1040], F32)
            kst_in = sb(_c0, "kst_in", [64, 16], F32)
            kst = sb(_c0, "kst", [16, 64], F32)

            @block.sync
            def _(s):
                dma(ctx, s, pkv_sb[:, :], pkvs, "ld_pkv")

            @block.vector
            def _(v):
                ctx.wait(v, "ld_pkv")
                for a in range(1, 8):
                    i = v.tensor_tensor(pkv_sb[:, 0:1040], pkv_sb[:, 0:1040],
                                        pkv_sb[:, a * 1040:(a + 1) * 1040], ALU.add)
                    ctx.inc(i, "kvadd")
                    ctx.wait(v, "kvadd")
                i = v.tensor_scalar(kvf[:, :], pkv_sb[:, 0:1040], 1.0 / T, None,
                                    ALU.mult)
                ctx.inc(i, "kvsc")
                ctx.wait(v, "kvsc")
                for c in range(DC):
                    v.tensor_copy(kvbd[0:64, c * P:c * P + 64],
                                  kvf[:, (2 * c) * 65:(2 * c) * 65 + 64])
                    i = v.tensor_copy(kvbd[64:128, c * P + 64:c * P + 128],
                                      kvf[:, (2 * c + 1) * 65:(2 * c + 1) * 65 + 64])
                ctx.inc(i, "kvbw")

            @block.sync
            def _(s):
                ctx.wait(s, "kvsc")
                for h in range(16):
                    dma(ctx, s, kmd[0:1, h * 64:(h + 1) * 64],
                        kvf[:, h * 65 + 64:h * 65 + 65], "kmst")
                ctx.wait(s, "kmst")
                dma(ctx, s, kmrow[0:1, :], kmd, "kmld")

            @block.tensor
            def _(t):
                ctx.wait(t, "kmld")
                for j in range(2):
                    ctx.wait(t, "kmbc", j)
                    i = t.matmul(APS[:, :], C["ones_rowf"][0:1, :],
                                 kmrow[0:1, j * 512:(j + 1) * 512],
                                 start=True, stop=True)
                    ctx.inc(i, "kmbm")

            @block.vector
            def _(v):
                for j in range(2):
                    ctx.wait(v, "kmbm", j + 1)
                    i = v.tensor_copy(kmb[:, j * 512:(j + 1) * 512], APS[:, :])
                    ctx.inc(i, "kmbc")

            es0.close()

            es1 = es.enter_context(ExitStack())
            _c1 = Ctx.__new__(Ctx); _c1.__dict__ = dict(ctx.__dict__); _c1.es = es1
            OPS = [ps(_c1, f"OPSf{j}", [P, 512], F32) for j in range(2)]
            TPB = ps(_c1, "TPBf", [P, P], BF16)
            xif = sb(_c1, "xif", [P, DC * 2564], BF16)
            qw_sb = sb(_c1, "qw_sb", [P, DC * D], BF16)
            ow_sb = sb(_c1, "ow_sb", [P, DC * D], BF16)
            wrow = sb(_c1, "wrow", [1, 2 * D], BF16)
            qtm = sb(_c1, "qtm", [P, 1024], F32)
            qsw = sb(_c1, "qsw", [P, 1024], F32)
            zr = sb(_c1, "zr", [P, 16], F32)
            qrzb = sb(_c1, "qrzb", [P, 1024], BF16)
            qfm = sb(_c1, "qfm", [P, DC * 512], BF16)
            art = sb(_c1, "art", [P, DC * 512], BF16)
            amb = sb(_c1, "amb", [P, DC * 512], BF16)
            xct = sb(_c1, "xct", [P, DC * 512], F32)
            x7t = sb(_c1, "x7t", [P, DC * 512], F32)

            @block.gpsimd
            def _(g):
                ctx.wait(g, "kvbw", ctx.val("kvbw"))
                ctx.wait(g, "kmst", ctx.val("kmst"))
                ctx.wait(g, "kmld", ctx.val("kmld"))
                for c in range(DC):
                    dma(ctx, g, qw_sb[:, c * D:(c + 1) * D],
                        qkwq[c * P:(c + 1) * P, :], "ldw")
                    dma(ctx, g, ow_sb[:, c * D:(c + 1) * D],
                        oww[c * P:(c + 1) * P, :], "ldw")
                dma(ctx, g, wrow[0:1, 0:D], qkbq, "ldw")
                dma(ctx, g, wrow[0:1, D:2 * D], owb, "ldw")
                for c in range(DC):
                    dma(ctx, g, xif[:, c * 2564:(c + 1) * 2564],
                        xiI[c * P:(c + 1) * P, FOF - 2:FOF - 2 + 2564], "ldxi")

            @block.vector
            def _(v):
                ctx.wait(v, "ldxi")
                for c in range(DC):
                    v.tensor_scalar(xif[:, c * 2564 + 257:c * 2564 + 258],
                                    xif[:, c * 2564 + 257:c * 2564 + 258],
                                    edge_sb[:, 0:1], None, ALU.mult)
                    i = v.tensor_scalar(xif[:, c * 2564 + 2306:c * 2564 + 2307],
                                        xif[:, c * 2564 + 2306:c * 2564 + 2307],
                                        edge_sb[:, 1:2], None, ALU.mult)
                ctx.inc(i, "xim")

            for it in range(NT5):
                tt = it * 512

                @block.sync
                def _(s):
                    ctx.wait(s, "amw", ctx.val("amw"))
                    ctx.wait(s, "x7w", ctx.val("x7w"))
                    for c in range(DC):
                        dma(ctx, s, art[:, c * 512:(c + 1) * 512],
                            arI[c * P:(c + 1) * P, FOF + tt:FOF + tt + 512], "arl")
                        dma(ctx, s, xct[:, c * 512:(c + 1) * 512],
                            xcI[c * P:(c + 1) * P, FOF + tt:FOF + tt + 512], "xcl")

                for tb in range(4):
                    @block.tensor
                    def _(t):
                        if it == 0 and tb == 0:
                            ctx.wait(t, "ldw")
                            ctx.wait(t, "xim")
                        else:
                            ctx.wait(t, "qelu1")
                        for half in range(2):
                            for c in range(DC):
                                t.matmul(PK[half][:, :],
                                         xif[:, c * 2564 + tt + tb * P + 2:
                                             c * 2564 + tt + tb * P + 2 + P],
                                         qw_sb[:, c * D + half * 512:
                                               c * D + half * 512 + 512],
                                         start=(c == 0), stop=False)
                            i = t.matmul(PK[half][:, :], C["ones_rowb"][0:1, 0:P],
                                         wrow[0:1, half * 512:half * 512 + 512],
                                         start=False, stop=True)
                        ctx.inc(i, "qmm")

                    @block.vector
                    def _(v):
                        ctx.wait(v, "qmm")
                        for half in range(2):
                            i = v.tensor_scalar(et2[:, half * 512:half * 512 + 512],
                                                PK[half][:, :], 0.0, None, ALU.min)
                        ctx.inc(i, "qelu0")

                    @block.scalar
                    def _(sE):
                        ctx.wait(sE, "qelu0")
                        i = sE.activation(et3[:, 0:1024], et2[:, 0:1024], AF.Exp)
                        ctx.inc(i, "qexp")

                    @block.vector
                    def _(v):
                        ctx.wait(v, "qexp")
                        for half in range(2):
                            i = v.tensor_scalar(et2[:, half * 512:half * 512 + 512],
                                                PK[half][:, :], 0.0, None, ALU.max)
                        ctx.inc(i, "qelu1")
                        ctx.wait(v, "qelu1")
                        i = v.tensor_tensor(qtm[:, :], et2[:, 0:1024], et3[:, 0:1024],
                                            ALU.add)
                        ctx.inc(i, "qtmw")
                        ctx.wait(v, "qtmw")
                        i = v.tensor_tensor(et2[:, 0:1024], qtm[:, :], kmb[:, :],
                                            ALU.mult)
                        ctx.inc(i, "zmul")
                        ctx.wait(v, "zmul")
                        i = v.reduce_sum(zr[:, :],
                                         et2.ap().rearrange("p (h d) -> p h d", d=64),
                                         axis=AX.X)
                        ctx.inc(i, "zred")
                        ctx.wait(v, "zred")
                        i = v.tensor_scalar(zr[:, :], zr[:, :], 1e-6, None, ALU.add)
                        ctx.inc(i, "zadd")
                        ctx.wait(v, "zadd")
                        i = v.reciprocal(zr[:, :], zr[:, :])
                        ctx.inc(i, "zrcp")
                        ctx.wait(v, "zrcp")
                        k3 = qtm.ap().rearrange("p (j two) -> p j two", two=2)
                        s3 = qsw.ap().rearrange("p (j two) -> p j two", two=2)
                        v.tensor_copy(s3[:, :, 0:1], k3[:, :, 1:2])
                        i = v.tensor_copy(s3[:, :, 1:2], k3[:, :, 0:1])
                        ctx.inc(i, "qswd")
                        ctx.wait(v, "qswd")
                        v.tensor_tensor(et2[:, 0:1024], qtm[:, :], rcosb[:, :],
                                        ALU.mult)
                        i = v.tensor_tensor(et3[:, 0:1024], qsw[:, :], rsinb[:, :],
                                            ALU.mult)
                        ctx.inc(i, "qrot0")
                        ctx.wait(v, "qrot0")
                        i = v.tensor_tensor(et2[:, 0:1024], et2[:, 0:1024],
                                            et3[:, 0:1024], ALU.add)
                        ctx.inc(i, "qrot1")
                        ctx.wait(v, "qrot1")
                        e3 = et2.ap().rearrange("p (h d) -> p h d", d=64)
                        z3 = zr.ap().rearrange("p (h o) -> p h o", o=1)
                        i = v.tensor_tensor(e3, e3,
                                            z3.broadcast_to([P, 16, 64]), ALU.mult)
                        ctx.inc(i, "zfold")
                        ctx.wait(v, "zfold")
                        i = v.tensor_copy(qrzb[:, :], et2[:, 0:1024])
                        ctx.inc(i, "qzb")

                    for c in range(DC):
                        @block.tensor
                        def _(t):
                            ctx.wait(t, "qzb")
                            if not (it == 0 and tb == 0 and c == 0):
                                ctx.wait(t, "qcp")
                            i = t.transpose(TPB[:, :], qrzb[:, c * P:(c + 1) * P],
                                            id_sb[:, :])
                            ctx.inc(i, "qtr")

                        @block.vector
                        def _(v):
                            ctx.wait(v, "qtr")
                            i = v.tensor_copy(qfm[:, c * 512 + tb * P:
                                                  c * 512 + (tb + 1) * P],
                                              TPB[:, :])
                            ctx.inc(i, "qcp")

                for c in range(DC):
                    @block.tensor
                    def _(t):
                        ctx.wait(t, "qcp")
                        if it == 0 and c == 0:
                            ctx.wait(t, "kvbw")
                        if not (it == 0 and c == 0):
                            ctx.wait(t, "amw")
                        i = t.matmul(APS[:, :], kvbd[:, c * P:(c + 1) * P],
                                     qfm[:, c * 512:(c + 1) * 512],
                                     start=True, stop=True)
                        ctx.inc(i, "amm")

                    @block.vector
                    def _(v):
                        ctx.wait(v, "amm")
                        ctx.wait(v, "arl")
                        i = v.tensor_copy(et3x[:, 0:514],
                                          xif[:, c * 2564 + tt + 1:
                                              c * 2564 + tt + 1 + 514])
                        ctx.inc(i, "lcp")
                        ctx.wait(v, "lcp")
                        v.tensor_scalar(et1x[:, 0:512], et3x[:, 0:512],
                                        lw_sb[:, c:c + 1], None, ALU.mult)
                        v.tensor_scalar(et2[:, 0:512], et3x[:, 1:513],
                                        lw_sb[:, DC + c:DC + c + 1], None, ALU.mult)
                        v.tensor_tensor(et1x[:, 0:512], et1x[:, 0:512],
                                        et2[:, 0:512], ALU.add)
                        v.tensor_scalar(et2[:, 0:512], et3x[:, 2:514],
                                        lw_sb[:, 2 * DC + c:2 * DC + c + 1], None,
                                        ALU.mult)
                        v.tensor_tensor(et1x[:, 0:512], et1x[:, 0:512],
                                        et2[:, 0:512], ALU.add)
                        i = v.tensor_scalar(et1x[:, 0:512], et1x[:, 0:512],
                                            lb_sb[:, c:c + 1], None, ALU.add)
                        ctx.inc(i, "lep")
                        ctx.wait(v, "lep")
                        i = v.tensor_tensor(et2[:, 0:512], APS[:, :],
                                            et1x[:, 0:512], ALU.add)
                        ctx.inc(i, "aadd")
                        ctx.wait(v, "aadd")
                        i = v.tensor_tensor(amb[:, c * 512:(c + 1) * 512],
                                            et2[:, 0:512],
                                            art[:, c * 512:(c + 1) * 512], ALU.mult)
                        ctx.inc(i, "amw")

                for oc in range(DC):
                    @block.tensor
                    def _(t):
                        ctx.wait(t, "amw", ctx.val("amw"))
                        ctx.wait(t, "x7w", max(0, ctx.val("x7w") - 1))
                        for c in range(DC):
                            t.matmul(OPS[oc % 2][:, :],
                                     ow_sb[:, c * D + oc * P: c * D + (oc + 1) * P],
                                     amb[:, c * 512:(c + 1) * 512],
                                     start=(c == 0), stop=False)
                        i = t.matmul(OPS[oc % 2][:, :],
                                     wrow[0:1, D + oc * P:D + (oc + 1) * P],
                                     tokrow_b[0:1, :], start=False, stop=True)
                        ctx.inc(i, "omm")

                    @block.vector
                    def _(v):
                        ctx.wait(v, "omm")
                        ctx.wait(v, "xcl")
                        i = v.tensor_tensor(x7t[:, oc * 512:(oc + 1) * 512],
                                            OPS[oc % 2][:, :],
                                            xct[:, oc * 512:(oc + 1) * 512], ALU.add)
                        ctx.inc(i, "x7w")

                @block.sync
                def _(s):
                    ctx.wait(s, "x7w", ctx.val("x7w"))
                    for c in range(DC):
                        dma(ctx, s, x7d[c * P:(c + 1) * P, tt:tt + 512],
                            x7t[:, c * 512:(c + 1) * 512], "x7st")

            es1.close()

            es2 = es.enter_context(ExitStack())
            _c2 = Ctx.__new__(Ctx); _c2.__dict__ = dict(ctx.__dict__); _c2.es = es2
            ctx.es, es_outer = es2, ctx.es
            ensure_ln_shared(ctx)
            x7h = sb(_c2, "x7h", [P, DC * 514], F32)
            xc2t = sb(_c2, "xc2t", [P, DC * 512], F32)
            base_t = sb(_c2, "base_t", [P, DC * 512], F32)
            x4t = sb(_c2, "x4t", [P, DC * 512], F32)
            xnb2 = sb(_c2, "xnb2", [P, DC * 512], BF16)
            w1b = [sb(_c2, f"w1b{j}", [P, DC * 512], BF16) for j in range(2)]
            w2b = [sb(_c2, f"w2b{j}", [P, 4 * D], BF16) for j in range(2)]
            hgb = sb(_c2, "hgb", [P, 4 * 512], BF16)
            mlpacc = sb(_c2, "mlpacc", [P, DC * 512], F32)
            b1r = sb(_c2, "b1r", [1, 4096], BF16)
            b2r = sb(_c2, "b2r", [1, D], BF16)
            etc2 = sb(_c2, "etc2", [P, 512], F32)
            etc3 = sb(_c2, "etc3", [P, 512], F32)

            @block.gpsimd
            def _(g):
                ctx.wait(g, "x7st", ctx.val("x7st"))
                dma(ctx, g, b1r[0:1, :], fc1b, "ldb2")
                dma(ctx, g, b2r[0:1, :], fc2b, "ldb2")

            ghc = [0]
            for it in range(TC // 512):
                tt = it * 512

                @block.sync
                def _(s):
                    ctx.wait(s, "x7st", ctx.val("x7st"))
                    ctx.wait(s, "xc2w", ctx.val("xc2w"))
                    ctx.wait(s, "basew", ctx.val("basew"))
                    for c in range(DC):
                        dma(ctx, s, x7h[:, c * 514:c * 514 + 514],
                            x7d[c * P:(c + 1) * P, 255 + tt:255 + tt + 514], "x7l")
                        dma(ctx, s, x4t[:, c * 512:(c + 1) * 512],
                            x4I[c * P:(c + 1) * P, tt:tt + 512], "x4l")

                @block.vector
                def _(v):
                    ctx.wait(v, "x7l")
                    ctx.wait(v, "ldc")
                    ctx.wait(v, "ln_apply", ctx.val("ln_apply"))
                    if it == 0:
                        for c in range(DC):
                            i = v.tensor_scalar(x7h[:, c * 514:c * 514 + 1],
                                                x7h[:, c * 514:c * 514 + 1],
                                                edge_sb[:, 0:1], None, ALU.mult)
                        ctx.inc(i, "x7m")
                    if it == TC // 512 - 1:
                        for c in range(DC):
                            i = v.tensor_scalar(x7h[:, c * 514 + 513:c * 514 + 514],
                                                x7h[:, c * 514 + 513:c * 514 + 514],
                                                edge_sb[:, 1:2], None, ALU.mult)
                        ctx.inc(i, "x7m")
                    for c in range(DC):
                        v.tensor_scalar(etc2[:, :], x7h[:, c * 514:c * 514 + 512],
                                        cw_sb[:, c:c + 1], None, ALU.mult)
                        v.tensor_scalar(etc3[:, :], x7h[:, c * 514 + 1:c * 514 + 513],
                                        cw_sb[:, DC + c:DC + c + 1], None, ALU.mult)
                        v.tensor_tensor(etc2[:, :], etc2[:, :], etc3[:, :], ALU.add)
                        v.tensor_scalar(etc3[:, :], x7h[:, c * 514 + 2:c * 514 + 514],
                                        cw_sb[:, 2 * DC + c:2 * DC + c + 1], None,
                                        ALU.mult)
                        v.tensor_tensor(etc2[:, :], etc2[:, :], etc3[:, :], ALU.add)
                        v.tensor_scalar(etc2[:, :], etc2[:, :], cb_sb[:, c:c + 1],
                                        None, ALU.add)
                        i = v.tensor_tensor(xc2t[:, c * 512:(c + 1) * 512],
                                            x7h[:, c * 514 + 1:c * 514 + 513],
                                            etc2[:, :], ALU.add)
                        ctx.inc(i, "xc2w")
                    ctx.wait(v, "x4l")
                    ctx.wait(v, "x5w", ctx.val("x5w"))
                    for c in range(DC):
                        i = v.tensor_tensor(base_t[:, c * 512:(c + 1) * 512],
                                            xc2t[:, c * 512:(c + 1) * 512],
                                            x4t[:, c * 512:(c + 1) * 512], ALU.add)
                    ctx.inc(i, "basew")

                @block.sync
                def _(s):
                    ctx.wait(s, "xc2w", ctx.val("xc2w"))
                    for c in range(DC):
                        dma(ctx, s, xc2d[c * P:(c + 1) * P, tt:tt + 512],
                            xc2t[:, c * 512:(c + 1) * 512], "xc2st")

                @block.scalar
                def _(s):
                    ctx.wait(s, "fmm", ctx.val("fmm"))

                emit_ln_fm(ctx, block, C, xc2t, 512, f"f1ln_{it}", out_bf=xnb2,
                           gamma=g_sb, beta=b_sb, wait_names=("xc2w",))

                for hc in range(8):
                    par = ghc[0] % 2

                    @block.gpsimd
                    def _(g):
                        ctx.wait(g, "hcdone", max(0, ghc[0] - 1))
                        for c in range(DC):
                            dma(ctx, g, w1b[par][:, c * 512:(c + 1) * 512],
                                fc1[c * P:(c + 1) * P, hc * 512:(hc + 1) * 512],
                                f"f1ld{par}")
                        for u in range(4):
                            dma(ctx, g, w2b[par][:, u * D:(u + 1) * D],
                                fc2[hc * 512 + u * P: hc * 512 + (u + 1) * P, :],
                                f"f1ld{par}")
                    ldv = ctx.val(f"f1ld{par}")

                    for u in range(4):
                        @block.tensor
                        def _(t):
                            ctx.wait(t, f"f1ld{par}", ldv)
                            if ghc[0] == 0 and u == 0:
                                ctx.wait(t, "ldb2")
                            ctx.wait(t, "ln_apply", ctx.val("ln_apply"))
                            ctx.wait(t, "hgel", ctx.val("hgel"))
                            for c in range(DC):
                                t.matmul(APS[:, :],
                                         w1b[par][:, c * 512 + u * P:
                                                  c * 512 + (u + 1) * P],
                                         xnb2[:, c * 512:(c + 1) * 512],
                                         start=(c == 0), stop=False)
                            i = t.matmul(APS[:, :],
                                         b1r[0:1, hc * 512 + u * P:
                                             hc * 512 + (u + 1) * P],
                                         tokrow_b[0:1, :], start=False, stop=True)
                            ctx.inc(i, "fmm")

                        @block.scalar
                        def _(sE):
                            ctx.wait(sE, "fmm")
                            i = sE.activation(hgb[:, u * 512:(u + 1) * 512],
                                              APS[:, :], AF.Gelu)
                            ctx.inc(i, "hgel")

                    for oc in range(DC):
                        @block.tensor
                        def _(t):
                            ctx.wait(t, "hgel", ctx.val("hgel"))
                            ctx.wait(t, "macw", max(0, ctx.val("macw") - 1))
                            for u in range(4):
                                i = t.matmul(PK[oc % 2][:, :],
                                             w2b[par][:, u * D + oc * P:
                                                      u * D + (oc + 1) * P],
                                             hgb[:, u * 512:(u + 1) * 512],
                                             start=(u == 0),
                                             stop=(u == 3 and hc != 0))
                            if hc == 0:
                                i = t.matmul(PK[oc % 2][:, :],
                                             b2r[0:1, oc * P:(oc + 1) * P],
                                             tokrow_b[0:1, :],
                                             start=False, stop=True)
                            ctx.inc(i, "dmm")

                        @block.vector
                        def _(v):
                            ctx.wait(v, "dmm")
                            if hc == 0:
                                ctx.wait(v, "x5st", ctx.val("x5st"))
                                i = v.tensor_copy(
                                    mlpacc[:, oc * 512:(oc + 1) * 512],
                                    PK[oc % 2][:, :])
                            else:
                                i = v.tensor_tensor(
                                    mlpacc[:, oc * 512:(oc + 1) * 512],
                                    mlpacc[:, oc * 512:(oc + 1) * 512],
                                    PK[oc % 2][:, :], ALU.add)
                            ctx.inc(i, "macw")

                    @block.gpsimd
                    def _(g):
                        ctx.wait(g, "macw", ctx.val("macw"))
                        ctx.inc(g.engine_nop(), "hcdone")
                    ghc[0] += 1

                @block.sync
                def _(s):
                    ctx.wait(s, "macw", ctx.val("macw"))
                    for c in range(DC):
                        dma(ctx, s, mlpd[c * P:(c + 1) * P, tt:tt + 512],
                            mlpacc[:, c * 512:(c + 1) * 512], "mlpst")

                @block.vector
                def _(v):
                    ctx.wait(v, "macw", ctx.val("macw"))
                    ctx.wait(v, "mlpst", ctx.val("mlpst"))
                    for oc in range(DC):
                        i = v.tensor_tensor(mlpacc[:, oc * 512:(oc + 1) * 512],
                                            base_t[:, oc * 512:(oc + 1) * 512],
                                            mlpacc[:, oc * 512:(oc + 1) * 512],
                                            ALU.add)
                        ctx.inc(i, "x5w")

                @block.sync
                def _(s):
                    ctx.wait(s, "x5w", ctx.val("x5w"))
                    for c in range(DC):
                        dma(ctx, s, x5o[c * P:(c + 1) * P, tt:tt + 512],
                            mlpacc[:, c * 512:(c + 1) * 512], f"x5st")

            @block.sync
            def _(s):
                ctx.wait(s, "x5st", ctx.val("x5st"))
            ctx.es = es_outer
    return nc


def run_F1(inputs, pkvE, xc, ar, xi, x4):
    nc = build_F1()
    cw, cb = _dwc_host(inputs["L2_attn_cpe2_w"], inputs["L2_attn_cpe2_b"])
    lw, lb = _dwc_host(inputs["L2_attn_lepe_w"], inputs["L2_attn_lepe_b"])
    n2g = _fm_vec(inputs["L2_attn_norm2_g"])
    n2b = _fm_vec(inputs["L2_attn_norm2_b"])
    qk_w = np.asarray(inputs["L2_attn_qk_w"], np.float32)
    qk_b = np.asarray(inputs["L2_attn_qk_b"], np.float32)
    kmax = D // 2
    theta = 1.0 / 10000.0 ** (np.arange(kmax, dtype=np.float64) / kmax)
    rcos = np.repeat(np.cos(theta), 2)
    rsin = np.empty(D, np.float64)
    rsin[0::2] = -np.sin(theta)
    rsin[1::2] = np.sin(theta)
    ropeC = np.concatenate([rcos, rsin]).astype(np.float32).reshape(1, 2 * D)
    import ml_dtypes
    identb = np.eye(P, dtype=np.float32).astype(ml_dtypes.bfloat16)
    identf = np.eye(P, dtype=np.float32)
    pkvs = np.ascontiguousarray(pkvE.transpose(1, 0, 2).reshape(64, 8 * 1040))
    base = dict(qkwq=np.ascontiguousarray(qk_w[:, :D]),
                qkbq=np.ascontiguousarray(qk_b[:D]).reshape(1, D),
                ropeC=ropeC, lepew=lw, lepeb=lb,
                oww=np.asarray(inputs["L2_attn_out_w"], np.float32),
                owb=np.asarray(inputs["L2_attn_out_b"], np.float32).reshape(1, D),
                cpe2w=cw, cpe2b=cb, n2g=n2g, n2b=n2b,
                fc1=np.asarray(inputs["L2_attn_fc1_w"], np.float32),
                fc1b=np.asarray(inputs["L2_attn_fc1_b"], np.float32).reshape(1, 4096),
                fc2=np.asarray(inputs["L2_attn_fc2_w"], np.float32),
                fc2b=np.asarray(inputs["L2_attn_fc2_b"], np.float32).reshape(1, D),
                identb=identb, identf=identf, pkvs=pkvs)
    in_maps = []
    for c in range(NCORES):
        m = dict(base)
        m["xiI"] = xi[c]
        m["arI"] = ar[c]
        m["xcI"] = xc[c]
        m["x4I"] = x4[c] if isinstance(x4, list) else \
            np.ascontiguousarray(x4[:, c * TC:(c + 1) * TC])
        e = np.ones((P, 2), np.float32)
        if c == 0:
            e[:, 0] = 0.0
        if c == NCORES - 1:
            e[:, 1] = 0.0
        m["edge"] = e
        in_maps.append(m)
    res = run_bass_kernel_spmd(nc, in_maps, core_ids=list(range(NCORES)))
    global _F1_DBG
    _F1_DBG = res
    return [res.results[c]["x5o"] for c in range(NCORES)]


_F1_DBG = None




def build_F2():
    nc = bass.Bass("TRN2", target_bir_lowering=False)
    x5I = nc.dram_tensor("x5I", [D, TC], F32, kind="ExternalInput").ap()
    gwT = nc.dram_tensor("gwT", [D, 32], F32, kind="ExternalInput").ap()
    gbias = nc.dram_tensor("gbias", [1, 32], F32, kind="ExternalInput").ap()
    wg = nc.dram_tensor("wg", [32 * D, 512], F32, kind="ExternalInput").ap()
    wu = nc.dram_tensor("wu", [32 * D, 512], F32, kind="ExternalInput").ap()
    wd = nc.dram_tensor("wd", [32 * 512, D], F32, kind="ExternalInput").ap()
    swg = nc.dram_tensor("swg", [D, 512], F32, kind="ExternalInput").ap()
    swu = nc.dram_tensor("swu", [D, 512], F32, kind="ExternalInput").ap()
    swd = nc.dram_tensor("swd", [512, D], F32, kind="ExternalInput").ap()
    awg = nc.dram_tensor("awg", [16 * D, 256], F32, kind="ExternalInput").ap()
    awu = nc.dram_tensor("awu", [16 * D, 256], F32, kind="ExternalInput").ap()
    awd = nc.dram_tensor("awd", [16 * 256, D], F32, kind="ExternalInput").ap()
    identf = nc.dram_tensor("identf", [P, P], F32, kind="ExternalInput").ap()
    x6o = nc.dram_tensor("x6o", [D, TC], F32, kind="ExternalOutput").ap()

    NT4 = TC // 512

    with ExitStack() as es:
        ctx = Ctx(nc, es)
        C = make_consts(ctx)
        ensure_ln_shared(ctx)
        tokrow_b = sb(ctx, "tokrow_b", [1, 512], BF16)
        ln2b = sb(ctx, "ln2b", [P, DC * TC], BF16)
        routed = sb(ctx, "routed", [P, DC * TC], F32)
        cwT32 = sb(ctx, "cwT32", [32, TC], F32)
        acwT = sb(ctx, "acwT", [16, TC], F32)
        id_sb = sb(ctx, "id_sb", [P, P], F32)

        with nc.Block() as block:
            @block.vector
            def _(v):
                v.memset(C["ones_colf"][:, :], 1.0)
                v.memset(C["ones_colb"][:, :], 1.0)
                v.memset(C["ones_rowf"][:, :], 1.0)
                v.memset(C["ones_rowb"][:, :], 1.0)
                v.memset(C["ones_tokb"][:, :], 1.0)
                v.memset(tokrow_b[:, :], 1.0)
                for c in range(DC):
                    i = v.memset(routed[:, c * TC:(c + 1) * TC], 0.0)
                ctx.inc(i, "consts")

            es2 = es.enter_context(ExitStack())
            _c2 = Ctx.__new__(Ctx); _c2.__dict__ = dict(ctx.__dict__); _c2.es = es2
            gwT_sb = sb(_c2, "gwT_sb", [P, DC * 32], F32)
            gb_bc = sb(_c2, "gb_bc", [P, 32], F32)
            x5t2 = sb(_c2, "x5t2", [P, DC * 512], F32)
            lnf_t = sb(_c2, "lnf_t", [P, DC * 512], F32)
            s_sb = sb(_c2, "s_sb", [P, 32], F32)
            sc_sb = sb(_c2, "sc_sb", [P, 32], F32)
            gm_sb = sb(_c2, "gm_sb", [P, 32], F32)
            msc = sb(_c2, "msc", [P, 32], F32)
            sel = sb(_c2, "sel", [P, 32], F32)
            sel1 = sb(_c2, "sel1", [P, 32], F32)
            selw = sb(_c2, "selw", [P, 32], F32)
            g1 = sb(_c2, "g1", [P, 4], F32)
            g2 = sb(_c2, "g2", [P, 4], F32)
            gsc = sb(_c2, "gsc", [P, 4], F32)
            gneg = sb(_c2, "gneg", [P, 4], F32)
            grow = sb(_c2, "grow", [P, 1], F32)
            wsum = sb(_c2, "wsum", [P, 1], F32)
            acw = sb(_c2, "acw", [P, 16], F32)
            zps = ps(_c2, "zps", [P, 512], F32)
            trps = ps(_c2, "trps", [32, P], F32)

            @block.sync
            def _(s):
                for c in range(DC):
                    dma(ctx, s, gwT_sb[:, c * 32:(c + 1) * 32],
                        gwT[c * P:(c + 1) * P, :], "ld_gw")
                dma(ctx, s, id_sb[:, :], identf, "ld_gw")

            @block.gpsimd
            def _(g):
                dma(ctx, g, gb_bc[0:1, 0:32], gbias, "ld_gb")

            @block.tensor
            def _(t):
                ctx.wait(t, "ld_gb")
                ctx.wait(t, "consts")
                i = t.matmul(zps[:, 0:32], C["ones_rowf"][0:1, :],
                             gb_bc[0:1, 0:32], start=True, stop=True)
                ctx.inc(i, "gbm")

            @block.vector
            def _(v):
                ctx.wait(v, "gbm")
                i = v.tensor_copy(gb_bc[:, :], zps[:, 0:32])
                ctx.inc(i, "gbc")

            def ve(v, i, tag="ve"):
                ctx.inc(i, tag)
                ctx.wait(v, tag)

            for it in range(NT4):
                tt = it * 512

                @block.sync
                def _(s):
                    if it > 0:
                        ctx.wait(s, f"route", 4 * it)
                        ctx.wait(s, "ln2bc", it)
                    for c in range(DC):
                        dma(ctx, s, x5t2[:, c * 512:(c + 1) * 512],
                            x5I[c * P:(c + 1) * P, tt:tt + 512], f"ld_x5r")

                emit_ln_fm(ctx, block, C, x5t2, 512, f"F2ln_{it}",
                           out_f32=lnf_t, wait_names=(f"ld_x5r",))

                @block.vector
                def _(v):
                    ctx.wait(v, "ln_apply")
                    for c in range(DC):
                        i = v.tensor_copy(ln2b[:, c * TC + tt: c * TC + tt + 512],
                                          lnf_t[:, c * 512:(c + 1) * 512])
                    ctx.inc(i, "ln2bc")

                for tb in range(4):
                    tg = tt + tb * P

                    @block.tensor
                    def _(t):
                        ctx.wait(t, "ln_apply")
                        ctx.wait(t, "ld_gw")
                        if it == 0 and tb == 0:
                            ctx.wait(t, "gbc")
                        else:
                            ctx.wait(t, "sccp")
                        for c in range(DC):
                            i = t.matmul(zps[:, 0:32],
                                         lnf_t[:, c * 512 + tb * P: c * 512 + (tb + 1) * P],
                                         gwT_sb[:, c * 32:(c + 1) * 32],
                                         start=(c == 0), stop=(c == DC - 1))
                        ctx.inc(i, "zmm")

                    @block.vector
                    def _(v):
                        ctx.wait(v, "zmm")
                        i = v.tensor_copy(sc_sb[:, :], zps[:, 0:32])
                        ve(v, i, "sccp")

                    @block.scalar
                    def _(s):
                        ctx.wait(s, "sccp")
                        i = s.activation(s_sb[:, :], sc_sb[:, :], AF.Sigmoid)
                        ctx.inc(i, "sig")

                    @block.vector
                    def _(v):
                        ctx.wait(v, "sig")
                        ctx.wait(v, "gbc")
                        i = v.tensor_tensor(sc_sb[:, :], s_sb[:, :], gb_bc[:, :],
                                            ALU.add)
                        ve(v, i)
                        sc3 = sc_sb.ap().rearrange("p (g e) -> p g e", e=8)
                        i = v.reduce_max(g1[:, :], sc3, axis=AX.X)
                        ve(v, i)
                        for g in range(4):
                            i = v.tensor_scalar(gm_sb[:, g * 8:(g + 1) * 8],
                                                sc_sb[:, g * 8:(g + 1) * 8],
                                                g1[:, g:g + 1], None, ALU.is_ge)
                        ve(v, i)
                        i = v.tensor_scalar(msc[:, :], gm_sb[:, :], 1e30, None,
                                            ALU.mult)
                        ve(v, i)
                        i = v.tensor_tensor(msc[:, :], sc_sb[:, :], msc[:, :],
                                            ALU.subtract)
                        ve(v, i)
                        m3 = msc.ap().rearrange("p (g e) -> p g e", e=8)
                        i = v.reduce_max(g2[:, :], m3, axis=AX.X)
                        ve(v, i)
                        i = v.tensor_tensor(gsc[:, :], g1[:, :], g2[:, :], ALU.add)
                        ve(v, i)
                        i = v.reduce_max(grow[:, :],
                                         gsc.ap().rearrange("p (o g) -> p o g", o=1),
                                         axis=AX.X)
                        ve(v, i)
                        i = v.tensor_scalar(gneg[:, :], gsc[:, :], grow[:, 0:1],
                                            None, ALU.is_ge)
                        ve(v, i)
                        i = v.tensor_scalar(g2[:, :], gneg[:, :], 1e30, None,
                                            ALU.mult)
                        ve(v, i)
                        i = v.tensor_tensor(g1[:, :], gsc[:, :], g2[:, :],
                                            ALU.subtract)
                        ve(v, i)
                        i = v.reduce_max(grow[:, :],
                                         g1.ap().rearrange("p (o g) -> p o g", o=1),
                                         axis=AX.X)
                        ve(v, i)
                        i = v.tensor_scalar(g2[:, :], g1[:, :], grow[:, 0:1],
                                            None, ALU.is_ge)
                        ve(v, i)
                        i = v.tensor_tensor(gneg[:, :], gneg[:, :], g2[:, :], ALU.add)
                        ve(v, i)
                        i = v.tensor_scalar(g2[:, :], gneg[:, :], -1.0, 1e30,
                                            ALU.add, ALU.mult)
                        ve(v, i)
                        m3 = msc.ap().rearrange("p (g e) -> p g e", e=8)
                        s3 = sc_sb.ap().rearrange("p (g e) -> p g e", e=8)
                        gn3 = gneg.ap().rearrange("p (g o) -> p g o", o=1)
                        gg3 = g2.ap().rearrange("p (g o) -> p g o", o=1)
                        i = v.tensor_tensor(m3, s3, gn3.broadcast_to([P, 4, 8]),
                                            ALU.mult)
                        ve(v, i)
                        i = v.tensor_tensor(m3, m3, gg3.broadcast_to([P, 4, 8]),
                                            ALU.add)
                        ve(v, i)
                        i = v.memset(sel[:, :], 0.0)
                        ve(v, i)
                        for kk in range(6):
                            i = v.reduce_max(grow[:, :],
                                             msc.ap().rearrange("p (o e) -> p o e",
                                                                o=1),
                                             axis=AX.X)
                            ve(v, i)
                            i = v.tensor_scalar(selw[:, :], msc[:, :], grow[:, 0:1],
                                                None, ALU.is_ge)
                            ve(v, i)
                            if kk == 0:
                                i = v.tensor_copy(sel1[:, :], selw[:, :])
                                ve(v, i)
                            i = v.tensor_tensor(sel[:, :], sel[:, :], selw[:, :],
                                                ALU.add)
                            ve(v, i)
                            i = v.tensor_scalar(selw[:, :], selw[:, :], 1e30, None,
                                                ALU.mult)
                            ve(v, i)
                            i = v.tensor_tensor(msc[:, :], msc[:, :], selw[:, :],
                                                ALU.subtract)
                            ve(v, i)
                        i = v.tensor_scalar(sel[:, :], sel[:, :], 1.0, None, ALU.min)
                        ve(v, i)
                        i = v.tensor_tensor(selw[:, :], sel[:, :], s_sb[:, :],
                                            ALU.mult)
                        ve(v, i)
                        i = v.reduce_sum(wsum[:, :],
                                         selw.ap().rearrange("p (o e) -> p o e", o=1),
                                         axis=AX.X)
                        ve(v, i)
                        i = v.tensor_scalar(wsum[:, :], wsum[:, :], 1e-20, None,
                                            ALU.add)
                        ve(v, i)
                        i = v.reciprocal(wsum[:, :], wsum[:, :])
                        ve(v, i)
                        i = v.tensor_scalar(selw[:, :], selw[:, :], wsum[:, 0:1],
                                            None, ALU.mult)
                        ve(v, i)
                        i = v.tensor_scalar(selw[:, :], selw[:, :], 0.2, None,
                                            ALU.mult)
                        ve(v, i)
                        i = v.reduce_sum(acw[:, :],
                                         sel1.ap().rearrange("p (l two) -> p l two",
                                                             two=2),
                                         axis=AX.X)
                        ve(v, i)
                        i = v.tensor_scalar(acw[:, :], acw[:, :], 0.1, None,
                                            ALU.mult)
                        ctx.inc(i, "cwrdy")

                    @block.tensor
                    def _(t):
                        ctx.wait(t, "cwrdy")
                        i = t.transpose(trps[0:32, :], selw[:, :], id_sb[:, :])
                        ctx.inc(i, "trmm")

                    @block.vector
                    def _(v):
                        ctx.wait(v, "trmm")
                        i = v.tensor_copy(cwT32[:, tg:tg + P], trps[0:32, :])
                        ctx.inc(i, "trcp")

                    @block.tensor
                    def _(t):
                        ctx.wait(t, "trcp")
                        i = t.transpose(trps[0:16, :], acw[:, :], id_sb[:, :])
                        ctx.inc(i, "trmm2")

                    @block.vector
                    def _(v):
                        ctx.wait(v, "trmm2")
                        i = v.tensor_copy(acwT[:, tg:tg + P], trps[0:16, :])
                        ctx.inc(i, "route")

            es2.close()

            es3 = es.enter_context(ExitStack())
            _c3 = Ctx.__new__(Ctx); _c3.__dict__ = dict(ctx.__dict__); _c3.es = es3
            wg_b = [sb(_c3, f"wg_b{j}", [P, DC * 512], BF16) for j in range(2)]
            wu_b = [sb(_c3, f"wu_b{j}", [P, DC * 512], BF16) for j in range(2)]
            wd_b = [sb(_c3, f"wd_b{j}", [P, 4 * D], BF16) for j in range(2)]
            sg_sb = sb(_c3, "sg_sb", [P, 512], F32)
            h_sb = sb(_c3, "h_sb", [P, 512], F32)
            hs_sb = sb(_c3, "hs_sb", [P, 4 * 512], BF16)
            cwb_sb = sb(_c3, "cwb_sb", [P, 512], F32)
            cwrow = sb(_c3, "cwrow", [1, 512], F32)
            gps_ = ps(_c3, "gps", [P, 512], F32)
            ups_ = ps(_c3, "ups", [P, 512], F32)
            dps_ = ps(_c3, "dps", [P, 512], F32)
            cps_ = ps(_c3, "cps", [P, 512], F32)
            ldtags = []
            NEXP = 49

            for e in range(NEXP):
                par = e % 2
                anchor = e >= 33
                hce = 2 if anchor else 4
                al = e - 33

                @block.gpsimd
                def _(g):
                    if e == 0:
                        ctx.wait(g, "route", 16)
                    if e >= 2:
                        ctx.wait(g, "edone", e - 1)
                    for c in range(DC):
                        if e < 32:
                            dma(ctx, g, wg_b[par][:, c * 512:(c + 1) * 512],
                                wg[e * D + c * P: e * D + (c + 1) * P, :], f"wld{par}")
                            dma(ctx, g, wu_b[par][:, c * 512:(c + 1) * 512],
                                wu[e * D + c * P: e * D + (c + 1) * P, :], f"wld{par}")
                        elif e == 32:
                            dma(ctx, g, wg_b[par][:, c * 512:(c + 1) * 512],
                                swg[c * P:(c + 1) * P, :], f"wld{par}")
                            dma(ctx, g, wu_b[par][:, c * 512:(c + 1) * 512],
                                swu[c * P:(c + 1) * P, :], f"wld{par}")
                        else:
                            dma(ctx, g, wg_b[par][:, c * 512:c * 512 + 256],
                                awg[al * D + c * P: al * D + (c + 1) * P, :],
                                f"wld{par}")
                            dma(ctx, g, wu_b[par][:, c * 512:c * 512 + 256],
                                awu[al * D + c * P: al * D + (c + 1) * P, :],
                                f"wld{par}")
                    for hc in range(hce):
                        if e < 32:
                            dma(ctx, g, wd_b[par][:, hc * D:(hc + 1) * D],
                                wd[e * 512 + hc * P: e * 512 + (hc + 1) * P, :],
                                f"wld{par}")
                        elif e == 32:
                            dma(ctx, g, wd_b[par][:, hc * D:(hc + 1) * D],
                                swd[hc * P:(hc + 1) * P, :], f"wld{par}")
                        else:
                            dma(ctx, g, wd_b[par][:, hc * D:(hc + 1) * D],
                                awd[al * 256 + hc * P: al * 256 + (hc + 1) * P, :],
                                f"wld{par}")
                ldtags.append(ctx.val(f"wld{par}"))

                for it in range(NT4):
                    tt = it * 512

                    if e != 32:
                        @block.sync
                        def _(sy):
                            if e == 0 and it == 0:
                                ctx.wait(sy, "route", 16)
                            if not (e == 0 and it == 0):
                                ctx.wait(sy, "cwmm")
                            if e < 32:
                                dma(ctx, sy, cwrow[0:1, :], cwT32[e:e + 1, tt:tt + 512],
                                    "cwdma")
                            else:
                                dma(ctx, sy, cwrow[0:1, :], acwT[al:al + 1, tt:tt + 512],
                                    "cwdma")

                        @block.tensor
                        def _(t):
                            ctx.wait(t, "cwdma")
                            if e == 0 and it == 0:
                                ctx.wait(t, "consts")
                            if not (e == 0 and it == 0):
                                ctx.wait(t, "cwbc")
                            i = t.matmul(cps_[:, :], C["ones_rowf"][0:1, :],
                                         cwrow[0:1, :], start=True, stop=True)
                            ctx.inc(i, "cwmm")

                        @block.vector
                        def _(v):
                            ctx.wait(v, "cwmm")
                            i = v.tensor_copy(cwb_sb[:, :], cps_[:, :])
                            ctx.inc(i, "cwbc")

                    for hc in range(hce):
                        @block.tensor
                        def _(t):
                            ctx.wait(t, f"wld{par}", ldtags[-1])
                            if not (e == 0 and it == 0 and hc == 0):
                                ctx.wait(t, "hsil")
                            for c in range(DC):
                                i = t.matmul(gps_[:, :],
                                             wg_b[par][:, c * 512 + hc * P: c * 512 + (hc + 1) * P],
                                             ln2b[:, c * TC + tt: c * TC + tt + 512],
                                             start=(c == 0), stop=(c == DC - 1))
                            ctx.inc(i, "gmm")
                            if not (e == 0 and it == 0 and hc == 0):
                                ctx.wait(t, "hmul")
                            for c in range(DC):
                                i = t.matmul(ups_[:, :],
                                             wu_b[par][:, c * 512 + hc * P: c * 512 + (hc + 1) * P],
                                             ln2b[:, c * TC + tt: c * TC + tt + 512],
                                             start=(c == 0), stop=(c == DC - 1))
                            ctx.inc(i, "umm")

                        @block.scalar
                        def _(s):
                            ctx.wait(s, "gmm")
                            if not (e == 0 and it == 0 and hc == 0):
                                ctx.wait(s, "hmul")
                            i = s.activation(sg_sb[:, :], gps_[:, :], AF.Silu)
                            ctx.inc(i, "hsil")

                        @block.vector
                        def _(v):
                            ctx.wait(v, "umm")
                            ctx.wait(v, "hsil")
                            if e != 32:
                                if not (e == 0 and it == 0 and hc == 0):
                                    ctx.wait(v, "hcw")
                                i = v.tensor_tensor(h_sb[:, :], sg_sb[:, :], ups_[:, :],
                                                    ALU.mult)
                                ctx.inc(i, "hmul")
                            else:
                                i = v.tensor_tensor(hs_sb[:, hc * 512:(hc + 1) * 512],
                                                    sg_sb[:, :], ups_[:, :], ALU.mult)
                                ctx.inc(i, "hmul")
                                ctx.inc(v.engine_nop(), "hcw")

                        if e != 32:
                            @block.gpsimd
                            def _(g):
                                ctx.wait(g, "hmul")
                                ctx.wait(g, "cwbc")
                                i = g.tensor_tensor(hs_sb[:, hc * 512:(hc + 1) * 512],
                                                    h_sb[:, :], cwb_sb[:, :], ALU.mult)
                                ctx.inc(i, "hcw")

                    for oc in range(DC):
                        @block.tensor
                        def _(t):
                            ctx.wait(t, "hcw")
                            if not (e == 0 and it == 0 and oc == 0):
                                ctx.wait(t, "racc")
                            for hc in range(hce):
                                i = t.matmul(dps_[:, :],
                                             wd_b[par][:, hc * D + oc * P: hc * D + (oc + 1) * P],
                                             hs_sb[:, hc * 512:(hc + 1) * 512],
                                             start=(hc == 0), stop=(hc == hce - 1))
                            ctx.inc(i, "dmm")

                        @block.vector
                        def _(v):
                            ctx.wait(v, "dmm")
                            i = v.tensor_tensor(routed[:, oc * TC + tt: oc * TC + tt + 512],
                                                routed[:, oc * TC + tt: oc * TC + tt + 512],
                                                dps_[:, :], ALU.add)
                            ctx.inc(i, "racc")

                @block.gpsimd
                def _(g):
                    ctx.wait(g, "racc")
                    ctx.inc(g.engine_nop(), "edone")

            es3.close()

            x5t3 = sb(ctx, "x5t3", [P, DC * 512], F32)
            x6t = sb(ctx, "x6t", [P, DC * 512], F32)

            for it in range(NT4):
                tt = it * 512

                @block.sync
                def _(s):
                    ctx.wait(s, "edone", NEXP)
                    if it > 0:
                        ctx.wait(s, f"x6st", 8 * it)
                    for c in range(DC):
                        dma(ctx, s, x5t3[:, c * 512:(c + 1) * 512],
                            x5I[c * P:(c + 1) * P, tt:tt + 512], f"ld_x5a")

                @block.vector
                def _(v):
                    ctx.wait(v, f"ld_x5a")
                    ctx.wait(v, "x6st", ctx.val("x6st"))
                    for c in range(DC):
                        i = v.tensor_tensor(x6t[:, c * 512:(c + 1) * 512],
                                            x5t3[:, c * 512:(c + 1) * 512],
                                            routed[:, c * TC + tt: c * TC + tt + 512],
                                            ALU.add)
                        ctx.inc(i, "x6w")

                @block.sync
                def _(s):
                    ctx.wait(s, "x6w", ctx.val("x6w"))
                    for c in range(DC):
                        dma(ctx, s, x6o[c * P:(c + 1) * P, tt:tt + 512],
                            x6t[:, c * 512:(c + 1) * 512], f"x6st")

            @block.sync
            def _(s):
                ctx.wait(s, "x6st", ctx.val("x6st"))
    return nc


def run_F2(inputs, x5):
    nc = build_F2()
    base = dict(
        gwT=np.ascontiguousarray(np.asarray(inputs["L2_ffn_gate_weight"],
                                            np.float32).T),
        gbias=np.asarray(inputs["L2_ffn_gate_bias"], np.float32).reshape(1, 32),
        wg=np.asarray(inputs["L2_ffn_w_gate"], np.float32).reshape(32 * D, 512),
        wu=np.asarray(inputs["L2_ffn_w_up"], np.float32).reshape(32 * D, 512),
        wd=np.asarray(inputs["L2_ffn_w_down"], np.float32).reshape(32 * 512, D),
        swg=np.asarray(inputs["L2_ffn_shared_w_gate"], np.float32),
        swu=np.asarray(inputs["L2_ffn_shared_w_up"], np.float32),
        swd=np.asarray(inputs["L2_ffn_shared_w_down"], np.float32),
        awg=np.asarray(inputs["L2_ffn_anchor_w_gate"], np.float32).reshape(16 * D, 256),
        awu=np.asarray(inputs["L2_ffn_anchor_w_up"], np.float32).reshape(16 * D, 256),
        awd=np.asarray(inputs["L2_ffn_anchor_w_down"], np.float32).reshape(16 * 256, D),
        identf=np.eye(P, dtype=np.float32))
    in_maps = []
    for c in range(NCORES):
        m = dict(base)
        m["x5I"] = x5[c]
        in_maps.append(m)
    res = run_bass_kernel_spmd(nc, in_maps, core_ids=list(range(NCORES)))
    return [res.results[c]["x6o"] for c in range(NCORES)]




def build_F3():
    nc = bass.Bass("TRN2", target_bir_lowering=False)
    x6I = nc.dram_tensor("x6I", [D, TC], F32, kind="ExternalInput").ap()
    l1g = nc.dram_tensor("l1g", [P, DC], F32, kind="ExternalInput").ap()
    l1b = nc.dram_tensor("l1b", [P, DC], F32, kind="ExternalInput").ap()
    qwT = nc.dram_tensor("qwT", [D, D], F32, kind="ExternalInput").ap()
    kwT = nc.dram_tensor("kwT", [D, D], F32, kind="ExternalInput").ap()
    vwT = nc.dram_tensor("vwT", [D, D], F32, kind="ExternalInput").ap()
    qkvb = nc.dram_tensor("qkvb", [1, 3 * D], F32, kind="ExternalInput").ap()
    owT = nc.dram_tensor("owT", [D, D], F32, kind="ExternalInput").ap()
    ob = nc.dram_tensor("ob", [1, D], F32, kind="ExternalInput").ap()
    mbd = nc.dram_tensor("mbd", [P, P], F32, kind="ExternalInput").ap()
    identb = nc.dram_tensor("identb", [P, P], BF16, kind="ExternalInput").ap()
    l2g = nc.dram_tensor("l2g", [P, DC], F32, kind="ExternalInput").ap()
    l2b = nc.dram_tensor("l2b", [P, DC], F32, kind="ExternalInput").ap()
    fcw = nc.dram_tensor("fcw", [D, 4096], F32, kind="ExternalInput").ap()
    fcb = nc.dram_tensor("fcb", [1, 4096], F32, kind="ExternalInput").ap()
    pjw = nc.dram_tensor("pjw", [4096, D], F32, kind="ExternalInput").ap()
    pjb = nc.dram_tensor("pjb", [1, D], F32, kind="ExternalInput").ap()
    x7o = nc.dram_tensor("x7o", [D, TC], F32, kind="ExternalOutput").ap()
    attd = nc.dram_tensor("attd", [TC, D], F32).ap()

    with ExitStack() as es:
        ctx = Ctx(nc, es)
        C = make_consts(ctx)
        ensure_ln_shared(ctx)
        tokrow_b = sb(ctx, "tokrow_b", [1, 512], BF16)
        id_sb = sb(ctx, "id_sb", [P, P], BF16)
        mbd_sb = sb(ctx, "mbd_sb", [P, P], F32)
        g1_sb = sb(ctx, "g1_sb", [P, DC], F32)
        b1_sb = sb(ctx, "b1_sb", [P, DC], F32)
        g2_sb = sb(ctx, "g2_sb", [P, DC], F32)
        b2_sb = sb(ctx, "b2_sb", [P, DC], F32)
        wstr = sb(ctx, "wstr", [P, DC * D], BF16)
        brow = sb(ctx, "brow", [1, 4 * D], BF16)
        x6t = sb(ctx, "x6t", [P, DC * 512], F32)
        xlt = sb(ctx, "xlt", [P, DC * 512], F32)
        xnb = sb(ctx, "xnb", [P, DC * 512], BF16)
        qfb = sb(ctx, "qfb", [P, DC * 512], BF16)
        kfb = sb(ctx, "kfb", [P, DC * 512], BF16)
        vfb = sb(ctx, "vfb", [P, DC * 512], BF16)
        et1 = sb(ctx, "et1", [P, 512], F32)
        et2 = sb(ctx, "et2", [P, 512], F32)
        et3 = sb(ctx, "et3", [P, 512], F32)
        Qd = sb(ctx, "Qd", [64, 2048], BF16)
        Kd = sb(ctx, "Kd", [64, 2048], BF16)
        Vd = sb(ctx, "Vd", [64, 2048], BF16)
        smT = sb(ctx, "smT", [P, P], BF16)
        vst = sb(ctx, "vst", [P, 65], BF16)
        dcol = sb(ctx, "dcol", [P, 1], F32)
        atg = [sb(ctx, f"atg{j}", [P, 64], F32) for j in range(2)]
        atmb = sb(ctx, "atmb", [P, D], BF16)
        afm = sb(ctx, "afm", [P, DC * 512], BF16)
        x6a = sb(ctx, "x6a", [P, DC * 512], F32)
        mlpacc = sb(ctx, "mlpacc", [P, DC * 512], F32)
        hgb = sb(ctx, "hgb", [P, 4 * 512], BF16)
        _w1s = sb(ctx, "w1b0", [P, DC * 512], BF16)
        _w2s = sb(ctx, "w2b0", [P, 4 * D], BF16)
        w1b = [_w1s, _w1s]
        w2b = [_w2s, _w2s]
        b1r = sb(ctx, "b1r", [1, 4096], BF16)
        b2r = sb(ctx, "b2r", [1, D], BF16)
        PK = [ps(ctx, f"PKg{j}", [P, 512], F32) for j in range(2)]
        APS = ps(ctx, "APSg", [P, 512], F32)
        TPB = ps(ctx, "TPBg", [P, P], BF16)

        with nc.Block() as block:
            @block.vector
            def _(v):
                v.memset(C["ones_colf"][:, :], 1.0)
                v.memset(C["ones_colb"][:, :], 1.0)
                v.memset(C["ones_rowf"][:, :], 1.0)
                v.memset(C["ones_rowb"][:, :], 1.0)
                v.memset(C["ones_tokb"][:, :], 1.0)
                v.memset(tokrow_b[:, :], 1.0)
                i = v.memset(vst[:, 64:65], 1.0)
                ctx.inc(i, "consts")

            @block.gpsimd
            def _(g):
                dma(ctx, g, brow[0:1, 0:3 * D], qkvb, "ldw")
                dma(ctx, g, brow[0:1, 3 * D:4 * D], ob, "ldw")
                dma(ctx, g, b1r[0:1, :], fcb, "ldw")
                dma(ctx, g, b2r[0:1, :], pjb, "ldw")

            @block.sync
            def _(s):
                dma(ctx, s, id_sb[:, :], identb, "ldc")
                dma(ctx, s, mbd_sb[:, :], mbd, "ldc")
                dma(ctx, s, g1_sb[:, :], l1g, "ldc")
                dma(ctx, s, b1_sb[:, :], l1b, "ldc")
                dma(ctx, s, g2_sb[:, :], l2g, "ldc")
                dma(ctx, s, b2_sb[:, :], l2b, "ldc")

            ghc = [0]
            for it in range(TC // 512):
                tt = it * 512

                @block.sync
                def _(s):
                    ctx.wait(s, "x7wt", ctx.val("x7wt"))
                    for c in range(DC):
                        dma(ctx, s, x6t[:, c * 512:(c + 1) * 512],
                            x6I[c * P:(c + 1) * P, tt:tt + 512], "x6l")

                @block.scalar
                def _(s):
                    ctx.wait(s, "qkmm", ctx.val("qkmm"))
                    ctx.wait(s, "x6aw", ctx.val("x6aw"))

                emit_ln_fm(ctx, block, C, x6t, 512, f"f3lnO_{it}",
                           out_f32=xlt, wait_names=("x6l", "ldc"))
                emit_ln_fm(ctx, block, C, xlt, 512, f"f3ln_{it}", out_bf=xnb,
                           gamma=g1_sb, beta=b1_sb)

                for w in range(3):
                    wsrc = [qwT, kwT, vwT][w]
                    dst = [qfb, kfb, vfb][w]

                    @block.gpsimd
                    def _(g):
                        ctx.wait(g, "qkmm", ctx.val("qkmm"))
                        ctx.wait(g, "opmm", ctx.val("opmm"))
                        for c in range(DC):
                            dma(ctx, g, wstr[:, c * D:(c + 1) * D],
                                wsrc[c * P:(c + 1) * P, :], "wstl")
                    wldv = ctx.val("wstl")

                    for oc in range(DC):
                        @block.tensor
                        def _(t):
                            ctx.wait(t, "ln_apply")
                            ctx.wait(t, "wstl", wldv)
                            if it == 0 and w == 0 and oc == 0:
                                ctx.wait(t, "ldw")
                                ctx.wait(t, "consts")
                            else:
                                ctx.wait(t, "qkev", ctx.val("qkev"))
                            for c in range(DC):
                                t.matmul(PK[0][:, :],
                                         wstr[:, c * D + oc * P: c * D + (oc + 1) * P],
                                         xnb[:, c * 512:(c + 1) * 512],
                                         start=(c == 0), stop=False)
                            i = t.matmul(PK[0][:, :],
                                         brow[0:1, w * D + oc * P: w * D + (oc + 1) * P],
                                         tokrow_b[0:1, :], start=False, stop=True)
                            ctx.inc(i, "qkmm")

                        if w < 2:
                            @block.vector
                            def _(v):
                                ctx.wait(v, "qkmm")
                                i = v.tensor_scalar(et1[:, :], PK[0][:, :], 0.0,
                                                    None, ALU.min)
                                ctx.inc(i, "qke0")

                            @block.scalar
                            def _(sE):
                                ctx.wait(sE, "qke0")
                                i = sE.activation(et2[:, :], et1[:, :], AF.Exp)
                                ctx.inc(i, "qke1")

                            @block.vector
                            def _(v):
                                ctx.wait(v, "qke1")
                                i = v.tensor_scalar(et1[:, :], PK[0][:, :], 0.0,
                                                    None, ALU.max)
                                ctx.inc(i, "qke2")
                                ctx.wait(v, "qke2")
                                i = v.tensor_tensor(et3[:, :], et1[:, :], et2[:, :],
                                                    ALU.add)
                                ctx.inc(i, "qke3")
                                ctx.wait(v, "qke3")
                                i = v.tensor_copy(dst[:, oc * 512:(oc + 1) * 512],
                                                  et3[:, :])
                                ctx.inc(i, "qkev")
                        else:
                            @block.vector
                            def _(v):
                                ctx.wait(v, "qkmm")
                                i = v.tensor_copy(dst[:, oc * 512:(oc + 1) * 512],
                                                  PK[0][:, :])
                                ctx.inc(i, "qkev")

                for tb in range(4):
                    @block.vector
                    def _(v):
                        ctx.wait(v, "qkev", ctx.val("qkev"))
                        ctx.wait(v, "smm", ctx.val("smm"))
                        ctx.wait(v, "qvmm", ctx.val("qvmm"))
                        for h in range(16):
                            so = (h % 2) * 64
                            co = (h // 2) * 512 + tb * P
                            v.tensor_copy(
                                Qd.ap().rearrange("p (gt h) -> p gt h", h=16)[:, :, h:h + 1],
                                qfb[so:so + 64, co:co + P].rearrange(
                                    "p (x o) -> p x o", o=1))
                            v.tensor_copy(
                                Kd.ap().rearrange("p (gt h) -> p gt h", h=16)[:, :, h:h + 1],
                                kfb[so:so + 64, co:co + P].rearrange(
                                    "p (x o) -> p x o", o=1))
                            i = v.tensor_copy(
                                Vd.ap().rearrange("p (gt h) -> p gt h", h=16)[:, :, h:h + 1],
                                vfb[so:so + 64, co:co + P].rearrange(
                                    "p (x o) -> p x o", o=1))
                        ctx.inc(i, "dmj")

                    for g in range(16):
                        @block.tensor
                        def _(t):
                            ctx.wait(t, "dmj")
                            if not (it == 0 and tb == 0 and g == 0):
                                ctx.wait(t, "smev", ctx.val("smev"))
                            i = t.matmul(PK[0][:, 0:P], Kd[:, g * P:(g + 1) * P],
                                         Qd[:, g * P:(g + 1) * P],
                                         start=True, stop=True)
                            ctx.inc(i, "smm")

                        @block.vector
                        def _(v):
                            ctx.wait(v, "smm")
                            i = v.tensor_tensor(smT[:, :], PK[0][:, 0:P],
                                                mbd_sb[:, :], ALU.mult)
                            ctx.inc(i, "smev")

                        @block.tensor
                        def _(t):
                            ctx.wait(t, "smev")
                            if not (it == 0 and tb == 0 and g == 0):
                                ctx.wait(t, "vstc", ctx.val("vstc"))
                            i = t.transpose(TPB[:, 0:64], Vd[:, g * P:(g + 1) * P],
                                            id_sb[0:64, 0:64])
                            ctx.inc(i, "vtr")

                        @block.vector
                        def _(v):
                            ctx.wait(v, "vtr")
                            ctx.wait(v, "consts")
                            i = v.tensor_copy(vst[:, 0:64], TPB[:, 0:64])
                            ctx.inc(i, "vstc")

                        @block.tensor
                        def _(t):
                            ctx.wait(t, "vstc")
                            if not (it == 0 and tb == 0 and g == 0):
                                ctx.wait(t, "atw", ctx.val("atw"))
                            i = t.matmul(PK[1][:, 0:65], smT[:, :], vst[:, :],
                                         start=True, stop=True)
                            ctx.inc(i, "qvmm")

                        @block.vector
                        def _(v):
                            ctx.wait(v, "qvmm")
                            i = v.tensor_copy(dcol[:, :], PK[1][:, 64:65])
                            ctx.inc(i, "dcp")
                            ctx.wait(v, "dcp")
                            i = v.reciprocal(dcol[:, :], dcol[:, :])
                            ctx.inc(i, "drc")
                            ctx.wait(v, "drc")
                            ctx.wait(v, "atst", max(0, ctx.val("atst") - 16))
                            i = v.tensor_scalar(atg[g % 2][:, :], PK[1][:, 0:64],
                                                dcol[:, 0:1], None, ALU.mult)
                            ctx.inc(i, "atw")

                        @block.sync
                        def _(s):
                            ctx.wait(s, "atw", ctx.val("atw"))
                            rs = (tt + tb * P) * 16 + g * P
                            dma(ctx, s,
                                attd.rearrange("t (h e) -> (t h) e", e=64)[rs:rs + P, :],
                                atg[g % 2][:, :], "atst")

                for tb in range(4):
                    @block.gpsimd
                    def _(g):
                        ctx.wait(g, "atst", ctx.val("atst"))
                        ctx.wait(g, "atrc", ctx.val("atrc"))
                        dma(ctx, g, atmb[:, :],
                            attd[tt + tb * P: tt + (tb + 1) * P, :], "atld")

                    for c in range(DC):
                        @block.tensor
                        def _(t):
                            ctx.wait(t, "atld")
                            if not (it == 0 and tb == 0 and c == 0):
                                ctx.wait(t, "atrc", ctx.val("atrc"))
                            i = t.transpose(TPB[:, :], atmb[:, c * P:(c + 1) * P],
                                            id_sb[:, :])
                            ctx.inc(i, "atrm")

                        @block.vector
                        def _(v):
                            ctx.wait(v, "atrm")
                            i = v.tensor_copy(afm[:, c * 512 + tb * P:
                                                  c * 512 + (tb + 1) * P], TPB[:, :])
                            ctx.inc(i, "atrc")

                @block.gpsimd
                def _(g):
                    ctx.wait(g, "qkmm", ctx.val("qkmm"))
                    for c in range(DC):
                        dma(ctx, g, wstr[:, c * D:(c + 1) * D],
                            owT[c * P:(c + 1) * P, :], "wstl")
                owldv = ctx.val("wstl")

                for oc in range(DC):
                    @block.tensor
                    def _(t):
                        ctx.wait(t, "atrc", ctx.val("atrc"))
                        ctx.wait(t, "wstl", owldv)
                        ctx.wait(t, "x6aw", max(0, ctx.val("x6aw") - 1))
                        for c in range(DC):
                            t.matmul(PK[oc % 2][:, :],
                                     wstr[:, c * D + oc * P: c * D + (oc + 1) * P],
                                     afm[:, c * 512:(c + 1) * 512],
                                     start=(c == 0), stop=False)
                        i = t.matmul(PK[oc % 2][:, :],
                                     brow[0:1, 3 * D + oc * P:3 * D + (oc + 1) * P],
                                     tokrow_b[0:1, :], start=False, stop=True)
                        ctx.inc(i, "opmm")

                    @block.vector
                    def _(v):
                        ctx.wait(v, "opmm")
                        i = v.tensor_tensor(x6a[:, oc * 512:(oc + 1) * 512],
                                            PK[oc % 2][:, :],
                                            xlt[:, oc * 512:(oc + 1) * 512], ALU.add)
                        ctx.inc(i, "x6aw")

                @block.scalar
                def _(s):
                    ctx.wait(s, "fmm", ctx.val("fmm"))
                    ctx.wait(s, "qkmm", ctx.val("qkmm"))

                emit_ln_fm(ctx, block, C, x6a, 512, f"f3ln2_{it}", out_bf=xnb,
                           gamma=g2_sb, beta=b2_sb, wait_names=("x6aw",))

                for hc in range(8):
                    par = ghc[0] % 2

                    @block.gpsimd
                    def _(g):
                        ctx.wait(g, "hcdone", ghc[0])
                        for c in range(DC):
                            dma(ctx, g, w1b[par][:, c * 512:(c + 1) * 512],
                                fcw[c * P:(c + 1) * P, hc * 512:(hc + 1) * 512],
                                f"f3ld{par}")
                        for u in range(4):
                            dma(ctx, g, w2b[par][:, u * D:(u + 1) * D],
                                pjw[hc * 512 + u * P: hc * 512 + (u + 1) * P, :],
                                f"f3ld{par}")
                    ldv = ctx.val(f"f3ld{par}")

                    for u in range(4):
                        @block.tensor
                        def _(t):
                            ctx.wait(t, f"f3ld{par}", ldv)
                            ctx.wait(t, "ln_apply", ctx.val("ln_apply"))
                            ctx.wait(t, "hgel", ctx.val("hgel"))
                            for c in range(DC):
                                t.matmul(APS[:, :],
                                         w1b[par][:, c * 512 + u * P:
                                                  c * 512 + (u + 1) * P],
                                         xnb[:, c * 512:(c + 1) * 512],
                                         start=(c == 0), stop=False)
                            i = t.matmul(APS[:, :],
                                         b1r[0:1, hc * 512 + u * P:
                                             hc * 512 + (u + 1) * P],
                                         tokrow_b[0:1, :], start=False, stop=True)
                            ctx.inc(i, "fmm")

                        @block.scalar
                        def _(sE):
                            ctx.wait(sE, "fmm")
                            i = sE.activation(hgb[:, u * 512:(u + 1) * 512],
                                              APS[:, :], AF.Gelu)
                            ctx.inc(i, "hgel")

                    for oc in range(DC):
                        @block.tensor
                        def _(t):
                            ctx.wait(t, "hgel", ctx.val("hgel"))
                            ctx.wait(t, "macw", max(0, ctx.val("macw") - 1))
                            for u in range(4):
                                i = t.matmul(PK[oc % 2][:, :],
                                             w2b[par][:, u * D + oc * P:
                                                      u * D + (oc + 1) * P],
                                             hgb[:, u * 512:(u + 1) * 512],
                                             start=(u == 0),
                                             stop=(u == 3 and hc != 0))
                            if hc == 0:
                                i = t.matmul(PK[oc % 2][:, :],
                                             b2r[0:1, oc * P:(oc + 1) * P],
                                             tokrow_b[0:1, :],
                                             start=False, stop=True)
                            ctx.inc(i, "dmm")

                        @block.vector
                        def _(v):
                            ctx.wait(v, "dmm")
                            if hc == 0:
                                ctx.wait(v, "x7st", ctx.val("x7st"))
                                i = v.tensor_copy(mlpacc[:, oc * 512:(oc + 1) * 512],
                                                  PK[oc % 2][:, :])
                            else:
                                i = v.tensor_tensor(mlpacc[:, oc * 512:(oc + 1) * 512],
                                                    mlpacc[:, oc * 512:(oc + 1) * 512],
                                                    PK[oc % 2][:, :], ALU.add)
                            ctx.inc(i, "macw")

                    @block.gpsimd
                    def _(g):
                        ctx.wait(g, "macw", ctx.val("macw"))
                        ctx.inc(g.engine_nop(), "hcdone")
                    ghc[0] += 1

                @block.vector
                def _(v):
                    ctx.wait(v, "macw", ctx.val("macw"))
                    for oc in range(DC):
                        v.tensor_tensor(mlpacc[:, oc * 512:(oc + 1) * 512],
                                        x6a[:, oc * 512:(oc + 1) * 512],
                                        mlpacc[:, oc * 512:(oc + 1) * 512],
                                        ALU.add)
                        i = v.tensor_tensor(mlpacc[:, oc * 512:(oc + 1) * 512],
                                            mlpacc[:, oc * 512:(oc + 1) * 512],
                                            x6t[:, oc * 512:(oc + 1) * 512],
                                            ALU.add)
                        ctx.inc(i, "x7wt")

                @block.sync
                def _(s):
                    ctx.wait(s, "x7wt", ctx.val("x7wt"))
                    for c in range(DC):
                        dma(ctx, s, x7o[c * P:(c + 1) * P, tt:tt + 512],
                            mlpacc[:, c * 512:(c + 1) * 512], "x7st")

            @block.sync
            def _(s):
                ctx.wait(s, "x7st", ctx.val("x7st"))
    return nc


def run_F3(inputs, x6):
    nc = build_F3()
    import ml_dtypes
    mask = np.zeros((P, P), np.float32)
    for g in range(8):
        mask[g * 16:(g + 1) * 16, g * 16:(g + 1) * 16] = 1.0
    qkvb = np.concatenate([np.asarray(inputs["L3_attn_q_b"], np.float32),
                           np.asarray(inputs["L3_attn_k_b"], np.float32),
                           np.asarray(inputs["L3_attn_v_b"], np.float32)])
    base = dict(
        l1g=_fm_vec(inputs["L3_attn_ln1_w"]), l1b=_fm_vec(inputs["L3_attn_ln1_b"]),
        qwT=np.ascontiguousarray(np.asarray(inputs["L3_attn_q_w"], np.float32).T),
        kwT=np.ascontiguousarray(np.asarray(inputs["L3_attn_k_w"], np.float32).T),
        vwT=np.ascontiguousarray(np.asarray(inputs["L3_attn_v_w"], np.float32).T),
        qkvb=qkvb.reshape(1, 3 * D),
        owT=np.ascontiguousarray(np.asarray(inputs["L3_attn_o_w"], np.float32).T),
        ob=np.asarray(inputs["L3_attn_o_b"], np.float32).reshape(1, D),
        mbd=mask,
        identb=np.eye(P, dtype=np.float32).astype(ml_dtypes.bfloat16),
        l2g=_fm_vec(inputs["L3_attn_ln2_w"]), l2b=_fm_vec(inputs["L3_attn_ln2_b"]),
        fcw=np.ascontiguousarray(np.asarray(inputs["L3_attn_fc_w"], np.float32).T),
        fcb=np.asarray(inputs["L3_attn_fc_b"], np.float32).reshape(1, 4096),
        pjw=np.ascontiguousarray(np.asarray(inputs["L3_attn_proj_w"], np.float32).T),
        pjb=np.asarray(inputs["L3_attn_proj_b"], np.float32).reshape(1, D))
    in_maps = []
    for c in range(NCORES):
        m = dict(base)
        m["x6I"] = x6[c]
        in_maps.append(m)
    res = run_bass_kernel_spmd(nc, in_maps, core_ids=list(range(NCORES)))
    return [res.results[c]["x7o"] for c in range(NCORES)]




def build_F4():
    nc = bass.Bass("TRN2", target_bir_lowering=False)
    x7I = nc.dram_tensor("x7I", [D, TC], F32, kind="ExternalInput").ap()
    rwT = nc.dram_tensor("rwT", [D, 8], F32, kind="ExternalInput").ap()
    rbias = nc.dram_tensor("rbias", [1, 8], F32, kind="ExternalInput").ap()
    sw1 = nc.dram_tensor("sw1", [D, 2048], F32, kind="ExternalInput").ap()
    sw3 = nc.dram_tensor("sw3", [D, 2048], F32, kind="ExternalInput").ap()
    sw2 = nc.dram_tensor("sw2", [2048, D], F32, kind="ExternalInput").ap()
    ew1 = nc.dram_tensor("ew1", [8 * D, D], F32, kind="ExternalInput").ap()
    ew2 = nc.dram_tensor("ew2", [8 * D, D], F32, kind="ExternalInput").ap()
    identf = nc.dram_tensor("identf", [P, P], F32, kind="ExternalInput").ap()
    outo = nc.dram_tensor("outo", [D, TC], F32, kind="ExternalOutput").ap()

    NT4 = TC // 512

    with ExitStack() as es:
        ctx = Ctx(nc, es)
        C = make_consts(ctx)
        ensure_ln_shared(ctx)
        tokrow_b = sb(ctx, "tokrow_b", [1, 512], BF16)
        ln3b = sb(ctx, "ln3b", [P, DC * TC], BF16)
        routed = sb(ctx, "routed", [P, DC * TC], F32)
        cwT8 = sb(ctx, "cwT8", [8, TC], F32)
        id_sb = sb(ctx, "id_sb", [P, P], F32)

        with nc.Block() as block:
            @block.vector
            def _(v):
                v.memset(C["ones_colf"][:, :], 1.0)
                v.memset(C["ones_colb"][:, :], 1.0)
                v.memset(C["ones_rowf"][:, :], 1.0)
                v.memset(C["ones_rowb"][:, :], 1.0)
                v.memset(C["ones_tokb"][:, :], 1.0)
                v.memset(tokrow_b[:, :], 1.0)
                for c in range(DC):
                    i = v.memset(routed[:, c * TC:(c + 1) * TC], 0.0)
                ctx.inc(i, "consts")

            es2 = es.enter_context(ExitStack())
            _c2 = Ctx.__new__(Ctx); _c2.__dict__ = dict(ctx.__dict__); _c2.es = es2
            rwT_sb = sb(_c2, "rwT_sb", [P, DC * 8], F32)
            rb_bc = sb(_c2, "rb_bc", [P, 8], F32)
            x7t2 = sb(_c2, "x7t2", [P, DC * 512], F32)
            lnf_t = sb(_c2, "lnf_t", [P, DC * 512], F32)
            s_sb = sb(_c2, "s_sb", [P, 8], F32)
            sc_sb = sb(_c2, "sc_sb", [P, 8], F32)
            selm = sb(_c2, "selm", [P, 8], F32)
            selw = sb(_c2, "selw", [P, 8], F32)
            cwtb = sb(_c2, "cwtb", [P, 8], F32)
            grow = sb(_c2, "grow", [P, 1], F32)
            den = sb(_c2, "den", [P, 1], F32)
            dfl = sb(_c2, "dfl", [P, 1], F32)
            nf = sb(_c2, "nf", [P, 1], F32)
            zps = ps(_c2, "zps", [P, 512], F32)
            trps = ps(_c2, "trps", [8, P], F32)

            @block.sync
            def _(s):
                for c in range(DC):
                    dma(ctx, s, rwT_sb[:, c * 8:(c + 1) * 8],
                        rwT[c * P:(c + 1) * P, :], "ld_gw")
                dma(ctx, s, id_sb[:, :], identf, "ld_gw")

            @block.gpsimd
            def _(g):
                dma(ctx, g, rb_bc[0:1, 0:8], rbias, "ld_gb")

            @block.tensor
            def _(t):
                ctx.wait(t, "ld_gb")
                ctx.wait(t, "consts")
                i = t.matmul(zps[:, 0:8], C["ones_rowf"][0:1, :],
                             rb_bc[0:1, 0:8], start=True, stop=True)
                ctx.inc(i, "gbm")

            @block.vector
            def _(v):
                ctx.wait(v, "gbm")
                i = v.tensor_copy(rb_bc[:, :], zps[:, 0:8])
                ctx.inc(i, "gbc")

            def ve(v, i, tag="ve"):
                ctx.inc(i, tag)
                ctx.wait(v, tag)

            for it in range(NT4):
                tt = it * 512

                @block.sync
                def _(s):
                    if it > 0:
                        ctx.wait(s, "route", 4 * it)
                        ctx.wait(s, "ln3bc", it)
                    for c in range(DC):
                        dma(ctx, s, x7t2[:, c * 512:(c + 1) * 512],
                            x7I[c * P:(c + 1) * P, tt:tt + 512], "ld_x7r")

                emit_ln_fm(ctx, block, C, x7t2, 512, f"F4ln_{it}",
                           out_f32=lnf_t, wait_names=("ld_x7r",))

                @block.vector
                def _(v):
                    ctx.wait(v, "ln_apply")
                    for c in range(DC):
                        i = v.tensor_copy(ln3b[:, c * TC + tt: c * TC + tt + 512],
                                          lnf_t[:, c * 512:(c + 1) * 512])
                    ctx.inc(i, "ln3bc")

                for tb in range(4):
                    tg = tt + tb * P

                    @block.tensor
                    def _(t):
                        ctx.wait(t, "ln_apply")
                        ctx.wait(t, "ld_gw")
                        if it == 0 and tb == 0:
                            ctx.wait(t, "gbc")
                        else:
                            ctx.wait(t, "sccp")
                        for c in range(DC):
                            i = t.matmul(zps[:, 0:8],
                                         lnf_t[:, c * 512 + tb * P: c * 512 + (tb + 1) * P],
                                         rwT_sb[:, c * 8:(c + 1) * 8],
                                         start=(c == 0), stop=(c == DC - 1))
                        ctx.inc(i, "zmm")

                    @block.vector
                    def _(v):
                        ctx.wait(v, "zmm")
                        i = v.tensor_copy(sc_sb[:, :], zps[:, 0:8])
                        ve(v, i, "sccp")

                    @block.scalar
                    def _(s):
                        ctx.wait(s, "sccp")
                        i = s.activation(s_sb[:, :], sc_sb[:, :], AF.Sigmoid)
                        ctx.inc(i, "sig")

                    @block.vector
                    def _(v):
                        ctx.wait(v, "sig")
                        ctx.wait(v, "gbc")
                        i = v.tensor_tensor(sc_sb[:, :], s_sb[:, :], rb_bc[:, :],
                                            ALU.add)
                        ve(v, i)
                        i = v.memset(selm[:, :], 0.0)
                        ve(v, i)
                        for kk in range(2):
                            i = v.reduce_max(grow[:, :],
                                             sc_sb.ap().rearrange(
                                                 "p (o e) -> p o e", o=1),
                                             axis=AX.X)
                            ve(v, i)
                            i = v.tensor_scalar(selw[:, :], sc_sb[:, :],
                                                grow[:, 0:1], None, ALU.is_ge)
                            ve(v, i)
                            i = v.tensor_tensor(selm[:, :], selm[:, :], selw[:, :],
                                                ALU.add)
                            ve(v, i)
                            i = v.tensor_scalar(selw[:, :], selw[:, :], 1e30, None,
                                                ALU.mult)
                            ve(v, i)
                            i = v.tensor_tensor(sc_sb[:, :], sc_sb[:, :],
                                                selw[:, :], ALU.subtract)
                            ve(v, i)
                        i = v.tensor_scalar(selm[:, :], selm[:, :], 1.0, None,
                                            ALU.min)
                        ve(v, i)
                        i = v.tensor_tensor(selw[:, :], selm[:, :], s_sb[:, :],
                                            ALU.mult)
                        ve(v, i)
                        i = v.reduce_sum(den[:, :],
                                         selw.ap().rearrange("p (o e) -> p o e", o=1),
                                         axis=AX.X)
                        ve(v, i)
                        i = v.tensor_scalar(dfl[:, :], den[:, :], 1e-9, None,
                                            ALU.is_ge)
                        ve(v, i)
                        i = v.tensor_scalar(den[:, :], den[:, :], 1e-9, None,
                                            ALU.add)
                        ve(v, i)
                        i = v.reciprocal(den[:, :], den[:, :])
                        ve(v, i)
                        i = v.tensor_scalar(cwtb[:, :], selw[:, :], den[:, 0:1],
                                            None, ALU.mult)
                        ve(v, i)
                        i = v.tensor_scalar(cwtb[:, :], cwtb[:, :], dfl[:, 0:1],
                                            None, ALU.mult)
                        ve(v, i)
                        i = v.tensor_scalar(nf[:, :], dfl[:, :], -0.5, 0.5,
                                            ALU.mult, ALU.add)
                        ve(v, i)
                        i = v.tensor_scalar(selw[:, :], selm[:, :], nf[:, 0:1],
                                            None, ALU.mult)
                        ve(v, i)
                        i = v.tensor_tensor(cwtb[:, :], cwtb[:, :], selw[:, :],
                                            ALU.add)
                        ctx.inc(i, "cwrdy")

                    @block.tensor
                    def _(t):
                        ctx.wait(t, "cwrdy")
                        i = t.transpose(trps[0:8, :], cwtb[:, :], id_sb[:, :])
                        ctx.inc(i, "trmm")

                    @block.vector
                    def _(v):
                        ctx.wait(v, "trmm")
                        i = v.tensor_copy(cwT8[:, tg:tg + P], trps[0:8, :])
                        ctx.inc(i, "route")

            es2.close()

            es3 = es.enter_context(ExitStack())
            _c3 = Ctx.__new__(Ctx); _c3.__dict__ = dict(ctx.__dict__); _c3.es = es3
            wg_b = [sb(_c3, f"wg_b{j}", [P, DC * 1024], BF16) for j in range(2)]
            _wus = sb(_c3, "wu_b0", [P, DC * 512], BF16)
            wu_b = [_wus, _wus]
            _wds = sb(_c3, "wd_b0", [P, 8 * D], BF16)
            wd_b = [_wds, _wds]
            sg_sb = sb(_c3, "sg_sb", [P, 512], F32)
            h_sb = sb(_c3, "h_sb", [P, 512], F32)
            hs_sb = sb(_c3, "hs_sb", [P, 8 * 512], BF16)
            cwb_sb = sb(_c3, "cwb_sb", [P, 512], F32)
            cwrow = sb(_c3, "cwrow", [1, 512], F32)
            gps_ = ps(_c3, "gps", [P, 512], F32)
            ups_ = ps(_c3, "ups", [P, 512], F32)
            dps_ = ps(_c3, "dps", [P, 512], F32)
            cps_ = ps(_c3, "cps", [P, 512], F32)
            ldtags = []
            NEXP = 12

            for e in range(NEXP):
                par = e % 2
                shared = e >= 8
                sq = e - 8
                hce = 8 if not shared else 4

                @block.gpsimd
                def _(g):
                    if e == 0:
                        ctx.wait(g, "route", 16)
                    if e >= 1:
                        ctx.wait(g, "edone", e)
                    for c in range(DC):
                        if not shared:
                            dma(ctx, g, wg_b[par][:, c * 1024:(c + 1) * 1024],
                                ew1[e * D + c * P: e * D + (c + 1) * P, :],
                                f"wld{par}")
                        else:
                            dma(ctx, g, wg_b[par][:, c * 1024:c * 1024 + 512],
                                sw1[c * P:(c + 1) * P, sq * 512:(sq + 1) * 512],
                                f"wld{par}")
                            dma(ctx, g, wu_b[par][:, c * 512:(c + 1) * 512],
                                sw3[c * P:(c + 1) * P, sq * 512:(sq + 1) * 512],
                                f"wld{par}")
                    for hc in range(hce):
                        if not shared:
                            dma(ctx, g, wd_b[par][:, hc * D:(hc + 1) * D],
                                ew2[e * D + hc * P: e * D + (hc + 1) * P, :],
                                f"wld{par}")
                        else:
                            dma(ctx, g, wd_b[par][:, hc * D:(hc + 1) * D],
                                sw2[sq * 512 + hc * P: sq * 512 + (hc + 1) * P, :],
                                f"wld{par}")
                ldtags.append(ctx.val(f"wld{par}"))

                for it in range(NT4):
                    tt = it * 512

                    if not shared:
                        @block.sync
                        def _(sy):
                            if e == 0 and it == 0:
                                ctx.wait(sy, "route", 16)
                            if not (e == 0 and it == 0):
                                ctx.wait(sy, "cwmm")
                            dma(ctx, sy, cwrow[0:1, :], cwT8[e:e + 1, tt:tt + 512],
                                "cwdma")

                        @block.tensor
                        def _(t):
                            ctx.wait(t, "cwdma")
                            if not (e == 0 and it == 0):
                                ctx.wait(t, "cwbc")
                            i = t.matmul(cps_[:, :], C["ones_rowf"][0:1, :],
                                         cwrow[0:1, :], start=True, stop=True)
                            ctx.inc(i, "cwmm")

                        @block.vector
                        def _(v):
                            ctx.wait(v, "cwmm")
                            i = v.tensor_copy(cwb_sb[:, :], cps_[:, :])
                            ctx.inc(i, "cwbc")

                    for hc in range(hce):
                        @block.tensor
                        def _(t):
                            ctx.wait(t, f"wld{par}", ldtags[-1])
                            if not (e == 0 and it == 0 and hc == 0):
                                ctx.wait(t, "hsil")
                            for c in range(DC):
                                i = t.matmul(gps_[:, :],
                                             wg_b[par][:, c * 1024 + hc * P: c * 1024 + (hc + 1) * P]
                                             if not shared else
                                             wg_b[par][:, c * 1024 + hc * P: c * 1024 + (hc + 1) * P],
                                             ln3b[:, c * TC + tt: c * TC + tt + 512],
                                             start=(c == 0), stop=(c == DC - 1))
                            ctx.inc(i, "gmm")
                            if shared:
                                if not (e == 8 and it == 0 and hc == 0):
                                    ctx.wait(t, "hmul")
                                for c in range(DC):
                                    i = t.matmul(ups_[:, :],
                                                 wu_b[par][:, c * 512 + hc * P: c * 512 + (hc + 1) * P],
                                                 ln3b[:, c * TC + tt: c * TC + tt + 512],
                                                 start=(c == 0), stop=(c == DC - 1))
                                ctx.inc(i, "umm")

                        @block.scalar
                        def _(s):
                            ctx.wait(s, "gmm")
                            if not shared:
                                if not (e == 0 and it == 0 and hc == 0):
                                    ctx.wait(s, "hcw")
                                i = s.activation(h_sb[:, :], gps_[:, :], AF.Gelu)
                                ctx.inc(i, "hsil")
                            else:
                                if not (e == 8 and it == 0 and hc == 0):
                                    ctx.wait(s, "hmul")
                                i = s.activation(sg_sb[:, :], gps_[:, :], AF.Silu)
                                ctx.inc(i, "hsil")

                        if not shared:
                            @block.gpsimd
                            def _(g):
                                ctx.wait(g, "hsil")
                                ctx.wait(g, "cwbc")
                                i = g.tensor_tensor(hs_sb[:, hc * 512:(hc + 1) * 512],
                                                    h_sb[:, :], cwb_sb[:, :],
                                                    ALU.mult)
                                ctx.inc(i, "hcw")
                                ctx.inc(g.engine_nop(), "hmul")
                        else:
                            @block.vector
                            def _(v):
                                ctx.wait(v, "umm")
                                ctx.wait(v, "hsil")
                                i = v.tensor_tensor(hs_sb[:, hc * 512:(hc + 1) * 512],
                                                    sg_sb[:, :], ups_[:, :],
                                                    ALU.mult)
                                ctx.inc(i, "hmul")
                                ctx.inc(v.engine_nop(), "hcw")

                    for oc in range(DC):
                        @block.tensor
                        def _(t):
                            ctx.wait(t, "hcw", ctx.val("hcw"))
                            ctx.wait(t, "hmul", ctx.val("hmul"))
                            if not (e == 0 and it == 0 and oc == 0):
                                ctx.wait(t, "racc")
                            for hc in range(hce):
                                i = t.matmul(dps_[:, :],
                                             wd_b[par][:, hc * D + oc * P: hc * D + (oc + 1) * P],
                                             hs_sb[:, hc * 512:(hc + 1) * 512],
                                             start=(hc == 0), stop=(hc == hce - 1))
                            ctx.inc(i, "dmm")

                        @block.vector
                        def _(v):
                            ctx.wait(v, "dmm")
                            i = v.tensor_tensor(routed[:, oc * TC + tt: oc * TC + tt + 512],
                                                routed[:, oc * TC + tt: oc * TC + tt + 512],
                                                dps_[:, :], ALU.add)
                            ctx.inc(i, "racc")

                @block.gpsimd
                def _(g):
                    ctx.wait(g, "racc")
                    ctx.inc(g.engine_nop(), "edone")

            es3.close()

            x7t3 = sb(ctx, "x7t3", [P, DC * 512], F32)
            xot = sb(ctx, "xot", [P, DC * 512], F32)

            for it in range(NT4):
                tt = it * 512

                @block.sync
                def _(s):
                    ctx.wait(s, "edone", NEXP)
                    ctx.wait(s, "xost", ctx.val("xost"))
                    for c in range(DC):
                        dma(ctx, s, x7t3[:, c * 512:(c + 1) * 512],
                            x7I[c * P:(c + 1) * P, tt:tt + 512], "ld_x7a")

                @block.vector
                def _(v):
                    ctx.wait(v, "ld_x7a")
                    ctx.wait(v, "xost", ctx.val("xost"))
                    for c in range(DC):
                        i = v.tensor_tensor(xot[:, c * 512:(c + 1) * 512],
                                            x7t3[:, c * 512:(c + 1) * 512],
                                            routed[:, c * TC + tt: c * TC + tt + 512],
                                            ALU.add)
                        ctx.inc(i, "xow")

                @block.sync
                def _(s):
                    ctx.wait(s, "xow", ctx.val("xow"))
                    for c in range(DC):
                        dma(ctx, s, outo[c * P:(c + 1) * P, tt:tt + 512],
                            xot[:, c * 512:(c + 1) * 512], "xost")

            @block.sync
            def _(s):
                ctx.wait(s, "xost", ctx.val("xost"))
    return nc


def run_F4(inputs, x7):
    nc = build_F4()
    base = dict(
        rwT=np.ascontiguousarray(np.asarray(inputs["L3_ffn_router_w"],
                                            np.float32).T),
        rbias=np.asarray(inputs["L3_ffn_router_bias"], np.float32).reshape(1, 8),
        sw1=np.asarray(inputs["L3_ffn_sw1"], np.float32),
        sw3=np.asarray(inputs["L3_ffn_sw3"], np.float32),
        sw2=np.asarray(inputs["L3_ffn_sw2"], np.float32),
        ew1=np.asarray(inputs["L3_ffn_ew1"], np.float32).reshape(8 * D, D),
        ew2=np.asarray(inputs["L3_ffn_ew2"], np.float32).reshape(8 * D, D),
        identf=np.eye(P, dtype=np.float32))
    in_maps = []
    for c in range(NCORES):
        m = dict(base)
        m["x7I"] = x7[c]
        in_maps.append(m)
    res = run_bass_kernel_spmd(nc, in_maps, core_ids=list(range(NCORES)))
    return [res.results[c]["outo"] for c in range(NCORES)]




def _fm_shards(x):
    xt = np.ascontiguousarray(np.asarray(x, np.float32).reshape(T, D).T)
    return [np.ascontiguousarray(xt[:, c * TC:(c + 1) * TC]) for c in range(NCORES)]


def kernel(**inputs):
    shards = _fm_shards(inputs["L0_attn_x"])
    pkv, ln0 = run_A(inputs, shards)
    x2, u, _x1 = run_B(inputs, shards, pkv, ln0)
    khr, khi = run_C0(inputs)
    u_full = np.ascontiguousarray(
        np.concatenate([np.asarray(a, np.float32) for a in u], axis=1).T)
    yconv = run_C2(u_full, khr, khi)
    x4, _a, _b, _c = run_D(inputs, x2, u, yconv)
    x4_full = np.concatenate([np.asarray(a, np.float32) for a in x4], axis=1)
    pkvE, xc, ar, xi = run_E(inputs, x4_full)
    x5 = run_F1(inputs, pkvE, xc, ar, xi, x4)
    x6 = run_F2(inputs, x5)
    x7 = run_F3(inputs, x6)
    out = run_F4(inputs, x7)
    out_full = np.concatenate([np.asarray(a, np.float32) for a in out],
                              axis=1).T
    return np.ascontiguousarray(out_full).reshape(1, T, D)
